# version 02
# baseline (speedup 1.0000x reference)
import math
from contextlib import ExitStack

import numpy as np
import ml_dtypes

import concourse.bass as bass
import concourse.mybir as mybir
from concourse.bass_utils import run_bass_kernel_spmd

F32 = mybir.dt.float32
BF16 = mybir.dt.bfloat16
I32 = mybir.dt.int32
AF = mybir.ActivationFunctionType
ALU = mybir.AluOpType
AX = mybir.AxisListType.X
IOA = bass.IndirectOffsetOnAxis

NCORES = 8
H = 2048
NT = 1024
TCH = 8
NH = 32
DH = 64
SCALE = math.sqrt(32.0)
NKC = 32
EPC = 4
CAP = 512
KTOP = 4
NTOK = 8192
NSLOT = 16384
BIG = 2.0e6
EPS = 1e-5
VPAD = 80

DEBUG = False
ENGS = ("pe", "act", "dve", "pool", "sp")


class KB:
    def __init__(self, nc):
        self.nc = nc
        self.prog = {e: [] for e in ENGS}
        self.sems = {}
        self.cnt = {}
        self.lastw = {e: {} for e in ENGS}
        self.sem_stack = ExitStack()

    def sem(self, *names):
        for name in names:
            self.sems[name] = self.sem_stack.enter_context(self.nc.semaphore(name))
            self.cnt[name] = 0

    def n(self, name):
        return self.cnt[name]

    def bump(self, name, k=1):
        self.cnt[name] += k
        return self.cnt[name]

    def h(self, name):
        return self.sems[name]

    def step(self, eng, waits=None):
        wl = []
        for s, v in (waits or {}).items():
            if v > self.lastw[eng].get(s, 0):
                self.lastw[eng][s] = v
                wl.append((self.sems[s], v))

        def deco(fn):
            def thunk(api, wl=wl, fn=fn):
                for hh, v in wl:
                    api.wait_ge(hh, v)
                fn(api)
            self.prog[eng].append(thunk)
        return deco

    def barrier(self):
        vals = {e: self.bump("b_" + e) for e in ENGS}
        for e in ENGS:
            def f(api, e=e, vals=vals):
                api.sem_inc(self.sems["b_" + e], 1)
                for o in ENGS:
                    if o != e:
                        api.wait_ge(self.sems["b_" + o], vals[o])
            self.prog[e].append(f)
            for o in ENGS:
                self.lastw[e]["b_" + o] = vals[o]


def mx(*ds):
    out = {}
    for d in ds:
        for k, v in d.items():
            out[k] = max(out.get(k, 0), v)
    return out


def build_program(debug=False):
    nc = bass.Bass("TRN2", target_bir_lowering=False)

    def din(name, shape, dt=F32):
        return nc.dram_tensor(name, shape, dt, kind="ExternalInput").ap()

    x_d = din("x", [NT, H])
    wq_d = din("wq", [16, 128, 16, 128])
    wk_d = din("wk", [16, 128, 16, 128])
    wv_d = din("wv", [16, 128, 16, 128])
    wo_d = din("wo", [16, 128, H])
    bq_d = din("bq_t", [128, 16])
    bk_d = din("bk_t", [128, 16])
    bv_d = din("bv_t", [128, 16])
    bo_d = din("bo_r", [1, H])
    gwt_d = din("gwt", [128, 16, 32])
    gb_d = din("gb", [32, 1])
    w1_d = din("w1", [EPC, 8, 128, 16, 128])
    w3_d = din("w3", [EPC, 8, 128, 16, 128])
    w2_d = din("w2", [EPC, 8, 128, H])
    wsg_d = din("wsg", [16, 128, 16, 128])
    wsu_d = din("wsu", [16, 128, 16, 128])
    wsd_d = din("wsd", [16, 128, H])
    idf_d = din("idf", [128, 128])
    onesb_d = din("onesb", [128, 128], BF16)
    lstr_d = din("lstr", [128, 128], BF16)
    onesf_d = din("onesf", [1, 128])
    qmask_d = din("qmask", [16, 128, 32], BF16)
    ioe_d = din("ioe", [128, 32])
    pcol_d = din("pcol", [128, 1])
    pidx_d = din("pidx", [128, 1])
    cmask_d = din("cmask", [8, 1])

    out_d = nc.dram_tensor("out", [NT, H], F32, kind="ExternalOutput").ap()

    def dint(name, shape, dt):
        return nc.dram_tensor(name, shape, dt).ap()

    kvK_in = dint("kvK_in", [H, NT], BF16)
    kvV_in = dint("kvV_in", [H, NT], BF16)
    K_all = dint("K_all", [4 * H, NT], BF16)
    V_all = dint("V_all", [4 * H, NT], BF16)
    kn_in = dint("kn_in", [32, 1], F32)
    kn_all = dint("kn_all", [128, 1], F32)
    xn1_dr = dint("xn1_dr", [NT, H], BF16)
    xn2_in = dint("xn2_in", [NT, H], BF16)
    xlo_dr = dint("xlo_dr", [NT, H], BF16)
    xn2_all = dint("xn2_all", [NTOK + 128, H], BF16)
    tot_in = dint("tot_in", [1, 32], F32)
    tot_all = dint("tot_all", [8, 32], F32)
    map_in = dint("map_in", [NSLOT + 128, 1], F32)
    map_all = dint("map_all", [NSLOT + 128, 1], F32)
    bufdr = dint("bufdr", [EPC * CAP, H], BF16)
    ob_in = dint("ob_in", [EPC * CAP, H], BF16)
    ob_all = dint("ob_all", [NSLOT + 128, H], BF16)

    dbg = {}
    if debug:
        for name, shape, dt in [
            ("d_xn1", [NT, H], BF16), ("d_kall", [4 * H, NT], BF16),
            ("d_vall", [4 * H, NT], BF16), ("d_q", [128, 16 * NT], BF16),
            ("d_negc", [128, 32], F32), ("d_feats", [128, 16 * NT], BF16),
            ("d_x2", [128, TCH * H], F32), ("d_xn2", [NT, H], BF16),
            ("d_sc", [128, TCH * 32], F32), ("d_idx", [128, TCH * 4], F32),
            ("d_wk", [128, TCH * 4], F32), ("d_gidx", [128, TCH * 4], I32),
            ("d_cs", [128, TCH * 32], F32), ("d_offs", [128, 32], F32),
            ("d_map", [NSLOT + 128, 1], F32), ("d_buf", [EPC * CAP, H], BF16),
            ("d_ob", [EPC * CAP, H], BF16), ("d_hs", [128, 16 * NT], BF16),
            ("d_moe", [128, TCH * H], F32),
        ]:
            dbg[name] = nc.dram_tensor(name, shape, dt, kind="ExternalOutput").ap()

    S = KB(nc)
    S.sem(*["b_" + e for e in ENGS])
    S.sem("c_pe", "c_act", "c_dve", "c_pool")
    S.sem("d_sp", "d_aq", "d_pq")
    S.sem("cc_k", "cc_v", "cc_kn", "cc_x2", "cc_tot", "cc_map", "cc_ob")

    alloc = ExitStack()

    def sb(name, shape, dt):
        return alloc.enter_context(nc.sbuf_tensor(name, shape, dt))

    x_tm = sb("x_tm", [128, TCH, H], F32)
    slotA = sb("slotA", [128, 16 * 1024], BF16)
    slotB = sb("slotB", [128, 16 * 1024], BF16)
    slotC = sb("slotC", [128, 22528], BF16)
    slotF = sb("slotF", [128, 6144], F32)
    idf = sb("idf_s", [128, 128], F32)
    onesb = sb("onesb_s", [128, 128], BF16)
    lstr = sb("lstr_s", [128, 128], BF16)
    onesf = sb("onesf_s", [1, 128], F32)
    qmask = sb("qmask_s", [128, 16, 32], BF16)
    ioe = sb("ioe_s", [128, 32], F32)
    pcol = sb("pcol_s", [128, 1], F32)
    pidx = sb("pidx_s", [128, 1], F32)
    cmask = sb("cmask_s", [8, 1], F32)
    bq_t = sb("bq_s", [128, 16], F32)
    bk_t = sb("bk_s", [128, 16], F32)
    bv_t = sb("bv_s", [128, 16], F32)
    bqs = sb("bqs_s", [128, 16], F32)
    gwt_bf = sb("gwtbf_s", [128, 16, 32], BF16)
    glo_bf = sb("globf_s", [128, 16, 32], BF16)
    gb = sb("gb_s", [32, 1], F32)
    rev32 = sb("rev32_s", [128, 32], F32)
    negc_sb = sb("negc_s", [128, 32], F32)
    qmax = sb("qmax_s", [32, 1], F32)
    kmax = sb("kmax_s", [32, 1], F32)
    knal = sb("knal_s", [32, 4], F32)
    negcT = sb("negcT_s", [1, 32], F32)
    idxf = sb("idxf_s", [128, TCH, 4], F32)
    t2v = sb("t2_s", [128, TCH, 4], F32)
    wkv = sb("wk_s", [128, TCH, 4], F32)
    gidx = sb("gidx_s", [128, TCH, 4], I32)
    csv = sb("cs_s", [128, TCH, 32], F32)
    anv = sb("an_s", [128, TCH, 32], BF16)
    offsb = sb("offs_s", [128, 32], F32)
    totsb = sb("tot_s", [1, 32], F32)
    tota = sb("tota_s", [8, 32], F32)
    offs1 = sb("offs1_s", [1, 32], F32)
    eps_t = sb("eps_s", [128, 1], F32)
    mapt = sb("mapt_s", [128, 1], F32)
    mapi = sb("mapi_s", [128, 1], I32)
    mrow = sb("mrow_s", [128, 1], F32)
    mrow_i = sb("mrowi_s", [128, 1], I32)

    sco_t = slotF[:, 0:256].rearrange("p (a b) -> p a b", b=32)
    scb_t = slotF[:, 256:512].rearrange("p (a b) -> p a b", b=32)
    r_gs = slotF[:, 512:516]
    r_m1 = slotF[:, 516:517]
    r_m2 = slotF[:, 517:518]
    r_cnt = slotF[:, 518:519]
    r_w = slotF[:, 519:520]
    r_d = slotF[:, 520:521]
    r_th = slotF[:, 521:522]
    r_eq = slotF[:, 524:532]
    r_t8 = slotF[:, 532:540]
    r_wraw = slotF[:, 540:544]
    r_ws = slotF[:, 544:545]
    r_cat = slotF[:, 545:546]
    r_oat = slotF[:, 546:547]
    r_pos = slotF[:, 547:548]
    r_kp = slotF[:, 548:549]
    r_g0 = slotF[:, 549:550]
    r_adj = slotF[:, 550:551]
    r_tok = slotF[:, 551:552]
    r_work = slotF[:, 576:608]
    r_work2 = slotF[:, 608:640]
    r_r1 = slotF[:, 640:672]
    r_a = slotF[:, 672:704]
    r_oh = slotF[:, 704:832].rearrange("p (a b) -> p a b", b=32)
    gidx_f = slotF[:, 832:864].rearrange("p (a b) -> p a b", b=4)
    sco_f = slotF[0:32, 1024:2048]
    scb_f = slotF[0:32, 2048:3072]
    bn_t = slotF[:, 3104:3128].rearrange("p (a b) -> p a b", b=6)
    gwt_f = slotF[:, 512:1024].rearrange("p (a b) -> p a b", b=32)
    mv = slotF[:, 3128:3130]
    sd1 = slotF[:, 3130:3131]
    rstd = slotF[:, 3131:3132]
    acc = [slotF[:, 1024:3072], slotF[:, 3072:5120]]
    outst = [slotF[:, 5120:5632], slotF[:, 5632:6144]]

    @S.step("sp")
    def _(api):
        for dst, src in [
            (x_tm[:, :, :],
             x_d.rearrange("(a p) f -> p a f", p=128)[:, :, :]),
            (idf[:, :], idf_d[:, :]), (onesb[:, :], onesb_d[:, :]),
            (lstr[:, :], lstr_d[:, :]), (onesf[:, :], onesf_d[:, :]),
            (qmask[:, :, :],
             qmask_d.rearrange("a p e -> p a e")[:, :, :]),
            (ioe[:, :], ioe_d[:, :]), (pcol[:, :], pcol_d[:, :]),
            (pidx[:, :], pidx_d[:, :]), (cmask[:, :], cmask_d[:, :]),
            (bq_t[:, :], bq_d[:, :]), (bk_t[:, :], bk_d[:, :]),
            (bv_t[:, :], bv_d[:, :]), (gb[:, :], gb_d[:, :]),
            (gwt_f[:, :, :], gwt_d[:, :, :]),
        ]:
            api.dma_start(out=dst, in_=src).then_inc(S.h("d_sp"), 16)
    n_setup = S.bump("d_sp", 16 * 15)

    @S.step("pool")
    def _(api):
        api.engine_nop()
    n_setup_pq = S.bump("d_pq", 0)

    @S.step("dve", waits={"d_sp": n_setup})
    def _(api):
        api.tensor_scalar(out=bqs[:, :], in0=bq_t[:, :], scalar1=1.0 / SCALE,
                          scalar2=None, op0=ALU.mult).then_inc(S.h("c_dve"), 1)
        api.tensor_scalar(out=rev32[:, :], in0=ioe[:, :], scalar1=-1.0,
                          scalar2=32.0, op0=ALU.mult, op1=ALU.add).then_inc(S.h("c_dve"), 1)
        api.memset(eps_t[:, :], EPS).then_inc(S.h("c_dve"), 1)
        api.memset(slotC[:, 0:2048], 0.0).then_inc(S.h("c_dve"), 1)
        api.tensor_copy(out=gwt_bf.rearrange("p a e -> p (a e)")[:, :],
                        in_=gwt_f.rearrange("p a e -> p (a e)")[:, :]).then_inc(S.h("c_dve"), 1)
        api.memset(t2v.rearrange("p a k -> p (a k)")[:, :], 0.0).then_inc(S.h("c_dve"), 1)
    m_setup_dve = S.bump("c_dve", 6)
    @S.step("dve", waits={"c_dve": m_setup_dve})
    def _(api):
        api.tensor_tensor(out=glo_bf.rearrange("p a e -> p (a e)")[:, :],
                          in0=gwt_f.rearrange("p a e -> p (a e)")[:, :],
                          in1=gwt_bf.rearrange("p a e -> p (a e)")[:, :],
                          op=ALU.subtract).then_inc(S.h("c_dve"), 1)
    m_setup_dve = S.bump("c_dve")

    @S.step("pool", waits={"c_dve": m_setup_dve})
    def _(api):
        mz = map_in.rearrange("(a b) o -> a (b o)", a=128)
        api.dma_start(out=mz[:, 0:128], in_=slotC[:, 0:128]).then_inc(S.h("d_pq"), 16)
        api.dma_start(out=mz[:, 128:129], in_=slotC[:, 0:1], allow_slow_non_contiguous=True).then_inc(S.h("d_pq"), 16)
        api.dma_start(out=xn2_all[NTOK:NTOK + 128, :],
                      in_=slotC[:, 0:2048]).then_inc(S.h("d_pq"), 16)
        api.dma_start(out=ob_all[NSLOT:NSLOT + 128, :],
                      in_=slotC[:, 0:2048]).then_inc(S.h("d_pq"), 16)
    n_zero_pq = S.bump("d_pq", 64)

    def ph_layernorm(dram_out, dest_fm, first_wait, lo_dram=None, lo_fm=None):
        xnc = [slotC[:, 18432:20480], slotC[:, 20480:22528]]
        xloc = [slotC[:, 0:2048], slotC[:, 2048:4096]]
        xf32 = [slotF[:, 1024:3072], slotF[:, 3072:5120]]
        sp_st = {}
        v_sqrt = 0
        for tc in range(TCH):
            par = tc % 2
            w = dict(first_wait) if tc == 0 else {}
            w["c_act"] = max(w.get("c_act", 0), v_sqrt)
            w["c_dve"] = max(w.get("c_dve", 0), m_setup_dve)
            @S.step("dve", waits=w)
            def _(api, tc=tc):
                for j in range(4):
                    api.bn_stats(out=bn_t[:, j, :],
                                 in_=x_tm[:, tc, j * 512:(j + 1) * 512]).then_inc(S.h("c_dve"), 1)
            v_bn = S.bump("c_dve", 4)
            @S.step("dve", waits={"c_dve": v_bn})
            def _(api):
                api.bn_aggr(out=mv[:, :],
                            in_=slotF[:, 3104:3128]).then_inc(S.h("c_dve"), 1)
            v_ag = S.bump("c_dve")
            @S.step("act", waits={"c_dve": v_ag})
            def _(api):
                api.activation(out=sd1[:, :], in_=mv[:, 1:2], func=AF.Sqrt,
                               bias=eps_t[:, 0:1], scale=1.0).then_inc(S.h("c_act"), 1)
            v_sqrt = S.bump("c_act")
            @S.step("dve", waits={"c_act": v_sqrt})
            def _(api):
                api.reciprocal(out=rstd[:, :], in_=sd1[:, :]).then_inc(S.h("c_dve"), 1)
            v_rs = S.bump("c_dve")
            if lo_dram is None:
                @S.step("dve", waits=mx({"c_dve": v_rs},
                                        {"d_sp": sp_st.get(tc - 2, 0)}))
                def _(api, tc=tc, par=par):
                    api.tensor_scalar(out=xnc[par][:, :], in0=x_tm[:, tc, :],
                                      scalar1=mv[:, 0:1], scalar2=rstd[:, 0:1],
                                      op0=ALU.subtract, op1=ALU.mult).then_inc(S.h("c_dve"), 1)
                v_nm = S.bump("c_dve")
            else:
                @S.step("dve", waits=mx({"c_dve": v_rs},
                                        {"d_sp": sp_st.get(tc - 2, 0)}))
                def _(api, tc=tc, par=par):
                    api.tensor_scalar(out=xf32[par][:, :], in0=x_tm[:, tc, :],
                                      scalar1=mv[:, 0:1], scalar2=rstd[:, 0:1],
                                      op0=ALU.subtract, op1=ALU.mult).then_inc(S.h("c_dve"), 1)
                v_f = S.bump("c_dve")
                @S.step("dve", waits={"c_dve": v_f})
                def _(api, par=par):
                    api.tensor_copy(out=xnc[par][:, :],
                                    in_=xf32[par][:, :]).then_inc(S.h("c_dve"), 1)
                v_hi = S.bump("c_dve")
                @S.step("dve", waits={"c_dve": v_hi})
                def _(api, par=par):
                    api.tensor_tensor(out=xloc[par][:, :], in0=xf32[par][:, :],
                                      in1=xnc[par][:, :],
                                      op=ALU.subtract).then_inc(S.h("c_dve"), 1)
                v_nm = S.bump("c_dve")
            @S.step("sp", waits={"c_dve": v_nm})
            def _(api, tc=tc, par=par):
                api.dma_start(out=dram_out[tc * 128:(tc + 1) * 128, :],
                              in_=xnc[par][:, :]).then_inc(S.h("d_sp"), 16)
                if lo_dram is not None:
                    api.dma_start(out=lo_dram[tc * 128:(tc + 1) * 128, :],
                                  in_=xloc[par][:, :]).then_inc(S.h("d_sp"), 16)
            v_st = S.bump("d_sp", 16 if lo_dram is None else 32)
            sp_st[tc] = v_st
            @S.step("act", waits={"d_sp": v_st})
            def _(api, tc=tc):
                api.dma_start(out=dest_fm[:, :, tc * 128:(tc + 1) * 128],
                              in_=dram_out[tc * 128:(tc + 1) * 128, :],
                              transpose=True).then_inc(S.h("d_aq"), 16)
                if lo_fm is not None:
                    api.dma_start(out=lo_fm[:, :, tc * 128:(tc + 1) * 128],
                                  in_=lo_dram[tc * 128:(tc + 1) * 128, :],
                                  transpose=True).then_inc(S.h("d_aq"), 16)
            S.bump("d_aq", 16 if lo_fm is None else 32)
        return S.n("d_aq"), S.n("d_sp")

    xn_fm = slotA.rearrange("p (a t) -> p a t", t=1024)
    aq_xn1, sp_xn1 = ph_layernorm(xn1_dr, xn_fm, {"d_sp": n_setup})
    if debug:
        @S.step("sp", waits={"d_sp": sp_xn1})
        def _(api):
            api.dma_start(out=dbg["d_xn1"][:, :], in_=xn1_dr[:, :]).then_inc(S.h("d_sp"), 16)
        S.bump("d_sp", 16)

    S.barrier()

    q_fm = slotB.rearrange("p (a t) -> p a t", t=1024)
    wbuf = [slotC[:, 0:2048].rearrange("p (a j) -> p a j", j=128),
            slotC[:, 2048:4096].rearrange("p (a j) -> p a j", j=128)]
    kst = [slotC[:, 4096:5120], slotC[:, 5120:6144]]
    sqt = [slotC[:, 6144:7168], slotC[:, 7168:8192]]

    with nc.psum_tensor("pg", [128, 2, 1024], F32) as pgt, \
         nc.psum_tensor("qn", [32, 1024], F32) as qn_ps, \
         nc.psum_tensor("kn", [32, 1024], F32) as kn_ps:

        v_wb_free = {0: 0, 1: 0}
        v_ps_free = {0: 0, 1: 0}
        v_sq_free = {0: 0, 1: 0}
        v_kst_free = {0: 0, 1: 0}
        mats = [("q", wq_d, None, qn_ps), ("k", wk_d, kvK_in, kn_ps),
                ("v", wv_d, kvV_in, None)]
        for mi, (mn, wd, st_dram, nrm_ps) in enumerate(mats):
            for co in range(16):
                it = mi * 16 + co
                par = it % 2
                @S.step("pool", waits={"c_pe": v_wb_free[par],
                                       "d_pq": n_zero_pq})
                def _(api, wd=wd, co=co, par=par):
                    api.dma_start(out=wbuf[par].rearrange("p a j -> p (a j)")[:, :],
                                  in_=wd[co].rearrange("p a j -> p (a j)")[:, :]
                                  ).then_inc(S.h("d_pq"), 16)
                v_w = S.bump("d_pq", 16)
                @S.step("pe", waits={"d_pq": v_w, "d_aq": aq_xn1,
                                     "c_act": v_ps_free[par]})
                def _(api, par=par):
                    for ci in range(16):
                        for hf in range(2):
                            api.matmul(pgt[:, par, hf * 512:(hf + 1) * 512],
                                       wbuf[par][:, ci, :],
                                       xn_fm[:, ci, hf * 512:(hf + 1) * 512],
                                       start=(ci == 0), stop=(ci == 15)
                                       ).then_inc(S.h("c_pe"), 1)
                v_mm = S.bump("c_pe", 32)
                v_wb_free[par] = v_mm
                if mn == "q":
                    @S.step("act", waits={"c_pe": v_mm})
                    def _(api, co=co, par=par):
                        api.activation(out=q_fm[:, co, :], in_=pgt[:, par, :],
                                       func=AF.Identity, bias=bqs[:, co:co + 1],
                                       scale=1.0 / SCALE).then_inc(S.h("c_act"), 1)
                    v_ev = S.bump("c_act")
                    v_ps_free[par] = v_ev
                    @S.step("act", waits=mx({"c_act": v_ev},
                                            {"c_pe": v_sq_free[par]}))
                    def _(api, co=co, par=par):
                        api.activation(out=sqt[par][:, :], in_=q_fm[:, co, :],
                                       func=AF.Square).then_inc(S.h("c_act"), 1)
                    v_sq = S.bump("c_act")
                    @S.step("pe", waits={"c_act": v_sq})
                    def _(api, co=co, par=par, nrm_ps=nrm_ps):
                        sq = sqt[par].rearrange("p (a b) -> p a b", b=512)
                        for hf in range(2):
                            api.matmul(nrm_ps[:, hf * 512:(hf + 1) * 512],
                                       qmask[:, co, :], sq[:, hf, :],
                                       start=(co == 0), stop=(co == 15)
                                       ).then_inc(S.h("c_pe"), 1)
                    v_sq_free[par] = S.bump("c_pe", 2)
                else:
                    kpar = co % 2
                    @S.step("act", waits=mx({"c_pe": v_mm},
                                            {"d_sp": v_kst_free[kpar]}))
                    def _(api, co=co, par=par, kpar=kpar,
                          bias=(bk_t if mn == "k" else bv_t)):
                        api.activation(out=kst[kpar][:, :], in_=pgt[:, par, :],
                                       func=AF.Identity, bias=bias[:, co:co + 1],
                                       scale=1.0).then_inc(S.h("c_act"), 1)
                    v_ev = S.bump("c_act")
                    v_ps_free[par] = v_ev
                    @S.step("sp", waits={"c_act": v_ev})
                    def _(api, co=co, kpar=kpar, st_dram=st_dram):
                        api.dma_start(out=st_dram[co * 128:(co + 1) * 128, :],
                                      in_=kst[kpar][:, :]).then_inc(S.h("d_sp"), 16)
                    v_kst_free[kpar] = S.bump("d_sp", 16)
                    if mn == "k":
                        @S.step("act", waits=mx({"c_act": v_ev},
                                                {"c_pe": v_sq_free[par]}))
                        def _(api, kpar=kpar, par=par):
                            api.activation(out=sqt[par][:, :], in_=kst[kpar][:, :],
                                           func=AF.Square).then_inc(S.h("c_act"), 1)
                        v_sq = S.bump("c_act")
                        @S.step("pe", waits={"c_act": v_sq})
                        def _(api, co=co, par=par, nrm_ps=nrm_ps):
                            sq = sqt[par].rearrange("p (a b) -> p a b", b=512)
                            for hf in range(2):
                                api.matmul(nrm_ps[:, hf * 512:(hf + 1) * 512],
                                           qmask[:, co, :], sq[:, hf, :],
                                           start=(co == 0), stop=(co == 15)
                                           ).then_inc(S.h("c_pe"), 1)
                        v_sq_free[par] = S.bump("c_pe", 2)
        sp_kv = S.n("d_sp")
        pe_p2 = S.n("c_pe")

        @S.step("pool", waits={"d_sp": sp_kv})
        def _(api):
            api.collective_compute(
                "AllGather", ALU.bypass,
                replica_groups=[[0, 1, 2, 3], [4, 5, 6, 7]],
                ins=[kvK_in[:, :]], outs=[K_all[:, :]]).then_inc(S.h("cc_k"), 1)
            api.collective_compute(
                "AllGather", ALU.bypass,
                replica_groups=[[0, 1, 2, 3], [4, 5, 6, 7]],
                ins=[kvV_in[:, :]], outs=[V_all[:, :]]).then_inc(S.h("cc_v"), 1)
        S.bump("cc_k")
        S.bump("cc_v")

        @S.step("dve", waits={"c_pe": pe_p2})
        def _(api):
            api.tensor_reduce(out=qmax[:, :], in_=qn_ps[:, :], axis=AX,
                              op=ALU.max).then_inc(S.h("c_dve"), 1)
            api.tensor_reduce(out=kmax[:, :], in_=kn_ps[:, :], axis=AX,
                              op=ALU.max).then_inc(S.h("c_dve"), 1)
        v_nr = S.bump("c_dve", 2)
        @S.step("act", waits={"c_dve": v_nr})
        def _(api):
            api.activation(out=qmax[:, :], in_=qmax[:, :],
                           func=AF.Sqrt).then_inc(S.h("c_act"), 1)
            api.activation(out=kmax[:, :], in_=kmax[:, :],
                           func=AF.Sqrt).then_inc(S.h("c_act"), 1)
        v_sqn = S.bump("c_act", 2)
        @S.step("sp", waits={"c_act": v_sqn})
        def _(api):
            api.dma_start(out=kn_in[:, :], in_=kmax[:, :]).then_inc(S.h("d_sp"), 16)
        v_knst = S.bump("d_sp", 16)
        @S.step("pool", waits={"d_sp": v_knst})
        def _(api):
            api.collective_compute(
                "AllGather", ALU.bypass,
                replica_groups=[[0, 1, 2, 3], [4, 5, 6, 7]],
                ins=[kn_in[:, :]], outs=[kn_all[:, :]]).then_inc(S.h("cc_kn"), 1)
        S.bump("cc_kn")
        @S.step("pool", waits={"cc_kn": 1})
        def _(api):
            api.dma_start(out=knal[:, :],
                          in_=kn_all.rearrange("(r h) o -> h (r o)", h=32)[:, :],
                          allow_slow_non_contiguous=True).then_inc(S.h("d_pq"), 16)
        v_knl = S.bump("d_pq", 16)
        @S.step("dve", waits={"d_pq": v_knl, "c_act": v_sqn})
        def _(api):
            api.tensor_reduce(out=kmax[:, :], in_=knal[:, :], axis=AX,
                              op=ALU.max).then_inc(S.h("c_dve"), 1)
        v_km = S.bump("c_dve")
        @S.step("dve", waits={"c_dve": v_km})
        def _(api):
            api.tensor_tensor(out=qmax[:, :], in0=qmax[:, :], in1=kmax[:, :],
                              op=ALU.mult).then_inc(S.h("c_dve"), 1)
        v_ch = S.bump("c_dve")
        @S.step("dve", waits={"c_dve": v_ch})
        def _(api):
            api.tensor_scalar(out=qmax[:, :], in0=qmax[:, :], scalar1=1.0,
                              scalar2=-1.0, op0=ALU.add,
                              op1=ALU.mult).then_inc(S.h("c_dve"), 1)
        v_negc = S.bump("c_dve")

    with nc.psum_tensor("ngT", [1, 32], F32) as ngT, \
         nc.psum_tensor("ngB", [128, 32], F32) as ngB:
        @S.step("pe", waits={"c_dve": v_negc})
        def _(api):
            api.transpose(ngT[:, :], qmax[:, :], idf[0:32, 0:32]).then_inc(S.h("c_pe"), 1)
        v_t = S.bump("c_pe")
        @S.step("dve", waits={"c_pe": v_t})
        def _(api):
            api.tensor_copy(out=negcT[:, :], in_=ngT[:, :]).then_inc(S.h("c_dve"), 1)
        v_tc2 = S.bump("c_dve")
        @S.step("pe", waits={"c_dve": v_tc2})
        def _(api):
            for hh in range(NH):
                api.matmul(ngB[:, hh:hh + 1], onesf[:, :], negcT[:, hh:hh + 1],
                           start=True, stop=True).then_inc(S.h("c_pe"), 1)
        v_bc = S.bump("c_pe", NH)
        @S.step("dve", waits={"c_pe": v_bc})
        def _(api):
            api.tensor_copy(out=negc_sb[:, :], in_=ngB[:, :]).then_inc(S.h("c_dve"), 1)
        v_negcd = S.bump("c_dve")

    if debug:
        @S.step("sp", waits={"c_dve": v_negcd, "cc_k": 1, "cc_v": 1})
        def _(api):
            api.dma_start(out=dbg["d_negc"][:, :], in_=negc_sb[:, :]).then_inc(S.h("d_sp"), 16)
            api.dma_start(out=dbg["d_q"][:, :],
                          in_=q_fm.rearrange("p a t -> p (a t)")[:, :]).then_inc(S.h("d_sp"), 16)
            api.dma_start(out=dbg["d_kall"][:, :], in_=K_all[:, :]).then_inc(S.h("d_sp"), 16)
            api.dma_start(out=dbg["d_vall"][:, :], in_=V_all[:, :]).then_inc(S.h("d_sp"), 16)
        S.bump("d_sp", 64)

    S.barrier()

    feats_fm = slotA.rearrange("p (a t) -> p a t", t=1024)
    kp_t = [slotC[:, 0:4096], slotC[:, 4096:8192]]
    vt_t = [slotC[:, 8192 + i * 2560: 8192 + (i + 1) * 2560]
            .rearrange("p (a b) -> p a b", b=VPAD) for i in range(4)]
    pt_t = [slotC[:, 18432:19456], slotC[:, 19456:20480]]
    fsb = slotC[:, 20480:21504]
    recb = slotC[:, 21504:22528]

    with nc.psum_tensor("sS", [128, 2, 1024], F32) as sS, \
         nc.psum_tensor("fts", [65, 1024], F32) as fts, \
         nc.psum_tensor("rbc", [64, 1024], F32) as rbc:

        @S.step("dve")
        def _(api):
            for i in range(4):
                api.memset(vt_t[i][:, :, 64:65], 1.0).then_inc(S.h("c_dve"), 1)
        v_ones = S.bump("c_dve", 4)

        kp_ready = [0, 0]
        vt_ready = [0, 0, 0, 0]
        kp_free = [0, 0]
        vt_free = [0, 0, 0, 0]
        s_free = {0: 0, 1: 0}
        pt_free = {0: 0, 1: 0}
        fin = {"act": 0, "dve": 0, "mult": 0, "rb": 0}

        def load_pair(pr):
            slot = pr % 2
            @S.step("sp", waits={"cc_k": 1, "c_pe": kp_free[slot]})
            def _(api, pr=pr, slot=slot):
                kv = kp_t[slot].rearrange("p (r t) -> p r t", r=4)
                for r in range(4):
                    api.dma_start(
                        out=kv[:, r, :],
                        in_=K_all[r * H + pr * 128: r * H + (pr + 1) * 128, :]
                    ).then_inc(S.h("d_sp"), 16)
            kp_ready[slot] = S.bump("d_sp", 64)
            for hl in range(2):
                hh = 2 * pr + hl
                vslot = hh % 4
                @S.step("act", waits={"cc_v": 1, "c_dve": v_ones,
                                      "c_pe": vt_free[vslot]})
                def _(api, hh=hh, vslot=vslot):
                    for r in range(4):
                        api.dma_start(
                            out=vt_t[vslot][:, r * 8:(r + 1) * 8, 0:64],
                            in_=V_all[r * H + hh * 64: r * H + (hh + 1) * 64, :],
                            transpose=True).then_inc(S.h("d_aq"), 16)
                vt_ready[vslot] = S.bump("d_aq", 64)

        load_pair(0)
        load_pair(1)

        for hh in range(NH):
            pr = hh // 2
            hbase = (hh % 2) * 64
            hchunk = hh // 2
            kslot = pr % 2
            vslot = hh % 4
            kpr = kp_ready[kslot]
            vtr = vt_ready[vslot]

            def qk(kc):
                par = kc % 2
                @S.step("pe", waits=mx({"d_sp": kpr, "c_act": s_free[par]},
                                       {"c_dve": v_negcd}))
                def _(api, kc=kc, par=par, hbase=hbase, hchunk=hchunk,
                      kslot=kslot):
                    kv = kp_t[kslot].rearrange("p (r t) -> p r t", r=4)
                    r, lt = kc // 8, kc % 8
                    lhsT = kv[hbase:hbase + 64, r, lt * 128:(lt + 1) * 128]
                    for hf in range(2):
                        api.matmul(sS[:, par, hf * 512:(hf + 1) * 512], lhsT,
                                   q_fm[hbase:hbase + 64, hchunk,
                                        hf * 512:(hf + 1) * 512],
                                   start=True, stop=True).then_inc(S.h("c_pe"), 1)
                return S.bump("c_pe", 2)

            def ex(kc, v_qk):
                par = kc % 2
                @S.step("act", waits={"c_pe": max(v_qk, pt_free[par])})
                def _(api, kc=kc, par=par, hh=hh):
                    api.activation(out=pt_t[par][:, :], in_=sS[:, par, :],
                                   func=AF.Exp, bias=negc_sb[:, hh:hh + 1],
                                   scale=1.0).then_inc(S.h("c_act"), 1)
                return S.bump("c_act")

            def av(kc, v_ex):
                par = kc % 2
                w = {"c_act": v_ex, "d_aq": vtr}
                if kc == 0:
                    w = mx(w, {"c_act": fin["act"], "c_dve": fin["dve"]})
                @S.step("pe", waits=w)
                def _(api, kc=kc, par=par, vslot=vslot):
                    lhsT = vt_t[vslot][:, kc, 0:65]
                    for hf in range(2):
                        api.matmul(fts[:, hf * 512:(hf + 1) * 512], lhsT,
                                   pt_t[par][:, hf * 512:(hf + 1) * 512],
                                   start=(kc == 0), stop=(kc == NKC - 1)
                                   ).then_inc(S.h("c_pe"), 1)
                return S.bump("c_pe", 2)

            vq = {0: qk(0), 1: qk(1)}
            for kc in range(NKC):
                vex = ex(kc, vq[kc])
                s_free[kc % 2] = vex
                vav = av(kc, vex)
                pt_free[kc % 2] = vav
                if kc + 2 < NKC:
                    vq[kc + 2] = qk(kc + 2)
            v_avl = pt_free[(NKC - 1) % 2]
            kp_free[kslot] = max(kp_free[kslot], v_avl)
            vt_free[vslot] = v_avl

            if hh % 2 == 1 and pr + 2 <= 15:
                load_pair(pr + 2)

            @S.step("act", waits=mx({"c_pe": v_avl}, {"c_dve": fin["mult"]}))
            def _(api):
                api.activation(out=fsb[0:64, :], in_=fts[0:64, :],
                               func=AF.Copy).then_inc(S.h("c_act"), 1)
            fin["act"] = S.bump("c_act")
            @S.step("dve", waits=mx({"c_pe": max(v_avl, fin["rb"])}))
            def _(api):
                with nc.allow_low_precision(reason="softmax denom recip to bf16 operand"):
                    api.reciprocal(out=recb[64:65, :],
                                   in_=fts[64:65, :]).then_inc(S.h("c_dve"), 1)
            fin["dve"] = S.bump("c_dve")
            @S.step("pe", waits={"c_dve": max(fin["dve"], fin["mult"])})
            def _(api):
                for hf in range(2):
                    api.matmul(rbc[:, hf * 512:(hf + 1) * 512],
                               onesb[64:65, 0:64],
                               recb[64:65, hf * 512:(hf + 1) * 512],
                               start=True, stop=True).then_inc(S.h("c_pe"), 1)
            fin["rb"] = S.bump("c_pe", 2)
            @S.step("dve", waits={"c_pe": fin["rb"], "c_act": fin["act"]})
            def _(api, hh=hh, hbase=hbase, hchunk=hchunk):
                api.tensor_tensor(out=feats_fm[hbase:hbase + 64, hchunk, :],
                                  in0=fsb[0:64, :], in1=rbc[0:64, :],
                                  op=ALU.mult).then_inc(S.h("c_dve"), 1)
            fin["mult"] = S.bump("c_dve")

        v_attn = fin["mult"]

    if debug:
        @S.step("sp", waits={"c_dve": v_attn})
        def _(api):
            api.dma_start(out=dbg["d_feats"][:, :],
                          in_=feats_fm.rearrange("p a t -> p (a t)")[:, :]).then_inc(S.h("d_sp"), 16)
        S.bump("d_sp", 16)

    S.barrier()

    wob = [slotC[:, 0:512], slotC[:, 512:1024]]
    bo_bf = slotC[0:1, 1024:3072]
    @S.step("pool")
    def _(api):
        api.dma_start(out=bo_bf, in_=bo_d[:, :]).then_inc(S.h("d_pq"), 16)
    n_bo_pq = S.bump("d_pq", 16)
    with nc.psum_tensor("po", [128, TCH, 512], F32) as po:
        v_wo_free = {0: 0, 1: 0}
        v_ev = {}
        for fo4 in range(4):
            for ci in range(16):
                it = fo4 * 16 + ci
                par = it % 2
                @S.step("pool", waits={"c_pe": v_wo_free[par]})
                def _(api, fo4=fo4, ci=ci, par=par):
                    api.dma_start(out=wob[par][:, :],
                                  in_=wo_d[ci, :, fo4 * 512:(fo4 + 1) * 512]
                                  ).then_inc(S.h("d_pq"), 16)
                v_l = S.bump("d_pq", 16)
                w = {"d_pq": v_l, "c_dve": v_attn}
                if ci == 0:
                    w = mx(w, {"c_dve": max([v_ev.get((fo4 - 1, tb), 0)
                                             for tb in range(TCH)] + [0])})
                @S.step("pe", waits=w)
                def _(api, fo4=fo4, ci=ci, par=par):
                    for tb in range(TCH):
                        api.matmul(po[:, tb, :],
                                   feats_fm[:, ci, tb * 128:(tb + 1) * 128],
                                   wob[par][:, :], start=(ci == 0), stop=False
                                   ).then_inc(S.h("c_pe"), 1)
                v_wo_free[par] = S.bump("c_pe", TCH)
            @S.step("pe", waits={"d_pq": n_bo_pq})
            def _(api, fo4=fo4):
                for tb in range(TCH):
                    api.matmul(po[:, tb, :], onesb[0:1, :],
                               bo_bf[:, fo4 * 512:(fo4 + 1) * 512],
                               start=False, stop=True).then_inc(S.h("c_pe"), 1)
            v_bo = S.bump("c_pe", TCH)
            for tb in range(TCH):
                @S.step("dve", waits={"c_pe": v_bo})
                def _(api, fo4=fo4, tb=tb):
                    api.tensor_tensor(out=x_tm[:, tb, fo4 * 512:(fo4 + 1) * 512],
                                      in0=po[:, tb, :],
                                      in1=x_tm[:, tb, fo4 * 512:(fo4 + 1) * 512],
                                      op=ALU.add).then_inc(S.h("c_dve"), 1)
                v_ev[(fo4, tb)] = S.bump("c_dve")
    v_x2 = S.n("c_dve")

    if debug:
        @S.step("sp", waits={"c_dve": v_x2})
        def _(api):
            api.dma_start(out=dbg["d_x2"][:, :],
                          in_=x_tm.rearrange("p a f -> p (a f)")[:, :]).then_inc(S.h("d_sp"), 16)
        S.bump("d_sp", 16)

    S.barrier()

    xn2_fm = slotA.rearrange("p (a t) -> p a t", t=1024)
    xlo_fm = slotB.rearrange("p (a t) -> p a t", t=1024)
    aq_xn2, sp_xn2 = ph_layernorm(xn2_in, xn2_fm, {"c_dve": v_x2},
                                  lo_dram=xlo_dr, lo_fm=xlo_fm)
    @S.step("pool", waits={"d_sp": sp_xn2})
    def _(api):
        api.collective_compute(
            "AllGather", ALU.bypass, replica_groups=[list(range(8))],
            ins=[xn2_in[:, :]], outs=[xn2_all[0:NTOK, :]]).then_inc(S.h("cc_x2"), 1)
    S.bump("cc_x2")
    if debug:
        @S.step("sp", waits={"d_sp": sp_xn2})
        def _(api):
            api.dma_start(out=dbg["d_xn2"][:, :], in_=xn2_in[:, :]).then_inc(S.h("d_sp"), 16)
        S.bump("d_sp", 16)

    with nc.psum_tensor("lg", [32, 1024], F32) as lg, \
         nc.psum_tensor("trp", [128, 256], F32) as trp:
        @S.step("pe", waits={"d_aq": aq_xn2, "c_dve": m_setup_dve})
        def _(api):
            terms = [(gwt_bf, xn2_fm), (glo_bf, xn2_fm), (gwt_bf, xlo_fm)]
            for ti, (wv, xv) in enumerate(terms):
                for ci in range(16):
                    for hf in range(2):
                        api.matmul(lg[:, hf * 512:(hf + 1) * 512],
                                   wv[:, ci, :],
                                   xv[:, ci, hf * 512:(hf + 1) * 512],
                                   start=(ti == 0 and ci == 0),
                                   stop=(ti == 2 and ci == 15)).then_inc(S.h("c_pe"), 1)
        v_lg = S.bump("c_pe", 96)
        @S.step("act", waits={"c_pe": v_lg})
        def _(api):
            api.activation(out=sco_f[:, :], in_=lg[:, :],
                           func=AF.Sigmoid).then_inc(S.h("c_act"), 1)
        v_sg = S.bump("c_act")
        @S.step("dve", waits={"c_act": v_sg})
        def _(api):
            api.tensor_scalar(out=scb_f[:, :], in0=sco_f[:, :],
                              scalar1=gb[:, 0:1], scalar2=None,
                              op0=ALU.add).then_inc(S.h("c_dve"), 1)
        v_scb = S.bump("c_dve")
        @S.step("pe", waits={"c_dve": v_scb})
        def _(api):
            for tc in range(TCH):
                api.transpose(trp[:, tc * 32:(tc + 1) * 32],
                              sco_f[:, tc * 128:(tc + 1) * 128],
                              idf[0:32, 0:32]).then_inc(S.h("c_pe"), 1)
        v_t1 = S.bump("c_pe", TCH)
        @S.step("dve", waits={"c_pe": v_t1})
        def _(api):
            api.tensor_copy(out=sco_t.rearrange("p a b -> p (a b)")[:, :],
                            in_=trp[:, :]).then_inc(S.h("c_dve"), 1)
        v_c1 = S.bump("c_dve")
        @S.step("pe", waits={"c_dve": v_c1})
        def _(api):
            for tc in range(TCH):
                api.transpose(trp[:, tc * 32:(tc + 1) * 32],
                              scb_f[:, tc * 128:(tc + 1) * 128],
                              idf[0:32, 0:32]).then_inc(S.h("c_pe"), 1)
        v_t2 = S.bump("c_pe", TCH)
        @S.step("dve", waits={"c_pe": v_t2})
        def _(api):
            api.tensor_copy(out=scb_t.rearrange("p a b -> p (a b)")[:, :],
                            in_=trp[:, :]).then_inc(S.h("c_dve"), 1)
        v_sc_tm = S.bump("c_dve")

    hs_fm = slotB.rearrange("p (a t) -> p a t", t=1024)
    wgb = [slotC[:, 1024:3072].rearrange("p (a j) -> p a j", j=128),
           slotC[:, 3072:5120].rearrange("p (a j) -> p a j", j=128)]
    wub = [slotC[:, 5120:7168].rearrange("p (a j) -> p a j", j=128),
           slotC[:, 7168:9216].rearrange("p (a j) -> p a j", j=128)]
    gsb = [slotC[:, 9216:10240], slotC[:, 10240:11264]]

    vC = {}
    v_a_bf = {}

    def routing_tc(tc, cps):
        sc = scb_t[:, tc, :]
        sv = sco_t[:, tc, :]
        c = {"v": max(S.n("c_dve"), v_sc_tm)}

        def D(fn, extra=None):
            w = {"c_dve": c["v"]}
            if extra:
                w = mx(w, extra)
            @S.step("dve", waits=w)
            def _(api):
                fn(api).then_inc(S.h("c_dve"), 1)
            c["v"] = S.bump("c_dve")

        D(lambda api, sc=sc: api.tensor_copy(out=r_work[:, :], in_=sc))
        for g in range(4):
            D(lambda api, g=g: api.tensor_reduce(
                out=r_m1[:, :], in_=r_work[:, g * 8:(g + 1) * 8], axis=AX,
                op=ALU.max))
            D(lambda api, g=g: api.tensor_scalar(
                out=r_eq[:, :], in0=r_work[:, g * 8:(g + 1) * 8],
                scalar1=r_m1[:, 0:1], scalar2=None, op0=ALU.is_ge))
            D(lambda api: api.tensor_reduce(
                out=r_cnt[:, :], in_=r_eq[:, :], axis=AX, op=ALU.add))
            D(lambda api: api.tensor_scalar(
                out=r_t8[:, :], in0=r_eq[:, :], scalar1=BIG, scalar2=None,
                op0=ALU.mult))
            D(lambda api, g=g: api.tensor_tensor(
                out=r_t8[:, :], in0=r_work[:, g * 8:(g + 1) * 8],
                in1=r_t8[:, :], op=ALU.subtract))
            D(lambda api: api.tensor_reduce(
                out=r_m2[:, :], in_=r_t8[:, :], axis=AX, op=ALU.max))
            D(lambda api: api.tensor_scalar(
                out=r_w[:, :], in0=r_cnt[:, :], scalar1=1.5, scalar2=None,
                op0=ALU.is_ge))
            D(lambda api: api.tensor_tensor(
                out=r_d[:, :], in0=r_m1[:, :], in1=r_m2[:, :], op=ALU.subtract))
            D(lambda api: api.tensor_tensor(
                out=r_d[:, :], in0=r_w[:, :], in1=r_d[:, :], op=ALU.mult))
            D(lambda api: api.tensor_tensor(
                out=r_d[:, :], in0=r_m2[:, :], in1=r_d[:, :], op=ALU.add))
            D(lambda api, g=g: api.tensor_tensor(
                out=r_gs[:, g:g + 1], in0=r_m1[:, :], in1=r_d[:, :], op=ALU.add))
        D(lambda api: api.tensor_reduce(
            out=r_m1[:, :], in_=r_gs[:, :], axis=AX, op=ALU.max))
        D(lambda api: api.tensor_scalar(
            out=r_eq[:, 0:4], in0=r_gs[:, :], scalar1=r_m1[:, 0:1],
            scalar2=None, op0=ALU.is_ge))
        D(lambda api: api.tensor_reduce(
            out=r_cnt[:, :], in_=r_eq[:, 0:4], axis=AX, op=ALU.add))
        D(lambda api: api.tensor_scalar(
            out=r_t8[:, 0:4], in0=r_eq[:, 0:4], scalar1=BIG, scalar2=None,
            op0=ALU.mult))
        D(lambda api: api.tensor_tensor(
            out=r_t8[:, 0:4], in0=r_gs[:, :], in1=r_t8[:, 0:4],
            op=ALU.subtract))
        D(lambda api: api.tensor_reduce(
            out=r_m2[:, :], in_=r_t8[:, 0:4], axis=AX, op=ALU.max))
        D(lambda api: api.tensor_scalar(
            out=r_w[:, :], in0=r_cnt[:, :], scalar1=1.5, scalar2=None,
            op0=ALU.is_ge))
        D(lambda api: api.tensor_tensor(
            out=r_d[:, :], in0=r_m1[:, :], in1=r_m2[:, :], op=ALU.subtract))
        D(lambda api: api.tensor_tensor(
            out=r_d[:, :], in0=r_w[:, :], in1=r_d[:, :], op=ALU.mult))
        D(lambda api: api.tensor_tensor(
            out=r_th[:, :], in0=r_m2[:, :], in1=r_d[:, :], op=ALU.add))
        for g in range(4):
            D(lambda api, g=g: api.tensor_scalar(
                out=r_eq[:, g:g + 1], in0=r_gs[:, g:g + 1],
                scalar1=r_th[:, 0:1], scalar2=None, op0=ALU.is_ge))
            D(lambda api, g=g, sc=sc: api.tensor_scalar(
                out=r_work[:, g * 8:(g + 1) * 8], in0=sc[:, g * 8:(g + 1) * 8],
                scalar1=r_eq[:, g:g + 1], scalar2=None, op0=ALU.mult))
        for k in range(KTOP):
            src = r_work if k % 2 == 0 else r_work2
            dst = r_work2 if k % 2 == 0 else r_work
            D(lambda api, src=src: api.tensor_reduce(
                out=r_m1[:, :], in_=src[:, :], axis=AX, op=ALU.max))
            D(lambda api, src=src: api.tensor_scalar(
                out=r_r1[:, :], in0=src[:, :], scalar1=r_m1[:, 0:1],
                scalar2=None, op0=ALU.is_ge))
            D(lambda api: api.tensor_tensor(
                out=r_r1[:, :], in0=r_r1[:, :], in1=rev32[:, :], op=ALU.mult))
            D(lambda api: api.tensor_reduce(
                out=r_m2[:, :], in_=r_r1[:, :], axis=AX, op=ALU.max))
            D(lambda api, tc=tc, k=k: api.tensor_scalar(
                out=idxf[:, tc, k:k + 1], in0=r_m2[:, :], scalar1=-1.0,
                scalar2=32.0, op0=ALU.mult, op1=ALU.add))
            D(lambda api, tc=tc, k=k: api.tensor_scalar(
                out=r_oh[:, k, :], in0=ioe[:, :],
                scalar1=idxf[:, tc, k:k + 1], scalar2=None, op0=ALU.is_equal))
            D(lambda api, k=k, sv=sv: api.tensor_tensor(
                out=r_r1[:, :], in0=sv, in1=r_oh[:, k, :], op=ALU.mult))
            D(lambda api, k=k: api.tensor_reduce(
                out=r_wraw[:, k:k + 1], in_=r_r1[:, :], axis=AX, op=ALU.add))
            D(lambda api, k=k, src=src, dst=dst: api.scalar_tensor_tensor(
                out=dst[:, :], in0=r_oh[:, k, :], scalar=-BIG, in1=src[:, :],
                op0=ALU.mult, op1=ALU.add))
        for k in range(1, KTOP):
            for kp2 in range(k):
                D(lambda api, tc=tc, k=k, kp2=kp2: api.tensor_tensor(
                    out=r_d[:, :], in0=idxf[:, tc, kp2:kp2 + 1],
                    in1=idxf[:, tc, k:k + 1], op=ALU.is_equal))
                if kp2 == 0:
                    D(lambda api, tc=tc, k=k: api.tensor_copy(
                        out=t2v[:, tc, k:k + 1], in_=r_d[:, :]))
                else:
                    D(lambda api, tc=tc, k=k: api.tensor_tensor(
                        out=t2v[:, tc, k:k + 1], in0=t2v[:, tc, k:k + 1],
                        in1=r_d[:, :], op=ALU.add))
        D(lambda api: api.tensor_reduce(
            out=r_ws[:, :], in_=r_wraw[:, :], axis=AX, op=ALU.add))
        D(lambda api: api.tensor_scalar(
            out=r_ws[:, :], in0=r_ws[:, :], scalar1=1e-20, scalar2=None,
            op0=ALU.add))
        D(lambda api: api.reciprocal(out=r_ws[:, :], in_=r_ws[:, :]))
        D(lambda api, tc=tc: api.tensor_scalar(
            out=wkv[:, tc, :], in0=r_wraw[:, :], scalar1=r_ws[:, 0:1],
            scalar2=2.0, op0=ALU.mult, op1=ALU.mult))
        D(lambda api: api.tensor_tensor(
            out=r_a[:, :], in0=r_oh[:, 0, :], in1=r_oh[:, 1, :], op=ALU.add))
        D(lambda api: api.tensor_tensor(
            out=r_a[:, :], in0=r_a[:, :], in1=r_oh[:, 2, :], op=ALU.add))
        D(lambda api: api.tensor_tensor(
            out=r_a[:, :], in0=r_a[:, :], in1=r_oh[:, 3, :], op=ALU.add))
        D(lambda api, tc=tc: api.tensor_copy(out=anv[:, tc, :], in_=r_a[:, :]))
        v_a_bf[tc] = c["v"]
        @S.step("pe", waits={"c_dve": v_a_bf[tc]})
        def _(api, tc=tc):
            if tc == 0:
                api.matmul(cps[:, :], lstr[:, :], anv[:, 0, :],
                           start=True, stop=True).then_inc(S.h("c_pe"), 1)
            else:
                for j in range(tc):
                    api.matmul(cps[:, :], onesb[:, :], anv[:, j, :],
                               start=(j == 0), stop=False)
                api.matmul(cps[:, :], lstr[:, :], anv[:, tc, :],
                           start=False, stop=True).then_inc(S.h("c_pe"), 1)
        v_cp = S.bump("c_pe")
        @S.step("dve", waits={"c_pe": v_cp})
        def _(api, tc=tc):
            api.tensor_copy(out=csv[:, tc, :], in_=cps[:, :]).then_inc(S.h("c_dve"), 1)
        vC[tc] = S.bump("c_dve")

    with nc.psum_tensor("pg2", [128, 2, 1024], F32) as pg2, \
         nc.psum_tensor("pu2", [128, 1024], F32) as pu2, \
         nc.psum_tensor("cps", [128, 32], F32) as cps, \
         nc.psum_tensor("totp", [1, 32], F32) as totp:
        v_gb_free = {0: 0, 1: 0}
        v_ub_free = {0: 0, 1: 0}
        v_pg_free = {0: 0, 1: 0}
        v_pu_free = 0
        for co in range(16):
            par = co % 2
            @S.step("pool", waits={"c_pe": v_gb_free[par]})
            def _(api, co=co, par=par):
                api.dma_start(out=wgb[par].rearrange("p a j -> p (a j)")[:, :],
                              in_=wsg_d[co].rearrange("p a j -> p (a j)")[:, :]
                              ).then_inc(S.h("d_pq"), 16)
            v_lg2 = S.bump("d_pq", 16)
            @S.step("pool", waits={"c_pe": v_ub_free[par]})
            def _(api, co=co, par=par):
                api.dma_start(out=wub[par].rearrange("p a j -> p (a j)")[:, :],
                              in_=wsu_d[co].rearrange("p a j -> p (a j)")[:, :]
                              ).then_inc(S.h("d_pq"), 16)
            v_lu2 = S.bump("d_pq", 16)
            @S.step("pe", waits={"d_pq": v_lg2, "d_aq": aq_xn2,
                                 "c_act": v_pg_free[par]})
            def _(api, par=par):
                for ci in range(16):
                    for hf in range(2):
                        api.matmul(pg2[:, par, hf * 512:(hf + 1) * 512],
                                   wgb[par][:, ci, :],
                                   xn2_fm[:, ci, hf * 512:(hf + 1) * 512],
                                   start=(ci == 0), stop=(ci == 15)).then_inc(S.h("c_pe"), 1)
            v_g2 = S.bump("c_pe", 32)
            v_gb_free[par] = v_g2
            @S.step("pe", waits={"d_pq": v_lu2, "c_dve": v_pu_free})
            def _(api, par=par):
                for ci in range(16):
                    for hf in range(2):
                        api.matmul(pu2[:, hf * 512:(hf + 1) * 512],
                                   wub[par][:, ci, :],
                                   xn2_fm[:, ci, hf * 512:(hf + 1) * 512],
                                   start=(ci == 0), stop=(ci == 15)).then_inc(S.h("c_pe"), 1)
            v_u2 = S.bump("c_pe", 32)
            v_ub_free[par] = v_u2
            @S.step("act", waits={"c_pe": v_g2})
            def _(api, par=par):
                api.activation(out=gsb[par][:, :], in_=pg2[:, par, :],
                               func=AF.Silu).then_inc(S.h("c_act"), 1)
            v_si = S.bump("c_act")
            v_pg_free[par] = v_si
            @S.step("dve", waits={"c_act": v_si, "c_pe": max(v_u2, v_lg)})
            def _(api, co=co, par=par):
                api.tensor_tensor(out=hs_fm[:, co, :], in0=gsb[par][:, :],
                                  in1=pu2[:, :], op=ALU.mult).then_inc(S.h("c_dve"), 1)
            v_pu_free = S.bump("c_dve")
            if co % 2 == 1 and co // 2 < TCH:
                routing_tc(co // 2, cps)
        for tc in range(TCH):
            if tc not in vC:
                routing_tc(tc, cps)
        v_hs_done = v_pu_free

        @S.step("pe", waits={"c_dve": v_a_bf[TCH - 1]})
        def _(api):
            for tc in range(TCH):
                api.matmul(totp[:, :], onesb[:, 0:1], anv[:, tc, :],
                           start=(tc == 0), stop=(tc == TCH - 1)).then_inc(S.h("c_pe"), 1)
        v_tp = S.bump("c_pe", TCH)
        @S.step("dve", waits={"c_pe": v_tp})
        def _(api):
            api.tensor_copy(out=totsb[:, :], in_=totp[:, :]).then_inc(S.h("c_dve"), 1)
        v_ts = S.bump("c_dve")
        @S.step("sp", waits={"c_dve": v_ts})
        def _(api):
            api.dma_start(out=tot_in[:, :], in_=totsb[:, :]).then_inc(S.h("d_sp"), 16)
        v_tst = S.bump("d_sp", 16)
        @S.step("pool", waits={"d_sp": v_tst})
        def _(api):
            api.collective_compute(
                "AllGather", ALU.bypass, replica_groups=[list(range(8))],
                ins=[tot_in[:, :]], outs=[tot_all[:, :]]).then_inc(S.h("cc_tot"), 1)
        S.bump("cc_tot")
        @S.step("pool", waits={"cc_tot": 1})
        def _(api):
            api.dma_start(out=tota[:, :], in_=tot_all[:, :]).then_inc(S.h("d_pq"), 16)
        v_tl = S.bump("d_pq", 16)

    v_hs_pe = S.n("c_pe")

    with nc.psum_tensor("ofp", [1, 32], F32) as ofp, \
         nc.psum_tensor("ofb", [128, 32], F32) as ofb:
        @S.step("pe", waits={"d_pq": v_tl})
        def _(api):
            api.matmul(ofp[:, :], cmask[:, :], tota[:, :],
                       start=True, stop=True).then_inc(S.h("c_pe"), 1)
        v_of = S.bump("c_pe")
        @S.step("dve", waits={"c_pe": v_of})
        def _(api):
            api.tensor_copy(out=offs1[:, :], in_=ofp[:, :]).then_inc(S.h("c_dve"), 1)
        v_o1 = S.bump("c_dve")
        @S.step("pe", waits={"c_dve": v_o1})
        def _(api):
            api.matmul(ofb[:, :], onesf[:, :], offs1[:, :],
                       start=True, stop=True).then_inc(S.h("c_pe"), 1)
        v_ob2 = S.bump("c_pe")
        @S.step("dve", waits={"c_pe": v_ob2})
        def _(api):
            api.tensor_copy(out=offsb[:, :], in_=ofb[:, :]).then_inc(S.h("c_dve"), 1)
        v_offs = S.bump("c_dve")

    c2 = {"v": S.n("c_dve")}

    def D2(fn, extra=None):
        w = {"c_dve": c2["v"]}
        if extra:
            w = mx(w, extra)
        @S.step("dve", waits=w)
        def _(api):
            fn(api).then_inc(S.h("c_dve"), 1)
        c2["v"] = S.bump("c_dve")

    v_gidx = {}
    v_scat = 0
    for tc in range(TCH):
        for k in range(KTOP):
            D2(lambda api, tc=tc, k=k: api.tensor_scalar(
                out=r_a[:, :], in0=ioe[:, :], scalar1=idxf[:, tc, k:k + 1],
                scalar2=None, op0=ALU.is_equal),
               {"c_dve": max(v_offs, vC[tc])})
            D2(lambda api, tc=tc: api.tensor_tensor(
                out=r_r1[:, :], in0=csv[:, tc, :], in1=r_a[:, :], op=ALU.mult))
            D2(lambda api: api.tensor_reduce(
                out=r_cat[:, :], in_=r_r1[:, :], axis=AX, op=ALU.add))
            D2(lambda api: api.tensor_tensor(
                out=r_r1[:, :], in0=offsb[:, :], in1=r_a[:, :], op=ALU.mult))
            D2(lambda api: api.tensor_reduce(
                out=r_oat[:, :], in_=r_r1[:, :], axis=AX, op=ALU.add))
            D2(lambda api, tc=tc, k=k: api.tensor_tensor(
                out=r_pos[:, :], in0=r_cat[:, :], in1=t2v[:, tc, k:k + 1],
                op=ALU.add))
            D2(lambda api: api.tensor_tensor(
                out=r_pos[:, :], in0=r_pos[:, :], in1=r_oat[:, :], op=ALU.add))
            D2(lambda api: api.tensor_scalar(
                out=r_kp[:, :], in0=r_pos[:, :], scalar1=511.5, scalar2=None,
                op0=ALU.is_le))
            D2(lambda api, tc=tc, k=k: api.tensor_scalar(
                out=r_g0[:, :], in0=idxf[:, tc, k:k + 1], scalar1=512.0,
                scalar2=None, op0=ALU.mult))
            D2(lambda api: api.tensor_tensor(
                out=r_g0[:, :], in0=r_g0[:, :], in1=r_pos[:, :], op=ALU.add))
            D2(lambda api: api.tensor_scalar(
                out=r_adj[:, :], in0=r_kp[:, :], scalar1=-float(NSLOT),
                scalar2=float(NSLOT), op0=ALU.mult, op1=ALU.add))
            D2(lambda api: api.tensor_tensor(
                out=r_g0[:, :], in0=r_g0[:, :], in1=r_kp[:, :], op=ALU.mult))
            D2(lambda api, tc=tc, k=k: api.tensor_tensor(
                out=gidx_f[:, tc, k:k + 1], in0=r_g0[:, :],
                in1=r_adj[:, :], op=ALU.add))
            D2(lambda api, tc=tc, k=k: api.tensor_copy(
                out=gidx[:, tc, k:k + 1], in_=gidx_f[:, tc, k:k + 1]))
            v_gidx[(tc, k)] = c2["v"]
    for tc in range(TCH):
        D2(lambda api, tc=tc: api.tensor_scalar(
            out=r_tok[:, :], in0=pcol[:, :], scalar1=float(tc * 128 + 1),
            scalar2=None, op0=ALU.add),
           {"d_pq": v_scat})
        v_tok = c2["v"]
        @S.step("pool", waits={"c_dve": max(v_tok, v_gidx[(tc, KTOP - 1)])})
        def _(api, tc=tc):
            for k in range(KTOP):
                api.indirect_dma_start(
                    out=map_in[:, :],
                    out_offset=IOA(ap=gidx[:, tc, k:k + 1], axis=0),
                    in_=r_tok[:, 0:1], in_offset=None,
                ).then_inc(S.h("d_pq"), 16)
        v_scat = S.bump("d_pq", 16 * KTOP)

    @S.step("pool", waits={"d_pq": v_scat})
    def _(api):
        api.collective_compute(
            "AllReduce", ALU.add, replica_groups=[list(range(8))],
            ins=[map_in[:, :]], outs=[map_all[:, :]]).then_inc(S.h("cc_map"), 1)
    S.bump("cc_map")

    if debug:
        @S.step("sp", waits={"c_dve": c2["v"]})
        def _(api):
            api.dma_start(out=dbg["d_sc"][:, :],
                          in_=scb_t.rearrange("p a b -> p (a b)")[:, :]).then_inc(S.h("d_sp"), 16)
            api.dma_start(out=dbg["d_idx"][:, :],
                          in_=idxf.rearrange("p a b -> p (a b)")[:, :]).then_inc(S.h("d_sp"), 16)
            api.dma_start(out=dbg["d_wk"][:, :],
                          in_=wkv.rearrange("p a b -> p (a b)")[:, :]).then_inc(S.h("d_sp"), 16)
            api.dma_start(out=dbg["d_gidx"][:, :],
                          in_=gidx.rearrange("p a b -> p (a b)")[:, :]).then_inc(S.h("d_sp"), 16)
            api.dma_start(out=dbg["d_cs"][:, :],
                          in_=csv.rearrange("p a b -> p (a b)")[:, :]).then_inc(S.h("d_sp"), 16)
            api.dma_start(out=dbg["d_offs"][:, :], in_=offsb[:, :]).then_inc(S.h("d_sp"), 16)
        S.bump("d_sp", 96)
        @S.step("pool", waits={"cc_map": 1})
        def _(api):
            api.dma_start(out=dbg["d_map"][:, :], in_=map_all[:, :]).then_inc(S.h("d_pq"), 16)
        S.bump("d_pq", 16)

    S.barrier()

    w2sb = slotA.rearrange("p (a f) -> p a f", f=2048)
    buf_fm = slotC[:, 0:8192].rearrange("p (a b) -> p a b", b=512)
    obsb = [slotC[:, 0:2048], slotC[:, 2048:4096]]
    buf_tm = slotC[:, 8192:10240]
    hh_fm = slotC[:, 10240:14336].rearrange("p (a b) -> p a b", b=512)
    w1b = [slotC[:, 14336:16384].rearrange("p (a j) -> p a j", j=128),
           slotC[:, 16384:18432].rearrange("p (a j) -> p a j", j=128)]
    w3b = [slotC[:, 18432:20480].rearrange("p (a j) -> p a j", j=128),
           slotC[:, 20480:22528].rearrange("p (a j) -> p a j", j=128)]

    v_w2_free = v_hs_pe
    v_ob_store = 0
    v_gath_d = 0
    v_pq_prev = 0
    for le in range(EPC):
        @S.step("pool", waits={"c_pe": v_w2_free})
        def _(api, le=le):
            for ci in range(8):
                api.dma_start(out=w2sb[:, ci, :],
                              in_=w2_d[le, ci, :, :]).then_inc(S.h("d_pq"), 16)
        v_w2 = S.bump("d_pq", 16 * 8)
        v_gfin = 0
        for j in range(4):
            c3 = {"v": S.n("c_dve")}

            def D3(fn, extra=None):
                w = {"c_dve": c3["v"]}
                if extra:
                    w = mx(w, extra)
                @S.step("dve", waits=w)
                def _(api):
                    fn(api).then_inc(S.h("c_dve"), 1)
                c3["v"] = S.bump("c_dve")

            D3(lambda api: api.tensor_scalar(
                out=mrow[:, :], in0=pcol[:, :], scalar1=2.0, scalar2=None,
                op0=ALU.mult), {"d_pq": v_pq_prev})
            D3(lambda api: api.tensor_tensor(
                out=mrow[:, :], in0=mrow[:, :], in1=pidx[:, :],
                op=ALU.subtract))
            D3(lambda api, le=le, j=j: api.tensor_scalar(
                out=mrow[:, :], in0=mrow[:, :],
                scalar1=float(le * CAP + j * 128), scalar2=None, op0=ALU.add))
            D3(lambda api: api.tensor_copy(out=mrow_i[:, :], in_=mrow[:, :]))
            v_mr = c3["v"]
            @S.step("pool", waits={"c_dve": v_mr, "cc_map": 1})
            def _(api):
                api.indirect_dma_start(
                    out=mapt[:, :], out_offset=None,
                    in_=map_all[:, :], in_offset=IOA(ap=mrow_i[:, :], axis=0),
                ).then_inc(S.h("d_pq"), 16)
            v_mg = S.bump("d_pq", 16)
            D3(lambda api: api.tensor_scalar(
                out=r_kp[:, :], in0=mapt[:, :], scalar1=0.5,
                scalar2=float(NTOK + 1), op0=ALU.is_le, op1=ALU.mult),
               {"d_pq": v_mg})
            D3(lambda api: api.tensor_scalar(
                out=r_g0[:, :], in0=mapt[:, :], scalar1=1.0, scalar2=None,
                op0=ALU.subtract))
            D3(lambda api: api.tensor_tensor(
                out=r_g0[:, :], in0=r_g0[:, :], in1=r_kp[:, :], op=ALU.add))
            D3(lambda api: api.tensor_copy(out=mapi[:, :], in_=r_g0[:, :]))
            v_mi = c3["v"]
            @S.step("pool", waits={"c_dve": v_mi, "cc_x2": 1,
                                   "d_sp": v_gath_d})
            def _(api):
                api.indirect_dma_start(
                    out=buf_tm[:, :], out_offset=None,
                    in_=xn2_all[:, :], in_offset=IOA(ap=mapi[:, :], axis=0),
                ).then_inc(S.h("d_pq"), 16)
            v_bg = S.bump("d_pq", 16)
            @S.step("sp", waits={"d_pq": v_bg})
            def _(api, le=le, j=j):
                api.dma_start(
                    out=bufdr[le * CAP + j * 128: le * CAP + (j + 1) * 128, :],
                    in_=buf_tm[:, :]).then_inc(S.h("d_sp"), 16)
            v_gfin = S.bump("d_sp", 16)
            v_gath_d = v_gfin
            v_pq_prev = v_bg
        @S.step("act", waits={"d_sp": max(v_gfin, v_ob_store)})
        def _(api, le=le):
            api.dma_start(out=buf_fm[:, :, :],
                          in_=bufdr[le * CAP:(le + 1) * CAP, :],
                          transpose=True).then_inc(S.h("d_aq"), 16)
        v_bfm = S.bump("d_aq", 16)
        if debug and le == EPC - 1:
            @S.step("sp", waits={"d_sp": v_gfin})
            def _(api):
                api.dma_start(out=dbg["d_buf"][:, :], in_=bufdr[:, :]).then_inc(S.h("d_sp"), 16)
            S.bump("d_sp", 16)

        with nc.psum_tensor(f"h1p{le}", [128, 2, 512], F32) as h1p, \
             nc.psum_tensor(f"h3p{le}", [128, 2, 512], F32) as h3p:
            v_w13_free = {0: 0, 1: 0}
            v_hh_free = {0: 0, 1: 0}
            for hc in range(8):
                par = hc % 2
                @S.step("pool", waits={"c_pe": v_w13_free[par]})
                def _(api, le=le, hc=hc, par=par):
                    api.dma_start(out=w1b[par].rearrange("p a j -> p (a j)")[:, :],
                                  in_=w1_d[le, hc].rearrange("p a j -> p (a j)")[:, :]
                                  ).then_inc(S.h("d_pq"), 16)
                    api.dma_start(out=w3b[par].rearrange("p a j -> p (a j)")[:, :],
                                  in_=w3_d[le, hc].rearrange("p a j -> p (a j)")[:, :]
                                  ).then_inc(S.h("d_pq"), 16)
                v_wld = S.bump("d_pq", 32)
                @S.step("pe", waits={"d_pq": v_wld, "d_aq": v_bfm,
                                     "c_dve": v_hh_free[par]})
                def _(api, par=par):
                    for ci in range(16):
                        api.matmul(h1p[:, par, :], w1b[par][:, ci, :],
                                   buf_fm[:, ci, :], start=(ci == 0),
                                   stop=(ci == 15)).then_inc(S.h("c_pe"), 1)
                    for ci in range(16):
                        api.matmul(h3p[:, par, :], w3b[par][:, ci, :],
                                   buf_fm[:, ci, :], start=(ci == 0),
                                   stop=(ci == 15)).then_inc(S.h("c_pe"), 1)
                v_hm = S.bump("c_pe", 32)
                v_w13_free[par] = v_hm
                @S.step("act", waits={"c_pe": v_hm})
                def _(api, hc=hc, par=par):
                    api.activation(out=hh_fm[:, hc, :], in_=h1p[:, par, :],
                                   func=AF.Silu).then_inc(S.h("c_act"), 1)
                v_si2 = S.bump("c_act")
                @S.step("dve", waits={"c_act": v_si2})
                def _(api, hc=hc, par=par):
                    api.tensor_tensor(out=hh_fm[:, hc, :], in0=hh_fm[:, hc, :],
                                      in1=h3p[:, par, :], op=ALU.mult
                                      ).then_inc(S.h("c_dve"), 1)
                v_hh_free[par] = S.bump("c_dve")
            v_hh_all = max(v_hh_free[0], v_hh_free[1])

        with nc.psum_tensor(f"obp{le}", [128, 2, H], F32) as obp:
            v_obe = {0: 0, 1: 0}
            v_obs = {0: 0, 1: 0}
            for j in range(4):
                par = j % 2
                @S.step("pe", waits={"c_dve": v_hh_all, "d_pq": v_w2,
                                     "c_act": v_obe[par]})
                def _(api, j=j, par=par):
                    for hc in range(8):
                        for q in range(4):
                            api.matmul(obp[:, par, q * 512:(q + 1) * 512],
                                       hh_fm[:, hc, j * 128:(j + 1) * 128],
                                       w2sb[:, hc, q * 512:(q + 1) * 512],
                                       start=(hc == 0), stop=(hc == 7)
                                       ).then_inc(S.h("c_pe"), 1)
                v_od = S.bump("c_pe", 32)
                v_w2_free = v_od
                @S.step("act", waits={"c_pe": v_od, "d_sp": v_obs[par]})
                def _(api, par=par):
                    api.activation(out=obsb[par][:, :], in_=obp[:, par, :],
                                   func=AF.Copy).then_inc(S.h("c_act"), 1)
                v_oe = S.bump("c_act")
                v_obe[par] = v_oe
                @S.step("sp", waits={"c_act": v_oe})
                def _(api, le=le, j=j, par=par):
                    api.dma_start(
                        out=ob_in[le * CAP + j * 128: le * CAP + (j + 1) * 128, :],
                        in_=obsb[par][:, :]).then_inc(S.h("d_sp"), 16)
                v_obs[par] = S.bump("d_sp", 16)
            v_ob_store = S.n("d_sp")

    @S.step("pool", waits={"d_sp": v_ob_store})
    def _(api):
        api.collective_compute(
            "AllGather", ALU.bypass, replica_groups=[list(range(8))],
            ins=[ob_in[:, :]], outs=[ob_all[0:NSLOT, :]]).then_inc(S.h("cc_ob"), 1)
    S.bump("cc_ob")
    if debug:
        @S.step("sp", waits={"d_sp": v_ob_store})
        def _(api):
            api.dma_start(out=dbg["d_ob"][:, :], in_=ob_in[:, :]).then_inc(S.h("d_sp"), 16)
        S.bump("d_sp", 16)

    S.barrier()

    ysb = [slotC[:, 0:2048], slotC[:, 2048:4096]]
    wsb2 = [slotC[:, 4096:4608], slotC[:, 4608:5120]]
    v_y_free = {0: 0, 1: 0}
    v_xc = 0
    for tc in range(TCH):
        vdv = S.n("c_dve")
        for k in range(KTOP):
            par = k % 2
            @S.step("pool", waits={"cc_ob": 1, "c_dve": v_y_free[par]})
            def _(api, tc=tc, k=k, par=par):
                api.indirect_dma_start(
                    out=ysb[par][:, :], out_offset=None,
                    in_=ob_all[:, :],
                    in_offset=IOA(ap=gidx[:, tc, k:k + 1], axis=0),
                ).then_inc(S.h("d_pq"), 16)
            v_yg = S.bump("d_pq", 16)
            if k == 0:
                @S.step("dve", waits={"d_pq": v_yg, "c_dve": vdv})
                def _(api, tc=tc):
                    api.tensor_scalar(out=acc[0][:, :], in0=ysb[0][:, :],
                                      scalar1=wkv[:, tc, 0:1], scalar2=None,
                                      op0=ALU.mult).then_inc(S.h("c_dve"), 1)
            else:
                @S.step("dve", waits={"d_pq": v_yg, "c_dve": S.n("c_dve")})
                def _(api, tc=tc, k=k, par=par):
                    api.scalar_tensor_tensor(
                        out=acc[par][:, :], in0=ysb[par][:, :],
                        scalar=wkv[:, tc, k:k + 1], in1=acc[1 - par][:, :],
                        op0=ALU.mult, op1=ALU.add).then_inc(S.h("c_dve"), 1)
            v_y_free[par] = S.bump("c_dve")
        @S.step("dve", waits={"c_dve": S.n("c_dve")})
        def _(api, tc=tc):
            api.tensor_tensor(out=x_tm[:, tc, :], in0=x_tm[:, tc, :],
                              in1=acc[(KTOP - 1) % 2][:, :], op=ALU.add
                              ).then_inc(S.h("c_dve"), 1)
        v_xc = S.bump("c_dve")
        v_y_free[(KTOP - 1) % 2] = v_xc
    v_moe = v_xc
    if debug:
        @S.step("sp", waits={"c_dve": v_moe})
        def _(api):
            api.dma_start(out=dbg["d_moe"][:, :],
                          in_=x_tm.rearrange("p a f -> p (a f)")[:, :]).then_inc(S.h("d_sp"), 16)
        S.bump("d_sp", 16)

    with nc.psum_tensor("sdp", [128, TCH, 512], F32) as sdp:
        v_sd_free = {0: 0, 1: 0}
        v_oev = {}
        v_ost = {0: 0, 1: 0}
        for fo4 in range(4):
            for ci in range(16):
                it = fo4 * 16 + ci
                par = it % 2
                @S.step("pool", waits={"c_pe": v_sd_free[par]})
                def _(api, fo4=fo4, ci=ci, par=par):
                    api.dma_start(out=wsb2[par][:, :],
                                  in_=wsd_d[ci, :, fo4 * 512:(fo4 + 1) * 512]
                                  ).then_inc(S.h("d_pq"), 16)
                v_l = S.bump("d_pq", 16)
                w = {"d_pq": v_l, "c_dve": v_moe}
                if ci == 0:
                    w = mx(w, {"c_dve": max([v_oev.get((fo4 - 1, tb), 0)
                                             for tb in range(TCH)] + [0])})
                @S.step("pe", waits=w)
                def _(api, fo4=fo4, ci=ci, par=par):
                    for tb in range(TCH):
                        api.matmul(sdp[:, tb, :],
                                   hs_fm[:, ci, tb * 128:(tb + 1) * 128],
                                   wsb2[par][:, :], start=(ci == 0),
                                   stop=(ci == 15)).then_inc(S.h("c_pe"), 1)
                v_sd_free[par] = S.bump("c_pe", TCH)
            v_last = v_sd_free[(fo4 * 16 + 15) % 2]
            for tb in range(TCH):
                @S.step("dve", waits={"c_pe": v_last, "d_sp": v_ost[tb % 2]})
                def _(api, fo4=fo4, tb=tb):
                    api.tensor_tensor(out=outst[tb % 2][:, :],
                                      in0=sdp[:, tb, :],
                                      in1=x_tm[:, tb, fo4 * 512:(fo4 + 1) * 512],
                                      op=ALU.add).then_inc(S.h("c_dve"), 1)
                v_oe = S.bump("c_dve")
                v_oev[(fo4, tb)] = v_oe
                @S.step("sp", waits={"c_dve": v_oe})
                def _(api, fo4=fo4, tb=tb):
                    api.dma_start(out=out_d[tb * 128:(tb + 1) * 128,
                                            fo4 * 512:(fo4 + 1) * 512],
                                  in_=outst[tb % 2][:, :]).then_inc(S.h("d_sp"), 16)
                v_ost[tb % 2] = S.bump("d_sp", 16)

    v_final = S.n("d_sp")
    @S.step("sp", waits={"d_sp": v_final})
    def _(api):
        pass

    with nc.Block() as block:
        @block.tensor
        def _(api):
            for f in S.prog["pe"]:
                f(api)

        @block.scalar
        def _(api):
            for f in S.prog["act"]:
                f(api)

        @block.vector
        def _(api):
            for f in S.prog["dve"]:
                f(api)

        @block.gpsimd
        def _(api):
            for f in S.prog["pool"]:
                f(api)

        @block.sync
        def _(api):
            for f in S.prog["sp"]:
                f(api)

    alloc.close()
    S.sem_stack.close()
    return nc


def _host_inputs(inputs):
    f32 = np.float32
    bf = ml_dtypes.bfloat16
    x = np.asarray(inputs["L0_attn_x"], f32).reshape(NTOK, H)

    def lhsT16(w):
        w = np.asarray(w, f32)
        return np.ascontiguousarray(
            w.reshape(16, 128, 16, 128).transpose(2, 1, 0, 3))

    wq = lhsT16(inputs["L0_attn_Wq"])
    wk = lhsT16(inputs["L0_attn_Wk"])
    wv = lhsT16(inputs["L0_attn_Wv"])
    wsg = lhsT16(inputs["L0_ffn_shared_gate"])
    wsu = lhsT16(inputs["L0_ffn_shared_up"])
    wo = np.ascontiguousarray(
        np.asarray(inputs["L0_attn_Wo"], f32).reshape(16, 128, H))
    wsd = np.ascontiguousarray(
        np.asarray(inputs["L0_ffn_shared_down"], f32).reshape(16, 128, H))

    def bias_t(b):
        return np.ascontiguousarray(np.asarray(b, f32).reshape(16, 128).T)

    bq_t = bias_t(inputs["L0_attn_bq"])
    bk_t = bias_t(inputs["L0_attn_bk"])
    bv_t = bias_t(inputs["L0_attn_bv"])
    bo_r = np.ascontiguousarray(np.asarray(inputs["L0_attn_bo"], f32).reshape(1, H))
    gwt = np.ascontiguousarray(
        np.asarray(inputs["L0_ffn_gate_w"], f32).T.reshape(16, 128, 32)
        .transpose(1, 0, 2))
    gb = np.ascontiguousarray(np.asarray(inputs["L0_ffn_gate_bias"], f32).reshape(32, 1))

    w1_all = np.asarray(inputs["L0_ffn_w1"], f32)
    w3_all = np.asarray(inputs["L0_ffn_w3"], f32)
    w2_all = np.asarray(inputs["L0_ffn_w2"], f32)

    idf = np.eye(128, dtype=f32)
    onesb = np.ones((128, 128), dtype=bf)
    p_i = np.arange(128)
    lstr = (p_i[:, None] < p_i[None, :]).astype(bf)
    onesf = np.ones((1, 128), dtype=f32)
    qmask = np.zeros((16, 128, 32), dtype=f32)
    for ci in range(16):
        qmask[ci, 0:64, 2 * ci] = 1
        qmask[ci, 64:128, 2 * ci + 1] = 1
    qmask = qmask.astype(bf)
    ioe = np.ascontiguousarray(
        np.broadcast_to(np.arange(32, dtype=f32), (128, 32)))
    pidx = np.arange(128, dtype=f32).reshape(128, 1)

    common = dict(wq=wq, wk=wk, wv=wv, wo=wo, bq_t=bq_t, bk_t=bk_t,
                  bv_t=bv_t, bo_r=bo_r, gwt=gwt, gb=gb, wsg=wsg, wsu=wsu,
                  wsd=wsd, idf=idf, onesb=onesb, lstr=lstr, onesf=onesf,
                  qmask=qmask, ioe=ioe, pidx=pidx)

    in_maps = []
    for c in range(NCORES):
        m = dict(common)
        m["x"] = np.ascontiguousarray(x[c * NT:(c + 1) * NT])
        es = slice(c * EPC, (c + 1) * EPC)
        m["w1"] = np.ascontiguousarray(
            w1_all[es].reshape(EPC, 16, 128, 8, 128).transpose(0, 3, 2, 1, 4))
        m["w3"] = np.ascontiguousarray(
            w3_all[es].reshape(EPC, 16, 128, 8, 128).transpose(0, 3, 2, 1, 4))
        m["w2"] = np.ascontiguousarray(w2_all[es].reshape(EPC, 8, 128, H))
        m["pcol"] = (np.arange(128, dtype=f32) + c * NT).reshape(128, 1)
        m["cmask"] = (np.arange(8) < c).astype(f32).reshape(8, 1)
        in_maps.append(m)
    return in_maps


_LAST_RESULTS = None


def kernel(**inputs):
    global _LAST_RESULTS
    in_maps = _host_inputs(inputs)
    nc = build_program(debug=DEBUG)
    res = run_bass_kernel_spmd(nc, in_maps, core_ids=list(range(NCORES)))
    _LAST_RESULTS = res
    out = np.concatenate([np.asarray(res.results[c]["out"], np.float32)
                          for c in range(NCORES)], 0)
    return out.reshape(2, 4096, H)


# version 16
# speedup vs baseline: 1.0261x; 1.0261x over previous
import math
from contextlib import ExitStack

import numpy as np
import ml_dtypes

import concourse.bass as bass
import concourse.mybir as mybir
from concourse.bass_utils import run_bass_kernel_spmd

F32 = mybir.dt.float32
BF16 = mybir.dt.bfloat16
I32 = mybir.dt.int32
AF = mybir.ActivationFunctionType
ALU = mybir.AluOpType
AX = mybir.AxisListType.X
IOA = bass.IndirectOffsetOnAxis

NCORES = 8
H = 2048
NT = 1024
TCH = 8
NH = 32
DH = 64
SCALE = math.sqrt(32.0)
NKC = 32
EPC = 4
CAP = 512
KTOP = 4
NTOK = 8192
NSLOT = 16384
BIG = 2.0e6
EPS = 1e-5
VPAD = 80

DEBUG = False
ENGS = ("pe", "act", "dve", "pool", "sp")


class KB:
    def __init__(self, nc):
        self.nc = nc
        self.prog = {e: [] for e in ENGS}
        self.sems = {}
        self.cnt = {}
        self.lastw = {e: {} for e in ENGS}
        self.sem_stack = ExitStack()

    def sem(self, *names):
        for name in names:
            self.sems[name] = self.sem_stack.enter_context(self.nc.semaphore(name))
            self.cnt[name] = 0

    def n(self, name):
        return self.cnt[name]

    def bump(self, name, k=1):
        self.cnt[name] += k
        return self.cnt[name]

    def h(self, name):
        return self.sems[name]

    def step(self, eng, waits=None):
        wl = []
        for s, v in (waits or {}).items():
            if v > self.lastw[eng].get(s, 0):
                self.lastw[eng][s] = v
                wl.append((self.sems[s], v))

        def deco(fn):
            def thunk(api, wl=wl, fn=fn):
                for hh, v in wl:
                    api.wait_ge(hh, v)
                fn(api)
            self.prog[eng].append(thunk)
        return deco

    def barrier(self):
        vals = {e: self.bump("b_" + e) for e in ENGS}
        for e in ENGS:
            def f(api, e=e, vals=vals):
                api.sem_inc(self.sems["b_" + e], 1)
                for o in ENGS:
                    if o != e:
                        api.wait_ge(self.sems["b_" + o], vals[o])
            self.prog[e].append(f)
            for o in ENGS:
                self.lastw[e]["b_" + o] = vals[o]


def mx(*ds):
    out = {}
    for d in ds:
        for k, v in d.items():
            out[k] = max(out.get(k, 0), v)
    return out


def build_program(debug=False):
    nc = bass.Bass("TRN2", target_bir_lowering=False)

    def din(name, shape, dt=F32):
        return nc.dram_tensor(name, shape, dt, kind="ExternalInput").ap()

    x_d = din("x", [NT, H])
    wq_d = din("wq", [16, 128, 16, 128])
    wk_d = din("wk", [16, 128, 16, 128])
    wv_d = din("wv", [16, 128, 16, 128])
    wo_d = din("wo", [16, 128, H])
    bq_d = din("bq_t", [128, 16])
    bk_d = din("bk_t", [128, 16])
    bv_d = din("bv_t", [128, 16])
    bo_d = din("bo_r", [1, H])
    gwt_d = din("gwt", [128, 16, 32])
    gb_d = din("gb", [32, 1])
    w1_d = din("w1", [EPC, 8, 128, 16, 128])
    w3_d = din("w3", [EPC, 8, 128, 16, 128])
    w2_d = din("w2", [EPC, 8, 128, H])
    wsg_d = din("wsg", [16, 128, 16, 128])
    wsu_d = din("wsu", [16, 128, 16, 128])
    wsd_d = din("wsd", [16, 128, H])
    idf_d = din("idf", [128, 128])
    onesb_d = din("onesb", [128, 128], BF16)
    lstr_d = din("lstr", [128, 128], BF16)
    onesf_d = din("onesf", [1, 128])
    qmask_d = din("qmask", [16, 128, 32], BF16)
    ioe_d = din("ioe", [128, 32])
    pcol_d = din("pcol", [128, 1])
    pidx_d = din("pidx", [128, 1])
    cmask_d = din("cmask", [8, 1])

    out_d = nc.dram_tensor("out", [NT, H], F32, kind="ExternalOutput").ap()

    def dint(name, shape, dt):
        return nc.dram_tensor(name, shape, dt).ap()

    kvK_in = dint("kvK_in", [H, NT], BF16)
    kvV_in = dint("kvV_in", [H, NT], BF16)
    K_all = dint("K_all", [4 * H, NT], BF16)
    V_all = dint("V_all", [4 * H, NT], BF16)
    kn_in = dint("kn_in", [32, 1], F32)
    kn_all = dint("kn_all", [128, 1], F32)
    xn1_dr = dint("xn1_dr", [NT, H], BF16)
    xn2_in = dint("xn2_in", [NT, H], BF16)
    xlo_dr = dint("xlo_dr", [NT, H], BF16)
    xn2_all = dint("xn2_all", [NTOK + 128, H], BF16)
    tot_in = dint("tot_in", [1, 32], F32)
    tot_all = dint("tot_all", [8, 32], F32)
    map_in = dint("map_in", [NSLOT + 128, 1], F32)
    map_all = dint("map_all", [NSLOT + 128, 1], F32)
    bufdr = dint("bufdr", [EPC * CAP, H], BF16)
    ob_in = dint("ob_in", [EPC * CAP, H], BF16)
    ob_all = dint("ob_all", [NSLOT + 128, H], BF16)

    dbg = {}
    if debug:
        for name, shape, dt in [
            ("d_xn1", [NT, H], BF16), ("d_kall", [4 * H, NT], BF16),
            ("d_vall", [4 * H, NT], BF16), ("d_q", [128, 16 * NT], BF16),
            ("d_negc", [128, 32], F32), ("d_feats", [128, 16 * NT], BF16),
            ("d_x2", [128, TCH * H], F32), ("d_xn2", [NT, H], BF16),
            ("d_sc", [128, TCH * 32], F32), ("d_idx", [128, TCH * 4], F32),
            ("d_wk", [128, TCH * 4], F32), ("d_gidx", [128, TCH * 4], I32),
            ("d_cs", [128, TCH * 32], F32), ("d_offs", [128, 32], F32),
            ("d_map", [NSLOT + 128, 1], F32), ("d_buf", [EPC * CAP, H], BF16),
            ("d_ob", [EPC * CAP, H], BF16), ("d_hs", [128, 16 * NT], BF16),
            ("d_moe", [128, TCH * H], F32),
        ]:
            dbg[name] = nc.dram_tensor(name, shape, dt, kind="ExternalOutput").ap()

    S = KB(nc)
    S.sem(*["b_" + e for e in ENGS])
    S.sem("c_pe", "c_act", "c_dve", "c_pool")
    S.sem("d_sp", "d_aq", "d_pq")
    S.sem("cc_k", "cc_v", "cc_kn", "cc_x2", "cc_tot", "cc_map", "cc_ob")

    alloc = ExitStack()

    def sb(name, shape, dt):
        return alloc.enter_context(nc.sbuf_tensor(name, shape, dt))

    x_tm = sb("x_tm", [128, TCH, H], F32)
    slotA = sb("slotA", [128, 16 * 1024], BF16)
    slotB = sb("slotB", [128, 16 * 1024], BF16)
    slotC = sb("slotC", [128, 22528], BF16)
    slotF = sb("slotF", [128, 6144], F32)
    idf = sb("idf_s", [128, 128], F32)
    onesb = sb("onesb_s", [128, 128], BF16)
    lstr = sb("lstr_s", [128, 128], BF16)
    onesf = sb("onesf_s", [1, 128], F32)
    qmask = sb("qmask_s", [128, 16, 32], BF16)
    ioe = sb("ioe_s", [128, 32], F32)
    pcol = sb("pcol_s", [128, 1], F32)
    pidx = sb("pidx_s", [128, 1], F32)
    cmask = sb("cmask_s", [8, 1], F32)
    bq_t = sb("bq_s", [128, 16], F32)
    bk_t = sb("bk_s", [128, 16], F32)
    bv_t = sb("bv_s", [128, 16], F32)
    bqs = sb("bqs_s", [128, 16], F32)
    gwt_bf = sb("gwtbf_s", [128, 16, 32], BF16)
    glo_bf = sb("globf_s", [128, 16, 32], BF16)
    gb = sb("gb_s", [32, 1], F32)
    rev32 = sb("rev32_s", [128, 32], F32)
    negc_sb = sb("negc_s", [128, 32], F32)
    qmax = sb("qmax_s", [32, 1], F32)
    kmax = sb("kmax_s", [32, 1], F32)
    knal = sb("knal_s", [32, 4], F32)
    negcT = sb("negcT_s", [1, 32], F32)
    idxf = sb("idxf_s", [128, TCH, 4], F32)
    t2v = sb("t2_s", [128, TCH, 4], F32)
    wkv = sb("wk_s", [128, TCH, 4], F32)
    gidx = sb("gidx_s", [128, TCH, 4], I32)
    csv = sb("cs_s", [128, TCH, 32], F32)
    anv = sb("an_s", [128, TCH, 32], BF16)
    offsb = sb("offs_s", [128, 32], F32)
    totsb = sb("tot_s", [1, 32], F32)
    tota = sb("tota_s", [8, 32], F32)
    offs1 = sb("offs1_s", [1, 32], F32)
    eps_t = sb("eps_s", [128, 1], F32)
    trash_t = sb("trash_s", [128, 1], F32)
    zrow_t = sb("zrow_s", [128, 1], F32)
    mapt = sb("mapt_s", [128, 1], F32)
    mapi = sb("mapi_s", [128, 1], I32)
    mrow = sb("mrow_s", [128, 1], F32)
    mrow_i = sb("mrowi_s", [128, 1], I32)

    sco_t = slotF[:, 0:256].rearrange("p (a b) -> p a b", b=32)
    scb_t = slotF[:, 256:512].rearrange("p (a b) -> p a b", b=32)
    r_gs = slotF[:, 512:516]
    r_m1 = slotF[:, 516:517]
    r_m2 = slotF[:, 517:518]
    r_cnt = slotF[:, 518:519]
    r_w = slotF[:, 519:520]
    r_d = slotF[:, 520:521]
    r_th = slotF[:, 521:522]
    r_eq = slotF[:, 524:532]
    r_t8 = slotF[:, 532:540]
    r_wraw = slotF[:, 540:544]
    r_ws = slotF[:, 544:545]
    r_cat = slotF[:, 545:546]
    r_oat = slotF[:, 546:547]
    r_pos = slotF[:, 547:548]
    r_kp = slotF[:, 548:549]
    r_g0 = slotF[:, 549:550]
    r_adj = slotF[:, 550:551]
    r_tok = slotF[:, 552:560]
    r_work = slotF[:, 576:608]
    r_work2 = slotF[:, 608:640]
    r_r1 = slotF[:, 640:672]
    r_a = slotF[:, 672:704]
    r_oh = slotF[:, 704:832].rearrange("p (a b) -> p a b", b=32)
    gidx_f = slotF[:, 832:864].rearrange("p (a b) -> p a b", b=4)
    sco_f = slotF[0:32, 1024:2048]
    scb_f = slotF[0:32, 2048:3072]
    bn_t = slotF[:, 3104:3128].rearrange("p (a b) -> p a b", b=6)
    gwt_f = slotF[:, 512:1024].rearrange("p (a b) -> p a b", b=32)
    mv = slotF[:, 3128:3130]
    sd1 = slotF[:, 3130:3131]
    rstd = slotF[:, 3131:3132]
    acc = [slotF[:, 1024:3072], slotF[:, 3072:5120]]
    outst = [slotF[:, 5120:5632], slotF[:, 5632:6144]]

    @S.step("sp")
    def _(api):
        for dst, src in [
            (x_tm[:, :, :],
             x_d.rearrange("(a p) f -> p a f", p=128)[:, :, :]),
            (idf[:, :], idf_d[:, :]), (onesb[:, :], onesb_d[:, :]),
            (lstr[:, :], lstr_d[:, :]), (onesf[:, :], onesf_d[:, :]),
            (qmask[:, :, :],
             qmask_d.rearrange("a p e -> p a e")[:, :, :]),
            (ioe[:, :], ioe_d[:, :]), (pcol[:, :], pcol_d[:, :]),
            (pidx[:, :], pidx_d[:, :]), (cmask[:, :], cmask_d[:, :]),
            (bq_t[:, :], bq_d[:, :]), (bk_t[:, :], bk_d[:, :]),
            (bv_t[:, :], bv_d[:, :]), (gb[:, :], gb_d[:, :]),
            (gwt_f[:, :, :], gwt_d[:, :, :]),
        ]:
            api.dma_start(out=dst, in_=src).then_inc(S.h("d_sp"), 16)
    n_setup = S.bump("d_sp", 16 * 15)

    @S.step("pool")
    def _(api):
        api.engine_nop()
    n_setup_pq = S.bump("d_pq", 0)

    @S.step("dve", waits={"d_sp": n_setup})
    def _(api):
        api.tensor_scalar(out=bqs[:, :], in0=bq_t[:, :], scalar1=1.0 / SCALE,
                          scalar2=None, op0=ALU.mult).then_inc(S.h("c_dve"), 1)
        api.tensor_scalar(out=rev32[:, :], in0=ioe[:, :], scalar1=-1.0,
                          scalar2=32.0, op0=ALU.mult, op1=ALU.add).then_inc(S.h("c_dve"), 1)
        api.memset(eps_t[:, :], EPS).then_inc(S.h("c_dve"), 1)
        api.tensor_scalar(out=trash_t[:, :], in0=pidx[:, :],
                          scalar1=float(NSLOT), scalar2=None,
                          op0=ALU.add).then_inc(S.h("c_dve"), 1)
        api.tensor_scalar(out=zrow_t[:, :], in0=pidx[:, :],
                          scalar1=float(NTOK + 1), scalar2=None,
                          op0=ALU.add).then_inc(S.h("c_dve"), 1)
        api.memset(slotC[:, 0:2048], 0.0).then_inc(S.h("c_dve"), 1)
        api.tensor_copy(out=gwt_bf.rearrange("p a e -> p (a e)")[:, :],
                        in_=gwt_f.rearrange("p a e -> p (a e)")[:, :]).then_inc(S.h("c_dve"), 1)
        api.memset(t2v.rearrange("p a k -> p (a k)")[:, :], 0.0).then_inc(S.h("c_dve"), 1)
    m_setup_dve = S.bump("c_dve", 8)
    @S.step("dve", waits={"c_dve": m_setup_dve})
    def _(api):
        api.tensor_tensor(out=glo_bf.rearrange("p a e -> p (a e)")[:, :],
                          in0=gwt_f.rearrange("p a e -> p (a e)")[:, :],
                          in1=gwt_bf.rearrange("p a e -> p (a e)")[:, :],
                          op=ALU.subtract).then_inc(S.h("c_dve"), 1)
    m_setup_dve = S.bump("c_dve")

    @S.step("pool", waits={"c_dve": m_setup_dve})
    def _(api):
        mz = map_in.rearrange("(a b) o -> a (b o)", a=128)
        api.dma_start(out=mz[:, 0:128], in_=slotC[:, 0:128]).then_inc(S.h("d_pq"), 16)
        api.dma_start(out=mz[:, 128:129], in_=slotC[:, 0:1], allow_slow_non_contiguous=True).then_inc(S.h("d_pq"), 16)
        api.dma_start(out=xn2_all[NTOK:NTOK + 128, :],
                      in_=slotC[:, 0:2048]).then_inc(S.h("d_pq"), 16)
        api.dma_start(out=ob_all[NSLOT:NSLOT + 128, :],
                      in_=slotC[:, 0:2048]).then_inc(S.h("d_pq"), 16)
    n_zero_pq = S.bump("d_pq", 64)

    def ph_layernorm(dram_out, dest_fm, first_wait, lo_dram=None, lo_fm=None):
        xnc = [slotC[:, 18432:20480], slotC[:, 20480:22528]]
        xloc = [slotC[:, 0:2048], slotC[:, 2048:4096]]
        xf32 = [slotF[:, 1024:3072], slotF[:, 3072:5120]]
        sp_st = {}
        v_sqrt = 0
        for tc in range(TCH):
            par = tc % 2
            w = dict(first_wait) if tc == 0 else {}
            w["c_act"] = max(w.get("c_act", 0), v_sqrt)
            w["c_dve"] = max(w.get("c_dve", 0), m_setup_dve)
            @S.step("dve", waits=w)
            def _(api, tc=tc):
                for j in range(4):
                    api.bn_stats(out=bn_t[:, j, :],
                                 in_=x_tm[:, tc, j * 512:(j + 1) * 512]).then_inc(S.h("c_dve"), 1)
            v_bn = S.bump("c_dve", 4)
            @S.step("dve", waits={"c_dve": v_bn})
            def _(api):
                api.bn_aggr(out=mv[:, :],
                            in_=slotF[:, 3104:3128]).then_inc(S.h("c_dve"), 1)
            v_ag = S.bump("c_dve")
            @S.step("act", waits={"c_dve": v_ag})
            def _(api):
                api.activation(out=sd1[:, :], in_=mv[:, 1:2], func=AF.Sqrt,
                               bias=eps_t[:, 0:1], scale=1.0).then_inc(S.h("c_act"), 1)
            v_sqrt = S.bump("c_act")
            @S.step("dve", waits={"c_act": v_sqrt})
            def _(api):
                api.reciprocal(out=rstd[:, :], in_=sd1[:, :]).then_inc(S.h("c_dve"), 1)
            v_rs = S.bump("c_dve")
            if lo_dram is None:
                @S.step("dve", waits=mx({"c_dve": v_rs},
                                        {"d_sp": sp_st.get(tc - 2, 0)}))
                def _(api, tc=tc, par=par):
                    api.tensor_scalar(out=xnc[par][:, :], in0=x_tm[:, tc, :],
                                      scalar1=mv[:, 0:1], scalar2=rstd[:, 0:1],
                                      op0=ALU.subtract, op1=ALU.mult).then_inc(S.h("c_dve"), 1)
                v_nm = S.bump("c_dve")
            else:
                @S.step("dve", waits=mx({"c_dve": v_rs},
                                        {"d_sp": sp_st.get(tc - 2, 0)}))
                def _(api, tc=tc, par=par):
                    api.tensor_scalar(out=xf32[par][:, :], in0=x_tm[:, tc, :],
                                      scalar1=mv[:, 0:1], scalar2=rstd[:, 0:1],
                                      op0=ALU.subtract, op1=ALU.mult).then_inc(S.h("c_dve"), 1)
                v_f = S.bump("c_dve")
                @S.step("dve", waits={"c_dve": v_f})
                def _(api, par=par):
                    api.tensor_copy(out=xnc[par][:, :],
                                    in_=xf32[par][:, :]).then_inc(S.h("c_dve"), 1)
                v_hi = S.bump("c_dve")
                @S.step("dve", waits={"c_dve": v_hi})
                def _(api, par=par):
                    api.tensor_tensor(out=xloc[par][:, :], in0=xf32[par][:, :],
                                      in1=xnc[par][:, :],
                                      op=ALU.subtract).then_inc(S.h("c_dve"), 1)
                v_nm = S.bump("c_dve")
            @S.step("sp", waits={"c_dve": v_nm})
            def _(api, tc=tc, par=par):
                api.dma_start(out=dram_out[tc * 128:(tc + 1) * 128, :],
                              in_=xnc[par][:, :]).then_inc(S.h("d_sp"), 16)
                if lo_dram is not None:
                    api.dma_start(out=lo_dram[tc * 128:(tc + 1) * 128, :],
                                  in_=xloc[par][:, :]).then_inc(S.h("d_sp"), 16)
            v_st = S.bump("d_sp", 16 if lo_dram is None else 32)
            sp_st[tc] = v_st
            @S.step("act", waits={"d_sp": v_st})
            def _(api, tc=tc):
                api.dma_start(out=dest_fm[:, :, tc * 128:(tc + 1) * 128],
                              in_=dram_out[tc * 128:(tc + 1) * 128, :],
                              transpose=True).then_inc(S.h("d_aq"), 16)
                if lo_fm is not None:
                    api.dma_start(out=lo_fm[:, :, tc * 128:(tc + 1) * 128],
                                  in_=lo_dram[tc * 128:(tc + 1) * 128, :],
                                  transpose=True).then_inc(S.h("d_aq"), 16)
            S.bump("d_aq", 16 if lo_fm is None else 32)
        return S.n("d_aq"), S.n("d_sp")

    xn_fm = slotA.rearrange("p (a t) -> p a t", t=1024)
    aq_xn1, sp_xn1 = ph_layernorm(xn1_dr, xn_fm, {"d_sp": n_setup})
    if debug:
        @S.step("sp", waits={"d_sp": sp_xn1})
        def _(api):
            api.dma_start(out=dbg["d_xn1"][:, :], in_=xn1_dr[:, :]).then_inc(S.h("d_sp"), 16)
        S.bump("d_sp", 16)

    S.barrier()

    q_fm = slotB.rearrange("p (a t) -> p a t", t=1024)
    wbuf = [slotC[:, 0:2048].rearrange("p (a j) -> p a j", j=128),
            slotC[:, 2048:4096].rearrange("p (a j) -> p a j", j=128)]
    kst = [slotC[:, 4096:5120], slotC[:, 5120:6144]]
    sqt = [slotC[:, 6144:7168], slotC[:, 7168:8192]]

    with nc.psum_tensor("pg", [128, 2, 1024], F32) as pgt, \
         nc.psum_tensor("qn", [32, 1024], F32) as qn_ps, \
         nc.psum_tensor("kn", [32, 1024], F32) as kn_ps:

        v_wb_free = {0: 0, 1: 0}
        v_ps_free = {0: 0, 1: 0}
        v_sq_free = {0: 0, 1: 0}
        v_kst_free = {0: 0, 1: 0}
        mats = [("q", wq_d, None, qn_ps), ("k", wk_d, kvK_in, kn_ps),
                ("v", wv_d, kvV_in, None)]
        for mi, (mn, wd, st_dram, nrm_ps) in enumerate(mats):
            for co in range(16):
                it = mi * 16 + co
                par = it % 2
                @S.step("pool", waits={"c_pe": v_wb_free[par],
                                       "d_pq": n_zero_pq})
                def _(api, wd=wd, co=co, par=par):
                    api.dma_start(out=wbuf[par].rearrange("p a j -> p (a j)")[:, :],
                                  in_=wd[co].rearrange("p a j -> p (a j)")[:, :]
                                  ).then_inc(S.h("d_pq"), 16)
                v_w = S.bump("d_pq", 16)
                @S.step("pe", waits={"d_pq": v_w, "d_aq": aq_xn1,
                                     "c_act": v_ps_free[par]})
                def _(api, par=par):
                    for ci in range(16):
                        for hf in range(2):
                            api.matmul(pgt[:, par, hf * 512:(hf + 1) * 512],
                                       wbuf[par][:, ci, :],
                                       xn_fm[:, ci, hf * 512:(hf + 1) * 512],
                                       start=(ci == 0), stop=(ci == 15)
                                       ).then_inc(S.h("c_pe"), 1)
                v_mm = S.bump("c_pe", 32)
                v_wb_free[par] = v_mm
                if mn == "q":
                    @S.step("act", waits={"c_pe": v_mm})
                    def _(api, co=co, par=par):
                        api.activation(out=q_fm[:, co, :], in_=pgt[:, par, :],
                                       func=AF.Identity, bias=bqs[:, co:co + 1],
                                       scale=1.0 / SCALE).then_inc(S.h("c_act"), 1)
                    v_ev = S.bump("c_act")
                    v_ps_free[par] = v_ev
                    @S.step("act", waits=mx({"c_act": v_ev},
                                            {"c_pe": v_sq_free[par]}))
                    def _(api, co=co, par=par):
                        api.activation(out=sqt[par][:, :], in_=q_fm[:, co, :],
                                       func=AF.Square).then_inc(S.h("c_act"), 1)
                    v_sq = S.bump("c_act")
                    @S.step("pe", waits={"c_act": v_sq})
                    def _(api, co=co, par=par, nrm_ps=nrm_ps):
                        sq = sqt[par].rearrange("p (a b) -> p a b", b=512)
                        for hf in range(2):
                            api.matmul(nrm_ps[:, hf * 512:(hf + 1) * 512],
                                       qmask[:, co, :], sq[:, hf, :],
                                       start=(co == 0), stop=(co == 15)
                                       ).then_inc(S.h("c_pe"), 1)
                    v_sq_free[par] = S.bump("c_pe", 2)
                else:
                    kpar = co % 2
                    @S.step("act", waits=mx({"c_pe": v_mm},
                                            {"d_sp": v_kst_free[kpar]}))
                    def _(api, co=co, par=par, kpar=kpar,
                          bias=(bk_t if mn == "k" else bv_t)):
                        api.activation(out=kst[kpar][:, :], in_=pgt[:, par, :],
                                       func=AF.Identity, bias=bias[:, co:co + 1],
                                       scale=1.0).then_inc(S.h("c_act"), 1)
                    v_ev = S.bump("c_act")
                    v_ps_free[par] = v_ev
                    @S.step("sp", waits={"c_act": v_ev})
                    def _(api, co=co, kpar=kpar, st_dram=st_dram):
                        api.dma_start(out=st_dram[co * 128:(co + 1) * 128, :],
                                      in_=kst[kpar][:, :]).then_inc(S.h("d_sp"), 16)
                    v_kst_free[kpar] = S.bump("d_sp", 16)
                    if mn == "k":
                        @S.step("act", waits=mx({"c_act": v_ev},
                                                {"c_pe": v_sq_free[par]}))
                        def _(api, kpar=kpar, par=par):
                            api.activation(out=sqt[par][:, :], in_=kst[kpar][:, :],
                                           func=AF.Square).then_inc(S.h("c_act"), 1)
                        v_sq = S.bump("c_act")
                        @S.step("pe", waits={"c_act": v_sq})
                        def _(api, co=co, par=par, nrm_ps=nrm_ps):
                            sq = sqt[par].rearrange("p (a b) -> p a b", b=512)
                            for hf in range(2):
                                api.matmul(nrm_ps[:, hf * 512:(hf + 1) * 512],
                                           qmask[:, co, :], sq[:, hf, :],
                                           start=(co == 0), stop=(co == 15)
                                           ).then_inc(S.h("c_pe"), 1)
                        v_sq_free[par] = S.bump("c_pe", 2)
        sp_kv = S.n("d_sp")
        pe_p2 = S.n("c_pe")

        @S.step("pool", waits={"d_sp": sp_kv})
        def _(api):
            api.collective_compute(
                "AllGather", ALU.bypass,
                replica_groups=[[0, 1, 2, 3], [4, 5, 6, 7]],
                ins=[kvK_in[:, :]], outs=[K_all[:, :]]).then_inc(S.h("cc_k"), 1)
            api.collective_compute(
                "AllGather", ALU.bypass,
                replica_groups=[[0, 1, 2, 3], [4, 5, 6, 7]],
                ins=[kvV_in[:, :]], outs=[V_all[:, :]]).then_inc(S.h("cc_v"), 1)
        S.bump("cc_k")
        S.bump("cc_v")

        @S.step("dve", waits={"c_pe": pe_p2})
        def _(api):
            api.tensor_reduce(out=qmax[:, :], in_=qn_ps[:, :], axis=AX,
                              op=ALU.max).then_inc(S.h("c_dve"), 1)
            api.tensor_reduce(out=kmax[:, :], in_=kn_ps[:, :], axis=AX,
                              op=ALU.max).then_inc(S.h("c_dve"), 1)
        v_nr = S.bump("c_dve", 2)
        @S.step("act", waits={"c_dve": v_nr})
        def _(api):
            api.activation(out=qmax[:, :], in_=qmax[:, :],
                           func=AF.Sqrt).then_inc(S.h("c_act"), 1)
            api.activation(out=kmax[:, :], in_=kmax[:, :],
                           func=AF.Sqrt).then_inc(S.h("c_act"), 1)
        v_sqn = S.bump("c_act", 2)
        @S.step("sp", waits={"c_act": v_sqn})
        def _(api):
            api.dma_start(out=kn_in[:, :], in_=kmax[:, :]).then_inc(S.h("d_sp"), 16)
        v_knst = S.bump("d_sp", 16)
        @S.step("pool", waits={"d_sp": v_knst})
        def _(api):
            api.collective_compute(
                "AllGather", ALU.bypass,
                replica_groups=[[0, 1, 2, 3], [4, 5, 6, 7]],
                ins=[kn_in[:, :]], outs=[kn_all[:, :]]).then_inc(S.h("cc_kn"), 1)
        S.bump("cc_kn")
        @S.step("pool", waits={"cc_kn": 1})
        def _(api):
            api.dma_start(out=knal[:, :],
                          in_=kn_all.rearrange("(r h) o -> h (r o)", h=32)[:, :],
                          allow_slow_non_contiguous=True).then_inc(S.h("d_pq"), 16)
        v_knl = S.bump("d_pq", 16)
        @S.step("dve", waits={"d_pq": v_knl, "c_act": v_sqn})
        def _(api):
            api.tensor_reduce(out=kmax[:, :], in_=knal[:, :], axis=AX,
                              op=ALU.max).then_inc(S.h("c_dve"), 1)
        v_km = S.bump("c_dve")
        @S.step("dve", waits={"c_dve": v_km})
        def _(api):
            api.tensor_tensor(out=qmax[:, :], in0=qmax[:, :], in1=kmax[:, :],
                              op=ALU.mult).then_inc(S.h("c_dve"), 1)
        v_ch = S.bump("c_dve")
        @S.step("dve", waits={"c_dve": v_ch})
        def _(api):
            api.tensor_scalar(out=qmax[:, :], in0=qmax[:, :], scalar1=1.0,
                              scalar2=-1.0, op0=ALU.add,
                              op1=ALU.mult).then_inc(S.h("c_dve"), 1)
        v_negc = S.bump("c_dve")

    with nc.psum_tensor("ngT", [1, 32], F32) as ngT, \
         nc.psum_tensor("ngB", [128, 32], F32) as ngB:
        @S.step("pe", waits={"c_dve": v_negc})
        def _(api):
            api.transpose(ngT[:, :], qmax[:, :], idf[0:32, 0:32]).then_inc(S.h("c_pe"), 1)
        v_t = S.bump("c_pe")
        @S.step("dve", waits={"c_pe": v_t})
        def _(api):
            api.tensor_copy(out=negcT[:, :], in_=ngT[:, :]).then_inc(S.h("c_dve"), 1)
        v_tc2 = S.bump("c_dve")
        @S.step("pe", waits={"c_dve": v_tc2})
        def _(api):
            for hh in range(NH):
                api.matmul(ngB[:, hh:hh + 1], onesf[:, :], negcT[:, hh:hh + 1],
                           start=True, stop=True).then_inc(S.h("c_pe"), 1)
        v_bc = S.bump("c_pe", NH)
        @S.step("dve", waits={"c_pe": v_bc})
        def _(api):
            api.tensor_copy(out=negc_sb[:, :], in_=ngB[:, :]).then_inc(S.h("c_dve"), 1)
        v_negcd = S.bump("c_dve")

    if debug:
        @S.step("sp", waits={"c_dve": v_negcd, "cc_k": 1, "cc_v": 1})
        def _(api):
            api.dma_start(out=dbg["d_negc"][:, :], in_=negc_sb[:, :]).then_inc(S.h("d_sp"), 16)
            api.dma_start(out=dbg["d_q"][:, :],
                          in_=q_fm.rearrange("p a t -> p (a t)")[:, :]).then_inc(S.h("d_sp"), 16)
            api.dma_start(out=dbg["d_kall"][:, :], in_=K_all[:, :]).then_inc(S.h("d_sp"), 16)
            api.dma_start(out=dbg["d_vall"][:, :], in_=V_all[:, :]).then_inc(S.h("d_sp"), 16)
        S.bump("d_sp", 64)

    S.barrier()

    feats_fm = slotA.rearrange("p (a t) -> p a t", t=1024)
    kp_t = [slotC[:, 0:4096], slotC[:, 4096:8192]]
    vt_t = [slotC[:, 8192 + i * 2560: 8192 + (i + 1) * 2560]
            .rearrange("p (a b) -> p a b", b=VPAD) for i in range(4)]
    pt_t = [slotC[:, 18432:19456], slotC[:, 19456:20480]]
    fsb = slotC[:, 20480:21504]
    recb = slotC[:, 21504:22528]

    with nc.psum_tensor("sS", [128, 2, 1024], F32) as sS, \
         nc.psum_tensor("fts", [65, 1024], F32) as fts, \
         nc.psum_tensor("rbc", [64, 1024], F32) as rbc:

        @S.step("dve")
        def _(api):
            for i in range(4):
                api.memset(vt_t[i][:, :, 64:65], 1.0).then_inc(S.h("c_dve"), 1)
        v_ones = S.bump("c_dve", 4)

        kp_ready = [0, 0]
        vt_ready = [0, 0, 0, 0]
        kp_free = [0, 0]
        vt_free = [0, 0, 0, 0]
        s_free = {0: 0, 1: 0}
        pt_free = {0: 0, 1: 0}
        fin = {"act": 0, "dve": 0, "mult": 0, "rb": 0}

        def load_pair(pr):
            slot = pr % 2
            @S.step("sp", waits={"cc_k": 1, "c_pe": kp_free[slot]})
            def _(api, pr=pr, slot=slot):
                kv = kp_t[slot].rearrange("p (r t) -> p r t", r=4)
                for r in range(4):
                    api.dma_start(
                        out=kv[:, r, :],
                        in_=K_all[r * H + pr * 128: r * H + (pr + 1) * 128, :]
                    ).then_inc(S.h("d_sp"), 16)
            kp_ready[slot] = S.bump("d_sp", 64)
            for hl in range(2):
                hh = 2 * pr + hl
                vslot = hh % 4
                @S.step("act", waits={"cc_v": 1, "c_dve": v_ones,
                                      "c_pe": vt_free[vslot]})
                def _(api, hh=hh, vslot=vslot):
                    for r in range(4):
                        api.dma_start(
                            out=vt_t[vslot][:, r * 8:(r + 1) * 8, 0:64],
                            in_=V_all[r * H + hh * 64: r * H + (hh + 1) * 64, :],
                            transpose=True).then_inc(S.h("d_aq"), 16)
                vt_ready[vslot] = S.bump("d_aq", 64)

        load_pair(0)
        load_pair(1)

        for hh in range(NH):
            pr = hh // 2
            hbase = (hh % 2) * 64
            hchunk = hh // 2
            kslot = pr % 2
            vslot = hh % 4
            kpr = kp_ready[kslot]
            vtr = vt_ready[vslot]

            def qk(kc):
                par = kc % 2
                @S.step("pe", waits=mx({"d_sp": kpr, "c_act": s_free[par]},
                                       {"c_dve": v_negcd}))
                def _(api, kc=kc, par=par, hbase=hbase, hchunk=hchunk,
                      kslot=kslot):
                    kv = kp_t[kslot].rearrange("p (r t) -> p r t", r=4)
                    r, lt = kc // 8, kc % 8
                    lhsT = kv[hbase:hbase + 64, r, lt * 128:(lt + 1) * 128]
                    for hf in range(2):
                        api.matmul(sS[:, par, hf * 512:(hf + 1) * 512], lhsT,
                                   q_fm[hbase:hbase + 64, hchunk,
                                        hf * 512:(hf + 1) * 512],
                                   start=True, stop=True).then_inc(S.h("c_pe"), 1)
                return S.bump("c_pe", 2)

            def ex(kc, v_qk):
                par = kc % 2
                @S.step("act", waits={"c_pe": max(v_qk, pt_free[par])})
                def _(api, kc=kc, par=par, hh=hh):
                    api.activation(out=pt_t[par][:, :], in_=sS[:, par, :],
                                   func=AF.Exp, bias=negc_sb[:, hh:hh + 1],
                                   scale=1.0).then_inc(S.h("c_act"), 1)
                return S.bump("c_act")

            def av(kc, v_ex):
                par = kc % 2
                w = {"c_act": v_ex, "d_aq": vtr}
                if kc == 0:
                    w = mx(w, {"c_act": fin["act"], "c_dve": fin["dve"]})
                @S.step("pe", waits=w)
                def _(api, kc=kc, par=par, vslot=vslot):
                    lhsT = vt_t[vslot][:, kc, 0:65]
                    for hf in range(2):
                        api.matmul(fts[:, hf * 512:(hf + 1) * 512], lhsT,
                                   pt_t[par][:, hf * 512:(hf + 1) * 512],
                                   start=(kc == 0), stop=(kc == NKC - 1)
                                   ).then_inc(S.h("c_pe"), 1)
                return S.bump("c_pe", 2)

            vq = {0: qk(0), 1: qk(1)}
            for kc in range(NKC):
                vex = ex(kc, vq[kc])
                s_free[kc % 2] = vex
                vav = av(kc, vex)
                pt_free[kc % 2] = vav
                if kc + 2 < NKC:
                    vq[kc + 2] = qk(kc + 2)
            v_avl = pt_free[(NKC - 1) % 2]
            kp_free[kslot] = max(kp_free[kslot], v_avl)
            vt_free[vslot] = v_avl

            if hh % 2 == 1 and pr + 2 <= 15:
                load_pair(pr + 2)

            @S.step("act", waits=mx({"c_pe": v_avl}, {"c_dve": fin["mult"]}))
            def _(api):
                api.activation(out=fsb[0:64, :], in_=fts[0:64, :],
                               func=AF.Copy).then_inc(S.h("c_act"), 1)
            fin["act"] = S.bump("c_act")
            @S.step("dve", waits=mx({"c_pe": max(v_avl, fin["rb"])}))
            def _(api):
                with nc.allow_low_precision(reason="softmax denom recip to bf16 operand"):
                    api.reciprocal(out=recb[64:65, :],
                                   in_=fts[64:65, :]).then_inc(S.h("c_dve"), 1)
            fin["dve"] = S.bump("c_dve")
            @S.step("pe", waits={"c_dve": max(fin["dve"], fin["mult"])})
            def _(api):
                for hf in range(2):
                    api.matmul(rbc[:, hf * 512:(hf + 1) * 512],
                               onesb[64:65, 0:64],
                               recb[64:65, hf * 512:(hf + 1) * 512],
                               start=True, stop=True).then_inc(S.h("c_pe"), 1)
            fin["rb"] = S.bump("c_pe", 2)
            @S.step("dve", waits={"c_pe": fin["rb"], "c_act": fin["act"]})
            def _(api, hh=hh, hbase=hbase, hchunk=hchunk):
                api.tensor_tensor(out=feats_fm[hbase:hbase + 64, hchunk, :],
                                  in0=fsb[0:64, :], in1=rbc[0:64, :],
                                  op=ALU.mult).then_inc(S.h("c_dve"), 1)
            fin["mult"] = S.bump("c_dve")

        v_attn = fin["mult"]

    if debug:
        @S.step("sp", waits={"c_dve": v_attn})
        def _(api):
            api.dma_start(out=dbg["d_feats"][:, :],
                          in_=feats_fm.rearrange("p a t -> p (a t)")[:, :]).then_inc(S.h("d_sp"), 16)
        S.bump("d_sp", 16)

    S.barrier()

    wob = [slotC[:, 0:512], slotC[:, 512:1024]]
    bo_bf = slotC[0:1, 1024:3072]
    @S.step("pool")
    def _(api):
        api.dma_start(out=bo_bf, in_=bo_d[:, :]).then_inc(S.h("d_pq"), 16)
    n_bo_pq = S.bump("d_pq", 16)
    with nc.psum_tensor("po", [128, TCH, 512], F32) as po:
        v_wo_free = {0: 0, 1: 0}
        v_ev = {}
        for fo4 in range(4):
            for ci in range(16):
                it = fo4 * 16 + ci
                par = it % 2
                @S.step("pool", waits={"c_pe": v_wo_free[par]})
                def _(api, fo4=fo4, ci=ci, par=par):
                    api.dma_start(out=wob[par][:, :],
                                  in_=wo_d[ci, :, fo4 * 512:(fo4 + 1) * 512]
                                  ).then_inc(S.h("d_pq"), 16)
                v_l = S.bump("d_pq", 16)
                w = {"d_pq": v_l, "c_dve": v_attn}
                if ci == 0:
                    w = mx(w, {"c_dve": max([v_ev.get((fo4 - 1, tb), 0)
                                             for tb in range(TCH)] + [0])})
                @S.step("pe", waits=w)
                def _(api, fo4=fo4, ci=ci, par=par):
                    for tb in range(TCH):
                        api.matmul(po[:, tb, :],
                                   feats_fm[:, ci, tb * 128:(tb + 1) * 128],
                                   wob[par][:, :], start=(ci == 0), stop=False
                                   ).then_inc(S.h("c_pe"), 1)
                v_wo_free[par] = S.bump("c_pe", TCH)
            @S.step("pe", waits={"d_pq": n_bo_pq})
            def _(api, fo4=fo4):
                for tb in range(TCH):
                    api.matmul(po[:, tb, :], onesb[0:1, :],
                               bo_bf[:, fo4 * 512:(fo4 + 1) * 512],
                               start=False, stop=True).then_inc(S.h("c_pe"), 1)
            v_bo = S.bump("c_pe", TCH)
            for tb in range(TCH):
                @S.step("dve", waits={"c_pe": v_bo})
                def _(api, fo4=fo4, tb=tb):
                    api.tensor_tensor(out=x_tm[:, tb, fo4 * 512:(fo4 + 1) * 512],
                                      in0=po[:, tb, :],
                                      in1=x_tm[:, tb, fo4 * 512:(fo4 + 1) * 512],
                                      op=ALU.add).then_inc(S.h("c_dve"), 1)
                v_ev[(fo4, tb)] = S.bump("c_dve")
    v_x2 = S.n("c_dve")

    if debug:
        @S.step("sp", waits={"c_dve": v_x2})
        def _(api):
            api.dma_start(out=dbg["d_x2"][:, :],
                          in_=x_tm.rearrange("p a f -> p (a f)")[:, :]).then_inc(S.h("d_sp"), 16)
        S.bump("d_sp", 16)

    S.barrier()

    xn2_fm = slotA.rearrange("p (a t) -> p a t", t=1024)
    xlo_fm = slotB.rearrange("p (a t) -> p a t", t=1024)
    aq_xn2, sp_xn2 = ph_layernorm(xn2_in, xn2_fm, {"c_dve": v_x2},
                                  lo_dram=xlo_dr, lo_fm=xlo_fm)
    @S.step("pool", waits={"d_sp": sp_xn2})
    def _(api):
        api.collective_compute(
            "AllGather", ALU.bypass, replica_groups=[list(range(8))],
            ins=[xn2_in[:, :]], outs=[xn2_all[0:NTOK, :]]).then_inc(S.h("cc_x2"), 1)
    S.bump("cc_x2")
    if debug:
        @S.step("sp", waits={"d_sp": sp_xn2})
        def _(api):
            api.dma_start(out=dbg["d_xn2"][:, :], in_=xn2_in[:, :]).then_inc(S.h("d_sp"), 16)
        S.bump("d_sp", 16)

    with nc.psum_tensor("lg", [32, 1024], F32) as lg, \
         nc.psum_tensor("trp", [128, 256], F32) as trp:
        @S.step("pe", waits={"d_aq": aq_xn2, "c_dve": m_setup_dve})
        def _(api):
            terms = [(gwt_bf, xn2_fm), (glo_bf, xn2_fm), (gwt_bf, xlo_fm)]
            for ti, (wv, xv) in enumerate(terms):
                for ci in range(16):
                    for hf in range(2):
                        api.matmul(lg[:, hf * 512:(hf + 1) * 512],
                                   wv[:, ci, :],
                                   xv[:, ci, hf * 512:(hf + 1) * 512],
                                   start=(ti == 0 and ci == 0),
                                   stop=(ti == 2 and ci == 15)).then_inc(S.h("c_pe"), 1)
        v_lg = S.bump("c_pe", 96)
        @S.step("act", waits={"c_pe": v_lg})
        def _(api):
            api.activation(out=sco_f[:, :], in_=lg[:, :],
                           func=AF.Sigmoid).then_inc(S.h("c_act"), 1)
        v_sg = S.bump("c_act")
        @S.step("dve", waits={"c_act": v_sg})
        def _(api):
            api.tensor_scalar(out=scb_f[:, :], in0=sco_f[:, :],
                              scalar1=gb[:, 0:1], scalar2=None,
                              op0=ALU.add).then_inc(S.h("c_dve"), 1)
        v_scb = S.bump("c_dve")
        @S.step("pe", waits={"c_dve": v_scb})
        def _(api):
            for tc in range(TCH):
                api.transpose(trp[:, tc * 32:(tc + 1) * 32],
                              sco_f[:, tc * 128:(tc + 1) * 128],
                              idf[0:32, 0:32]).then_inc(S.h("c_pe"), 1)
        v_t1 = S.bump("c_pe", TCH)
        @S.step("dve", waits={"c_pe": v_t1})
        def _(api):
            api.tensor_copy(out=sco_t.rearrange("p a b -> p (a b)")[:, :],
                            in_=trp[:, :]).then_inc(S.h("c_dve"), 1)
        v_c1 = S.bump("c_dve")
        @S.step("pe", waits={"c_dve": v_c1})
        def _(api):
            for tc in range(TCH):
                api.transpose(trp[:, tc * 32:(tc + 1) * 32],
                              scb_f[:, tc * 128:(tc + 1) * 128],
                              idf[0:32, 0:32]).then_inc(S.h("c_pe"), 1)
        v_t2 = S.bump("c_pe", TCH)
        @S.step("dve", waits={"c_pe": v_t2})
        def _(api):
            api.tensor_copy(out=scb_t.rearrange("p a b -> p (a b)")[:, :],
                            in_=trp[:, :]).then_inc(S.h("c_dve"), 1)
        v_sc_tm = S.bump("c_dve")

    hs_fm = slotB.rearrange("p (a t) -> p a t", t=1024)
    wgb = [slotC[:, 1024:3072].rearrange("p (a j) -> p a j", j=128),
           slotC[:, 3072:5120].rearrange("p (a j) -> p a j", j=128)]
    wub = [slotC[:, 5120:7168].rearrange("p (a j) -> p a j", j=128),
           slotC[:, 7168:9216].rearrange("p (a j) -> p a j", j=128)]
    gsb = [slotC[:, 9216:10240], slotC[:, 10240:11264]]

    vC = {}
    v_a_bf = {}

    def routing_tc(tc, cps):
        sc = scb_t[:, tc, :]
        sv = sco_t[:, tc, :]
        c = {"v": max(S.n("c_dve"), v_sc_tm)}

        def D(fn, extra=None):
            w = {"c_dve": c["v"]}
            if extra:
                w = mx(w, extra)
            @S.step("dve", waits=w)
            def _(api):
                fn(api).then_inc(S.h("c_dve"), 1)
            c["v"] = S.bump("c_dve")

        D(lambda api, sc=sc: api.tensor_copy(out=r_work[:, :], in_=sc))
        for g in range(4):
            D(lambda api, g=g: api.tensor_reduce(
                out=r_m1[:, :], in_=r_work[:, g * 8:(g + 1) * 8], axis=AX,
                op=ALU.max))
            D(lambda api, g=g: api.tensor_scalar(
                out=r_eq[:, :], in0=r_work[:, g * 8:(g + 1) * 8],
                scalar1=r_m1[:, 0:1], scalar2=None, op0=ALU.is_ge))
            D(lambda api: api.tensor_reduce(
                out=r_cnt[:, :], in_=r_eq[:, :], axis=AX, op=ALU.add))
            D(lambda api: api.tensor_scalar(
                out=r_t8[:, :], in0=r_eq[:, :], scalar1=BIG, scalar2=None,
                op0=ALU.mult))
            D(lambda api, g=g: api.tensor_tensor(
                out=r_t8[:, :], in0=r_work[:, g * 8:(g + 1) * 8],
                in1=r_t8[:, :], op=ALU.subtract))
            D(lambda api: api.tensor_reduce(
                out=r_m2[:, :], in_=r_t8[:, :], axis=AX, op=ALU.max))
            D(lambda api: api.tensor_scalar(
                out=r_w[:, :], in0=r_cnt[:, :], scalar1=1.5, scalar2=None,
                op0=ALU.is_ge))
            D(lambda api: api.tensor_tensor(
                out=r_d[:, :], in0=r_m1[:, :], in1=r_m2[:, :], op=ALU.subtract))
            D(lambda api: api.tensor_tensor(
                out=r_d[:, :], in0=r_w[:, :], in1=r_d[:, :], op=ALU.mult))
            D(lambda api: api.tensor_tensor(
                out=r_d[:, :], in0=r_m2[:, :], in1=r_d[:, :], op=ALU.add))
            D(lambda api, g=g: api.tensor_tensor(
                out=r_gs[:, g:g + 1], in0=r_m1[:, :], in1=r_d[:, :], op=ALU.add))
        D(lambda api: api.tensor_reduce(
            out=r_m1[:, :], in_=r_gs[:, :], axis=AX, op=ALU.max))
        D(lambda api: api.tensor_scalar(
            out=r_eq[:, 0:4], in0=r_gs[:, :], scalar1=r_m1[:, 0:1],
            scalar2=None, op0=ALU.is_ge))
        D(lambda api: api.tensor_reduce(
            out=r_cnt[:, :], in_=r_eq[:, 0:4], axis=AX, op=ALU.add))
        D(lambda api: api.tensor_scalar(
            out=r_t8[:, 0:4], in0=r_eq[:, 0:4], scalar1=BIG, scalar2=None,
            op0=ALU.mult))
        D(lambda api: api.tensor_tensor(
            out=r_t8[:, 0:4], in0=r_gs[:, :], in1=r_t8[:, 0:4],
            op=ALU.subtract))
        D(lambda api: api.tensor_reduce(
            out=r_m2[:, :], in_=r_t8[:, 0:4], axis=AX, op=ALU.max))
        D(lambda api: api.tensor_scalar(
            out=r_w[:, :], in0=r_cnt[:, :], scalar1=1.5, scalar2=None,
            op0=ALU.is_ge))
        D(lambda api: api.tensor_tensor(
            out=r_d[:, :], in0=r_m1[:, :], in1=r_m2[:, :], op=ALU.subtract))
        D(lambda api: api.tensor_tensor(
            out=r_d[:, :], in0=r_w[:, :], in1=r_d[:, :], op=ALU.mult))
        D(lambda api: api.tensor_tensor(
            out=r_th[:, :], in0=r_m2[:, :], in1=r_d[:, :], op=ALU.add))
        for g in range(4):
            D(lambda api, g=g: api.tensor_scalar(
                out=r_eq[:, g:g + 1], in0=r_gs[:, g:g + 1],
                scalar1=r_th[:, 0:1], scalar2=None, op0=ALU.is_ge))
            D(lambda api, g=g, sc=sc: api.tensor_scalar(
                out=r_work[:, g * 8:(g + 1) * 8], in0=sc[:, g * 8:(g + 1) * 8],
                scalar1=r_eq[:, g:g + 1], scalar2=None, op0=ALU.mult))
        for k in range(KTOP):
            src = r_work if k % 2 == 0 else r_work2
            dst = r_work2 if k % 2 == 0 else r_work
            D(lambda api, src=src: api.tensor_reduce(
                out=r_m1[:, :], in_=src[:, :], axis=AX, op=ALU.max))
            D(lambda api, src=src: api.tensor_scalar(
                out=r_r1[:, :], in0=src[:, :], scalar1=r_m1[:, 0:1],
                scalar2=None, op0=ALU.is_ge))
            D(lambda api: api.tensor_tensor(
                out=r_r1[:, :], in0=r_r1[:, :], in1=rev32[:, :], op=ALU.mult))
            D(lambda api: api.tensor_reduce(
                out=r_m2[:, :], in_=r_r1[:, :], axis=AX, op=ALU.max))
            D(lambda api, tc=tc, k=k: api.tensor_scalar(
                out=idxf[:, tc, k:k + 1], in0=r_m2[:, :], scalar1=-1.0,
                scalar2=32.0, op0=ALU.mult, op1=ALU.add))
            D(lambda api, tc=tc, k=k: api.tensor_scalar(
                out=r_oh[:, k, :], in0=ioe[:, :],
                scalar1=idxf[:, tc, k:k + 1], scalar2=None, op0=ALU.is_equal))
            D(lambda api, k=k, sv=sv: api.tensor_tensor(
                out=r_r1[:, :], in0=sv, in1=r_oh[:, k, :], op=ALU.mult))
            D(lambda api, k=k: api.tensor_reduce(
                out=r_wraw[:, k:k + 1], in_=r_r1[:, :], axis=AX, op=ALU.add))
            D(lambda api, k=k, src=src, dst=dst: api.scalar_tensor_tensor(
                out=dst[:, :], in0=r_oh[:, k, :], scalar=-BIG, in1=src[:, :],
                op0=ALU.mult, op1=ALU.add))
        for k in range(1, KTOP):
            for kp2 in range(k):
                D(lambda api, tc=tc, k=k, kp2=kp2: api.tensor_tensor(
                    out=r_d[:, :], in0=idxf[:, tc, kp2:kp2 + 1],
                    in1=idxf[:, tc, k:k + 1], op=ALU.is_equal))
                if kp2 == 0:
                    D(lambda api, tc=tc, k=k: api.tensor_copy(
                        out=t2v[:, tc, k:k + 1], in_=r_d[:, :]))
                else:
                    D(lambda api, tc=tc, k=k: api.tensor_tensor(
                        out=t2v[:, tc, k:k + 1], in0=t2v[:, tc, k:k + 1],
                        in1=r_d[:, :], op=ALU.add))
        D(lambda api: api.tensor_reduce(
            out=r_ws[:, :], in_=r_wraw[:, :], axis=AX, op=ALU.add))
        D(lambda api: api.tensor_scalar(
            out=r_ws[:, :], in0=r_ws[:, :], scalar1=1e-20, scalar2=None,
            op0=ALU.add))
        D(lambda api: api.reciprocal(out=r_ws[:, :], in_=r_ws[:, :]))
        D(lambda api, tc=tc: api.tensor_scalar(
            out=wkv[:, tc, :], in0=r_wraw[:, :], scalar1=r_ws[:, 0:1],
            scalar2=2.0, op0=ALU.mult, op1=ALU.mult))
        D(lambda api: api.tensor_tensor(
            out=r_a[:, :], in0=r_oh[:, 0, :], in1=r_oh[:, 1, :], op=ALU.add))
        D(lambda api: api.tensor_tensor(
            out=r_a[:, :], in0=r_a[:, :], in1=r_oh[:, 2, :], op=ALU.add))
        D(lambda api: api.tensor_tensor(
            out=r_a[:, :], in0=r_a[:, :], in1=r_oh[:, 3, :], op=ALU.add))
        D(lambda api, tc=tc: api.tensor_copy(out=anv[:, tc, :], in_=r_a[:, :]))
        v_a_bf[tc] = c["v"]
        @S.step("pe", waits={"c_dve": v_a_bf[tc]})
        def _(api, tc=tc):
            if tc == 0:
                api.matmul(cps[:, :], lstr[:, :], anv[:, 0, :],
                           start=True, stop=True).then_inc(S.h("c_pe"), 1)
            else:
                for j in range(tc):
                    api.matmul(cps[:, :], onesb[:, :], anv[:, j, :],
                               start=(j == 0), stop=False)
                api.matmul(cps[:, :], lstr[:, :], anv[:, tc, :],
                           start=False, stop=True).then_inc(S.h("c_pe"), 1)
        v_cp = S.bump("c_pe")
        @S.step("dve", waits={"c_pe": v_cp})
        def _(api, tc=tc):
            api.tensor_copy(out=csv[:, tc, :], in_=cps[:, :]).then_inc(S.h("c_dve"), 1)
        vC[tc] = S.bump("c_dve")

    with nc.psum_tensor("pg2", [128, 2, 1024], F32) as pg2, \
         nc.psum_tensor("pu2", [128, 1024], F32) as pu2, \
         nc.psum_tensor("cps", [128, 32], F32) as cps, \
         nc.psum_tensor("totp", [1, 32], F32) as totp:
        v_gb_free = {0: 0, 1: 0}
        v_ub_free = {0: 0, 1: 0}
        v_pg_free = {0: 0, 1: 0}
        v_pu_free = 0
        for co in range(16):
            par = co % 2
            @S.step("pool", waits={"c_pe": v_gb_free[par], "cc_x2": 1})
            def _(api, co=co, par=par):
                api.dma_start(out=wgb[par].rearrange("p a j -> p (a j)")[:, :],
                              in_=wsg_d[co].rearrange("p a j -> p (a j)")[:, :]
                              ).then_inc(S.h("d_pq"), 16)
            v_lg2 = S.bump("d_pq", 16)
            @S.step("pool", waits={"c_pe": v_ub_free[par], "cc_x2": 1})
            def _(api, co=co, par=par):
                api.dma_start(out=wub[par].rearrange("p a j -> p (a j)")[:, :],
                              in_=wsu_d[co].rearrange("p a j -> p (a j)")[:, :]
                              ).then_inc(S.h("d_pq"), 16)
            v_lu2 = S.bump("d_pq", 16)
            @S.step("pe", waits={"d_pq": v_lg2, "d_aq": aq_xn2,
                                 "c_act": v_pg_free[par]})
            def _(api, par=par):
                for ci in range(16):
                    for hf in range(2):
                        api.matmul(pg2[:, par, hf * 512:(hf + 1) * 512],
                                   wgb[par][:, ci, :],
                                   xn2_fm[:, ci, hf * 512:(hf + 1) * 512],
                                   start=(ci == 0), stop=(ci == 15)).then_inc(S.h("c_pe"), 1)
            v_g2 = S.bump("c_pe", 32)
            v_gb_free[par] = v_g2
            @S.step("pe", waits={"d_pq": v_lu2, "c_dve": v_pu_free})
            def _(api, par=par):
                for ci in range(16):
                    for hf in range(2):
                        api.matmul(pu2[:, hf * 512:(hf + 1) * 512],
                                   wub[par][:, ci, :],
                                   xn2_fm[:, ci, hf * 512:(hf + 1) * 512],
                                   start=(ci == 0), stop=(ci == 15)).then_inc(S.h("c_pe"), 1)
            v_u2 = S.bump("c_pe", 32)
            v_ub_free[par] = v_u2
            @S.step("act", waits={"c_pe": v_g2})
            def _(api, par=par):
                api.activation(out=gsb[par][:, :], in_=pg2[:, par, :],
                               func=AF.Silu).then_inc(S.h("c_act"), 1)
            v_si = S.bump("c_act")
            v_pg_free[par] = v_si
            @S.step("dve", waits={"c_act": v_si, "c_pe": max(v_u2, v_lg)})
            def _(api, co=co, par=par):
                api.tensor_tensor(out=hs_fm[:, co, :], in0=gsb[par][:, :],
                                  in1=pu2[:, :], op=ALU.mult).then_inc(S.h("c_dve"), 1)
            v_pu_free = S.bump("c_dve")
            if co % 2 == 1 and co // 2 < TCH:
                routing_tc(co // 2, cps)
        for tc in range(TCH):
            if tc not in vC:
                routing_tc(tc, cps)
        v_hs_done = v_pu_free

        @S.step("pe", waits={"c_dve": v_a_bf[TCH - 1]})
        def _(api):
            for tc in range(TCH):
                api.matmul(totp[:, :], onesb[:, 0:1], anv[:, tc, :],
                           start=(tc == 0), stop=(tc == TCH - 1)).then_inc(S.h("c_pe"), 1)
        v_tp = S.bump("c_pe", TCH)
        @S.step("dve", waits={"c_pe": v_tp})
        def _(api):
            api.tensor_copy(out=totsb[:, :], in_=totp[:, :]).then_inc(S.h("c_dve"), 1)
        v_ts = S.bump("c_dve")
        @S.step("sp", waits={"c_dve": v_ts})
        def _(api):
            api.dma_start(out=tot_in[:, :], in_=totsb[:, :]).then_inc(S.h("d_sp"), 16)
        v_tst = S.bump("d_sp", 16)
        @S.step("pool", waits={"d_sp": v_tst})
        def _(api):
            api.collective_compute(
                "AllGather", ALU.bypass, replica_groups=[list(range(8))],
                ins=[tot_in[:, :]], outs=[tot_all[:, :]]).then_inc(S.h("cc_tot"), 1)
        S.bump("cc_tot")
        @S.step("pool", waits={"cc_tot": 1})
        def _(api):
            api.dma_start(out=tota[:, :], in_=tot_all[:, :]).then_inc(S.h("d_pq"), 16)
        v_tl = S.bump("d_pq", 16)

    v_hs_pe = S.n("c_pe")

    with nc.psum_tensor("ofp", [1, 32], F32) as ofp, \
         nc.psum_tensor("ofb", [128, 32], F32) as ofb:
        @S.step("pe", waits={"d_pq": v_tl})
        def _(api):
            api.matmul(ofp[:, :], cmask[:, :], tota[:, :],
                       start=True, stop=True).then_inc(S.h("c_pe"), 1)
        v_of = S.bump("c_pe")
        @S.step("dve", waits={"c_pe": v_of})
        def _(api):
            api.tensor_copy(out=offs1[:, :], in_=ofp[:, :]).then_inc(S.h("c_dve"), 1)
        v_o1 = S.bump("c_dve")
        @S.step("pe", waits={"c_dve": v_o1})
        def _(api):
            api.matmul(ofb[:, :], onesf[:, :], offs1[:, :],
                       start=True, stop=True).then_inc(S.h("c_pe"), 1)
        v_ob2 = S.bump("c_pe")
        @S.step("dve", waits={"c_pe": v_ob2})
        def _(api):
            api.tensor_copy(out=offsb[:, :], in_=ofb[:, :]).then_inc(S.h("c_dve"), 1)
        v_offs = S.bump("c_dve")

    c2 = {"v": S.n("c_dve")}

    def D2(fn, extra=None):
        w = {"c_dve": c2["v"]}
        if extra:
            w = mx(w, extra)
        @S.step("dve", waits=w)
        def _(api):
            fn(api).then_inc(S.h("c_dve"), 1)
        c2["v"] = S.bump("c_dve")

    v_gidx = {}
    v_scat = 0
    for tc in range(TCH):
        for k in range(KTOP):
            D2(lambda api, tc=tc, k=k: api.tensor_scalar(
                out=r_a[:, :], in0=ioe[:, :], scalar1=idxf[:, tc, k:k + 1],
                scalar2=None, op0=ALU.is_equal),
               {"c_dve": max(v_offs, vC[tc])})
            D2(lambda api, tc=tc: api.tensor_tensor(
                out=r_r1[:, :], in0=csv[:, tc, :], in1=r_a[:, :], op=ALU.mult))
            D2(lambda api: api.tensor_reduce(
                out=r_cat[:, :], in_=r_r1[:, :], axis=AX, op=ALU.add))
            D2(lambda api: api.tensor_tensor(
                out=r_r1[:, :], in0=offsb[:, :], in1=r_a[:, :], op=ALU.mult))
            D2(lambda api: api.tensor_reduce(
                out=r_oat[:, :], in_=r_r1[:, :], axis=AX, op=ALU.add))
            D2(lambda api, tc=tc, k=k: api.tensor_tensor(
                out=r_pos[:, :], in0=r_cat[:, :], in1=t2v[:, tc, k:k + 1],
                op=ALU.add))
            D2(lambda api: api.tensor_tensor(
                out=r_pos[:, :], in0=r_pos[:, :], in1=r_oat[:, :], op=ALU.add))
            D2(lambda api: api.tensor_scalar(
                out=r_kp[:, :], in0=r_pos[:, :], scalar1=511.5, scalar2=None,
                op0=ALU.is_le))
            D2(lambda api, tc=tc, k=k: api.tensor_scalar(
                out=r_g0[:, :], in0=idxf[:, tc, k:k + 1], scalar1=512.0,
                scalar2=None, op0=ALU.mult))
            D2(lambda api: api.tensor_tensor(
                out=r_g0[:, :], in0=r_g0[:, :], in1=r_pos[:, :], op=ALU.add))
            D2(lambda api: api.tensor_scalar(
                out=r_adj[:, :], in0=r_kp[:, :], scalar1=-1.0, scalar2=1.0,
                op0=ALU.mult, op1=ALU.add))
            D2(lambda api: api.tensor_tensor(
                out=r_adj[:, :], in0=r_adj[:, :], in1=trash_t[:, :],
                op=ALU.mult))
            D2(lambda api: api.tensor_tensor(
                out=r_g0[:, :], in0=r_g0[:, :], in1=r_kp[:, :], op=ALU.mult))
            D2(lambda api, tc=tc, k=k: api.tensor_tensor(
                out=gidx_f[:, tc, k:k + 1], in0=r_g0[:, :],
                in1=r_adj[:, :], op=ALU.add))
            D2(lambda api, tc=tc, k=k: api.tensor_copy(
                out=gidx[:, tc, k:k + 1], in_=gidx_f[:, tc, k:k + 1]))
            v_gidx[(tc, k)] = c2["v"]
    for tc in range(TCH):
        D2(lambda api, tc=tc: api.tensor_scalar(
            out=r_tok[:, tc:tc + 1], in0=pcol[:, :],
            scalar1=float(tc * 128 + 1), scalar2=None, op0=ALU.add))
        v_tok = c2["v"]
        @S.step("pool", waits={"c_dve": max(v_tok, v_gidx[(tc, KTOP - 1)])})
        def _(api, tc=tc):
            for k in range(KTOP):
                api.indirect_dma_start(
                    out=map_in[:, :],
                    out_offset=IOA(ap=gidx[:, tc, k:k + 1], axis=0),
                    in_=r_tok[:, tc:tc + 1], in_offset=None,
                ).then_inc(S.h("d_pq"), 16)
        v_scat = S.bump("d_pq", 16 * KTOP)

    @S.step("pool", waits={"d_pq": v_scat})
    def _(api):
        api.collective_compute(
            "AllReduce", ALU.add, replica_groups=[list(range(8))],
            ins=[map_in[:, :]], outs=[map_all[:, :]]).then_inc(S.h("cc_map"), 1)
    S.bump("cc_map")

    if debug:
        @S.step("sp", waits={"c_dve": c2["v"]})
        def _(api):
            api.dma_start(out=dbg["d_sc"][:, :],
                          in_=scb_t.rearrange("p a b -> p (a b)")[:, :]).then_inc(S.h("d_sp"), 16)
            api.dma_start(out=dbg["d_idx"][:, :],
                          in_=idxf.rearrange("p a b -> p (a b)")[:, :]).then_inc(S.h("d_sp"), 16)
            api.dma_start(out=dbg["d_wk"][:, :],
                          in_=wkv.rearrange("p a b -> p (a b)")[:, :]).then_inc(S.h("d_sp"), 16)
            api.dma_start(out=dbg["d_gidx"][:, :],
                          in_=gidx.rearrange("p a b -> p (a b)")[:, :]).then_inc(S.h("d_sp"), 16)
            api.dma_start(out=dbg["d_cs"][:, :],
                          in_=csv.rearrange("p a b -> p (a b)")[:, :]).then_inc(S.h("d_sp"), 16)
            api.dma_start(out=dbg["d_offs"][:, :], in_=offsb[:, :]).then_inc(S.h("d_sp"), 16)
        S.bump("d_sp", 96)
        @S.step("pool", waits={"cc_map": 1})
        def _(api):
            api.dma_start(out=dbg["d_map"][:, :], in_=map_all[:, :]).then_inc(S.h("d_pq"), 16)
        S.bump("d_pq", 16)

    S.barrier()

    w2sb = slotA.rearrange("p (a f) -> p a f", f=2048)
    buf_fm = slotC[:, 0:8192].rearrange("p (a b) -> p a b", b=512)
    obsb = [slotC[:, 0:2048], slotC[:, 2048:4096]]
    buf_tm = slotC[:, 8192:10240]
    hh_fm = slotC[:, 10240:14336].rearrange("p (a b) -> p a b", b=512)
    w1b = [slotC[:, 14336:16384].rearrange("p (a j) -> p a j", j=128),
           slotC[:, 16384:18432].rearrange("p (a j) -> p a j", j=128)]
    w3b = [slotC[:, 18432:20480].rearrange("p (a j) -> p a j", j=128),
           slotC[:, 20480:22528].rearrange("p (a j) -> p a j", j=128)]

    v_w2_free = v_hs_pe
    v_ob_store = 0
    v_gath_d = 0
    v_pq_prev = 0
    for le in range(EPC):
        @S.step("pool", waits={"c_pe": v_w2_free})
        def _(api, le=le):
            for ci in range(8):
                api.dma_start(out=w2sb[:, ci, :],
                              in_=w2_d[le, ci, :, :]).then_inc(S.h("d_pq"), 16)
        v_w2 = S.bump("d_pq", 16 * 8)
        v_gfin = 0
        for j in range(4):
            c3 = {"v": S.n("c_dve")}

            def D3(fn, extra=None):
                w = {"c_dve": c3["v"]}
                if extra:
                    w = mx(w, extra)
                @S.step("dve", waits=w)
                def _(api):
                    fn(api).then_inc(S.h("c_dve"), 1)
                c3["v"] = S.bump("c_dve")

            D3(lambda api: api.tensor_scalar(
                out=mrow[:, :], in0=pcol[:, :], scalar1=2.0, scalar2=None,
                op0=ALU.mult), {"d_pq": v_pq_prev})
            D3(lambda api: api.tensor_tensor(
                out=mrow[:, :], in0=mrow[:, :], in1=pidx[:, :],
                op=ALU.subtract))
            D3(lambda api, le=le, j=j: api.tensor_scalar(
                out=mrow[:, :], in0=mrow[:, :],
                scalar1=float(le * CAP + j * 128), scalar2=None, op0=ALU.add))
            D3(lambda api: api.tensor_copy(out=mrow_i[:, :], in_=mrow[:, :]))
            v_mr = c3["v"]
            @S.step("pool", waits={"c_dve": v_mr, "cc_map": 1})
            def _(api):
                api.indirect_dma_start(
                    out=mapt[:, :], out_offset=None,
                    in_=map_all[:, :], in_offset=IOA(ap=mrow_i[:, :], axis=0),
                ).then_inc(S.h("d_pq"), 16)
            v_mg = S.bump("d_pq", 16)
            D3(lambda api: api.tensor_scalar(
                out=r_kp[:, :], in0=mapt[:, :], scalar1=0.5, scalar2=None,
                op0=ALU.is_le), {"d_pq": v_mg})
            D3(lambda api: api.tensor_tensor(
                out=r_kp[:, :], in0=r_kp[:, :], in1=zrow_t[:, :],
                op=ALU.mult))
            D3(lambda api: api.tensor_scalar(
                out=r_g0[:, :], in0=mapt[:, :], scalar1=1.0, scalar2=None,
                op0=ALU.subtract))
            D3(lambda api: api.tensor_tensor(
                out=r_g0[:, :], in0=r_g0[:, :], in1=r_kp[:, :], op=ALU.add))
            D3(lambda api: api.tensor_copy(out=mapi[:, :], in_=r_g0[:, :]))
            v_mi = c3["v"]
            @S.step("pool", waits={"c_dve": v_mi, "cc_x2": 1,
                                   "d_sp": v_gath_d})
            def _(api):
                api.indirect_dma_start(
                    out=buf_tm[:, :], out_offset=None,
                    in_=xn2_all[:, :], in_offset=IOA(ap=mapi[:, :], axis=0),
                ).then_inc(S.h("d_pq"), 16)
            v_bg = S.bump("d_pq", 16)
            @S.step("sp", waits={"d_pq": v_bg})
            def _(api, le=le, j=j):
                api.dma_start(
                    out=bufdr[le * CAP + j * 128: le * CAP + (j + 1) * 128, :],
                    in_=buf_tm[:, :]).then_inc(S.h("d_sp"), 16)
            v_gfin = S.bump("d_sp", 16)
            v_gath_d = v_gfin
            v_pq_prev = v_bg
        @S.step("act", waits={"d_sp": max(v_gfin, v_ob_store)})
        def _(api, le=le):
            api.dma_start(out=buf_fm[:, :, :],
                          in_=bufdr[le * CAP:(le + 1) * CAP, :],
                          transpose=True).then_inc(S.h("d_aq"), 16)
        v_bfm = S.bump("d_aq", 16)
        if debug and le == EPC - 1:
            @S.step("sp", waits={"d_sp": v_gfin})
            def _(api):
                api.dma_start(out=dbg["d_buf"][:, :], in_=bufdr[:, :]).then_inc(S.h("d_sp"), 16)
            S.bump("d_sp", 16)

        with nc.psum_tensor(f"h1p{le}", [128, 2, 512], F32) as h1p, \
             nc.psum_tensor(f"h3p{le}", [128, 2, 512], F32) as h3p:
            v_w13_free = {0: 0, 1: 0}
            v_hh_free = {0: 0, 1: 0}
            for hc in range(8):
                par = hc % 2
                @S.step("pool", waits={"c_pe": v_w13_free[par]})
                def _(api, le=le, hc=hc, par=par):
                    api.dma_start(out=w1b[par].rearrange("p a j -> p (a j)")[:, :],
                                  in_=w1_d[le, hc].rearrange("p a j -> p (a j)")[:, :]
                                  ).then_inc(S.h("d_pq"), 16)
                    api.dma_start(out=w3b[par].rearrange("p a j -> p (a j)")[:, :],
                                  in_=w3_d[le, hc].rearrange("p a j -> p (a j)")[:, :]
                                  ).then_inc(S.h("d_pq"), 16)
                v_wld = S.bump("d_pq", 32)
                @S.step("pe", waits={"d_pq": v_wld, "d_aq": v_bfm,
                                     "c_dve": v_hh_free[par]})
                def _(api, par=par):
                    for ci in range(16):
                        api.matmul(h1p[:, par, :], w1b[par][:, ci, :],
                                   buf_fm[:, ci, :], start=(ci == 0),
                                   stop=(ci == 15)).then_inc(S.h("c_pe"), 1)
                    for ci in range(16):
                        api.matmul(h3p[:, par, :], w3b[par][:, ci, :],
                                   buf_fm[:, ci, :], start=(ci == 0),
                                   stop=(ci == 15)).then_inc(S.h("c_pe"), 1)
                v_hm = S.bump("c_pe", 32)
                v_w13_free[par] = v_hm
                @S.step("act", waits={"c_pe": v_hm})
                def _(api, hc=hc, par=par):
                    api.activation(out=hh_fm[:, hc, :], in_=h1p[:, par, :],
                                   func=AF.Silu).then_inc(S.h("c_act"), 1)
                v_si2 = S.bump("c_act")
                @S.step("dve", waits={"c_act": v_si2})
                def _(api, hc=hc, par=par):
                    api.tensor_tensor(out=hh_fm[:, hc, :], in0=hh_fm[:, hc, :],
                                      in1=h3p[:, par, :], op=ALU.mult
                                      ).then_inc(S.h("c_dve"), 1)
                v_hh_free[par] = S.bump("c_dve")
            v_hh_all = max(v_hh_free[0], v_hh_free[1])

        with nc.psum_tensor(f"obp{le}", [128, 2, H], F32) as obp:
            v_obe = {0: 0, 1: 0}
            v_obs = {0: 0, 1: 0}
            for j in range(4):
                par = j % 2
                @S.step("pe", waits={"c_dve": v_hh_all, "d_pq": v_w2,
                                     "c_act": v_obe[par]})
                def _(api, j=j, par=par):
                    for hc in range(8):
                        for q in range(4):
                            api.matmul(obp[:, par, q * 512:(q + 1) * 512],
                                       hh_fm[:, hc, j * 128:(j + 1) * 128],
                                       w2sb[:, hc, q * 512:(q + 1) * 512],
                                       start=(hc == 0), stop=(hc == 7)
                                       ).then_inc(S.h("c_pe"), 1)
                v_od = S.bump("c_pe", 32)
                v_w2_free = v_od
                @S.step("act", waits={"c_pe": v_od, "d_sp": v_obs[par]})
                def _(api, par=par):
                    api.activation(out=obsb[par][:, :], in_=obp[:, par, :],
                                   func=AF.Copy).then_inc(S.h("c_act"), 1)
                v_oe = S.bump("c_act")
                v_obe[par] = v_oe
                @S.step("sp", waits={"c_act": v_oe})
                def _(api, le=le, j=j, par=par):
                    api.dma_start(
                        out=ob_in[le * CAP + j * 128: le * CAP + (j + 1) * 128, :],
                        in_=obsb[par][:, :]).then_inc(S.h("d_sp"), 16)
                v_obs[par] = S.bump("d_sp", 16)
            v_ob_store = S.n("d_sp")

    @S.step("pool", waits={"d_sp": v_ob_store})
    def _(api):
        api.collective_compute(
            "AllGather", ALU.bypass, replica_groups=[list(range(8))],
            ins=[ob_in[:, :]], outs=[ob_all[0:NSLOT, :]]).then_inc(S.h("cc_ob"), 1)
    S.bump("cc_ob")
    if debug:
        @S.step("sp", waits={"d_sp": v_ob_store})
        def _(api):
            api.dma_start(out=dbg["d_ob"][:, :], in_=ob_in[:, :]).then_inc(S.h("d_sp"), 16)
        S.bump("d_sp", 16)

    S.barrier()

    ysb = [slotC[:, 0:2048], slotC[:, 2048:4096]]
    wsb2 = [slotC[:, 4096:4608], slotC[:, 4608:5120]]
    v_y_free = {0: 0, 1: 0}
    v_xc = 0
    for tc in range(TCH):
        vdv = S.n("c_dve")
        for k in range(KTOP):
            par = k % 2
            @S.step("pool", waits={"cc_ob": 1, "c_dve": v_y_free[par]})
            def _(api, tc=tc, k=k, par=par):
                api.indirect_dma_start(
                    out=ysb[par][:, :], out_offset=None,
                    in_=ob_all[:, :],
                    in_offset=IOA(ap=gidx[:, tc, k:k + 1], axis=0),
                ).then_inc(S.h("d_pq"), 16)
            v_yg = S.bump("d_pq", 16)
            if k == 0:
                @S.step("dve", waits={"d_pq": v_yg, "c_dve": vdv})
                def _(api, tc=tc):
                    api.tensor_scalar(out=acc[0][:, :], in0=ysb[0][:, :],
                                      scalar1=wkv[:, tc, 0:1], scalar2=None,
                                      op0=ALU.mult).then_inc(S.h("c_dve"), 1)
            else:
                @S.step("dve", waits={"d_pq": v_yg, "c_dve": S.n("c_dve")})
                def _(api, tc=tc, k=k, par=par):
                    api.scalar_tensor_tensor(
                        out=acc[par][:, :], in0=ysb[par][:, :],
                        scalar=wkv[:, tc, k:k + 1], in1=acc[1 - par][:, :],
                        op0=ALU.mult, op1=ALU.add).then_inc(S.h("c_dve"), 1)
            v_y_free[par] = S.bump("c_dve")
        @S.step("dve", waits={"c_dve": S.n("c_dve")})
        def _(api, tc=tc):
            api.tensor_tensor(out=x_tm[:, tc, :], in0=x_tm[:, tc, :],
                              in1=acc[(KTOP - 1) % 2][:, :], op=ALU.add
                              ).then_inc(S.h("c_dve"), 1)
        v_xc = S.bump("c_dve")
        v_y_free[(KTOP - 1) % 2] = v_xc
    v_moe = v_xc
    if debug:
        @S.step("sp", waits={"c_dve": v_moe})
        def _(api):
            api.dma_start(out=dbg["d_moe"][:, :],
                          in_=x_tm.rearrange("p a f -> p (a f)")[:, :]).then_inc(S.h("d_sp"), 16)
        S.bump("d_sp", 16)

    with nc.psum_tensor("sdp", [128, TCH, 512], F32) as sdp:
        v_sd_free = {0: 0, 1: 0}
        v_oev = {}
        v_ost = {0: 0, 1: 0}
        for fo4 in range(4):
            for ci in range(16):
                it = fo4 * 16 + ci
                par = it % 2
                @S.step("pool", waits={"c_pe": v_sd_free[par], "cc_ob": 1})
                def _(api, fo4=fo4, ci=ci, par=par):
                    api.dma_start(out=wsb2[par][:, :],
                                  in_=wsd_d[ci, :, fo4 * 512:(fo4 + 1) * 512]
                                  ).then_inc(S.h("d_pq"), 16)
                v_l = S.bump("d_pq", 16)
                w = {"d_pq": v_l, "c_dve": v_moe}
                if ci == 0:
                    w = mx(w, {"c_dve": max([v_oev.get((fo4 - 1, tb), 0)
                                             for tb in range(TCH)] + [0])})
                @S.step("pe", waits=w)
                def _(api, fo4=fo4, ci=ci, par=par):
                    for tb in range(TCH):
                        api.matmul(sdp[:, tb, :],
                                   hs_fm[:, ci, tb * 128:(tb + 1) * 128],
                                   wsb2[par][:, :], start=(ci == 0),
                                   stop=(ci == 15)).then_inc(S.h("c_pe"), 1)
                v_sd_free[par] = S.bump("c_pe", TCH)
            v_last = v_sd_free[(fo4 * 16 + 15) % 2]
            for tb in range(TCH):
                @S.step("dve", waits={"c_pe": v_last, "d_sp": v_ost[tb % 2]})
                def _(api, fo4=fo4, tb=tb):
                    api.tensor_tensor(out=outst[tb % 2][:, :],
                                      in0=sdp[:, tb, :],
                                      in1=x_tm[:, tb, fo4 * 512:(fo4 + 1) * 512],
                                      op=ALU.add).then_inc(S.h("c_dve"), 1)
                v_oe = S.bump("c_dve")
                v_oev[(fo4, tb)] = v_oe
                @S.step("sp", waits={"c_dve": v_oe})
                def _(api, fo4=fo4, tb=tb):
                    api.dma_start(out=out_d[tb * 128:(tb + 1) * 128,
                                            fo4 * 512:(fo4 + 1) * 512],
                                  in_=outst[tb % 2][:, :]).then_inc(S.h("d_sp"), 16)
                v_ost[tb % 2] = S.bump("d_sp", 16)

    v_final = S.n("d_sp")
    @S.step("sp", waits={"d_sp": v_final})
    def _(api):
        pass

    with nc.Block() as block:
        @block.tensor
        def _(api):
            for f in S.prog["pe"]:
                f(api)

        @block.scalar
        def _(api):
            for f in S.prog["act"]:
                f(api)

        @block.vector
        def _(api):
            for f in S.prog["dve"]:
                f(api)

        @block.gpsimd
        def _(api):
            for f in S.prog["pool"]:
                f(api)

        @block.sync
        def _(api):
            for f in S.prog["sp"]:
                f(api)

    alloc.close()
    S.sem_stack.close()
    return nc


def _host_inputs(inputs):
    f32 = np.float32
    bf = ml_dtypes.bfloat16
    x = np.asarray(inputs["L0_attn_x"], f32).reshape(NTOK, H)

    def lhsT16(w):
        w = np.asarray(w, f32)
        return np.ascontiguousarray(
            w.reshape(16, 128, 16, 128).transpose(2, 1, 0, 3))

    wq = lhsT16(inputs["L0_attn_Wq"])
    wk = lhsT16(inputs["L0_attn_Wk"])
    wv = lhsT16(inputs["L0_attn_Wv"])
    wsg = lhsT16(inputs["L0_ffn_shared_gate"])
    wsu = lhsT16(inputs["L0_ffn_shared_up"])
    wo = np.ascontiguousarray(
        np.asarray(inputs["L0_attn_Wo"], f32).reshape(16, 128, H))
    wsd = np.ascontiguousarray(
        np.asarray(inputs["L0_ffn_shared_down"], f32).reshape(16, 128, H))

    def bias_t(b):
        return np.ascontiguousarray(np.asarray(b, f32).reshape(16, 128).T)

    bq_t = bias_t(inputs["L0_attn_bq"])
    bk_t = bias_t(inputs["L0_attn_bk"])
    bv_t = bias_t(inputs["L0_attn_bv"])
    bo_r = np.ascontiguousarray(np.asarray(inputs["L0_attn_bo"], f32).reshape(1, H))
    gwt = np.ascontiguousarray(
        np.asarray(inputs["L0_ffn_gate_w"], f32).T.reshape(16, 128, 32)
        .transpose(1, 0, 2))
    gb = np.ascontiguousarray(np.asarray(inputs["L0_ffn_gate_bias"], f32).reshape(32, 1))

    w1_all = np.asarray(inputs["L0_ffn_w1"], f32)
    w3_all = np.asarray(inputs["L0_ffn_w3"], f32)
    w2_all = np.asarray(inputs["L0_ffn_w2"], f32)

    idf = np.eye(128, dtype=f32)
    onesb = np.ones((128, 128), dtype=bf)
    p_i = np.arange(128)
    lstr = (p_i[:, None] < p_i[None, :]).astype(bf)
    onesf = np.ones((1, 128), dtype=f32)
    qmask = np.zeros((16, 128, 32), dtype=f32)
    for ci in range(16):
        qmask[ci, 0:64, 2 * ci] = 1
        qmask[ci, 64:128, 2 * ci + 1] = 1
    qmask = qmask.astype(bf)
    ioe = np.ascontiguousarray(
        np.broadcast_to(np.arange(32, dtype=f32), (128, 32)))
    pidx = np.arange(128, dtype=f32).reshape(128, 1)

    common = dict(wq=wq, wk=wk, wv=wv, wo=wo, bq_t=bq_t, bk_t=bk_t,
                  bv_t=bv_t, bo_r=bo_r, gwt=gwt, gb=gb, wsg=wsg, wsu=wsu,
                  wsd=wsd, idf=idf, onesb=onesb, lstr=lstr, onesf=onesf,
                  qmask=qmask, ioe=ioe, pidx=pidx)

    in_maps = []
    for c in range(NCORES):
        m = dict(common)
        m["x"] = np.ascontiguousarray(x[c * NT:(c + 1) * NT])
        es = slice(c * EPC, (c + 1) * EPC)
        m["w1"] = np.ascontiguousarray(
            w1_all[es].reshape(EPC, 16, 128, 8, 128).transpose(0, 3, 2, 1, 4))
        m["w3"] = np.ascontiguousarray(
            w3_all[es].reshape(EPC, 16, 128, 8, 128).transpose(0, 3, 2, 1, 4))
        m["w2"] = np.ascontiguousarray(w2_all[es].reshape(EPC, 8, 128, H))
        m["pcol"] = (np.arange(128, dtype=f32) + c * NT).reshape(128, 1)
        m["cmask"] = (np.arange(8) < c).astype(f32).reshape(8, 1)
        in_maps.append(m)
    return in_maps


_LAST_RESULTS = None


def kernel(**inputs):
    global _LAST_RESULTS
    in_maps = _host_inputs(inputs)
    nc = build_program(debug=DEBUG)
    res = run_bass_kernel_spmd(nc, in_maps, core_ids=list(range(NCORES)))
    _LAST_RESULTS = res
    out = np.concatenate([np.asarray(res.results[c]["out"], np.float32)
                          for c in range(NCORES)], 0)
    return out.reshape(2, 4096, H)


# version 17
# speedup vs baseline: 1.1134x; 1.0851x over previous
import math
from contextlib import ExitStack

import numpy as np
import ml_dtypes

import concourse.bass as bass
import concourse.mybir as mybir
from concourse.bass_utils import run_bass_kernel_spmd

F32 = mybir.dt.float32
BF16 = mybir.dt.bfloat16
I32 = mybir.dt.int32
AF = mybir.ActivationFunctionType
ALU = mybir.AluOpType
AX = mybir.AxisListType.X
IOA = bass.IndirectOffsetOnAxis

NCORES = 8
H = 2048
NT = 1024
TCH = 8
NH = 32
DH = 64
SCALE = math.sqrt(32.0)
NKC = 32
EPC = 4
CAP = 512
KTOP = 4
NTOK = 8192
NSLOT = 16384
BIG = 2.0e6
EPS = 1e-5
VPAD = 80

DEBUG = False
ENGS = ("pe", "act", "dve", "pool", "sp")


class KB:
    def __init__(self, nc):
        self.nc = nc
        self.prog = {e: [] for e in ENGS}
        self.sems = {}
        self.cnt = {}
        self.lastw = {e: {} for e in ENGS}
        self.sem_stack = ExitStack()

    def sem(self, *names):
        for name in names:
            self.sems[name] = self.sem_stack.enter_context(self.nc.semaphore(name))
            self.cnt[name] = 0

    def n(self, name):
        return self.cnt[name]

    def bump(self, name, k=1):
        self.cnt[name] += k
        return self.cnt[name]

    def h(self, name):
        return self.sems[name]

    def step(self, eng, waits=None):
        wl = []
        for s, v in (waits or {}).items():
            if v > self.lastw[eng].get(s, 0):
                self.lastw[eng][s] = v
                wl.append((self.sems[s], v))

        def deco(fn):
            def thunk(api, wl=wl, fn=fn):
                for hh, v in wl:
                    api.wait_ge(hh, v)
                fn(api)
            self.prog[eng].append(thunk)
        return deco

    def barrier(self):
        vals = {e: self.bump("b_" + e) for e in ENGS}
        for e in ENGS:
            def f(api, e=e, vals=vals):
                api.sem_inc(self.sems["b_" + e], 1)
                for o in ENGS:
                    if o != e:
                        api.wait_ge(self.sems["b_" + o], vals[o])
            self.prog[e].append(f)
            for o in ENGS:
                self.lastw[e]["b_" + o] = vals[o]


def mx(*ds):
    out = {}
    for d in ds:
        for k, v in d.items():
            out[k] = max(out.get(k, 0), v)
    return out


def build_program(debug=False):
    nc = bass.Bass("TRN2", target_bir_lowering=False)

    def din(name, shape, dt=F32):
        return nc.dram_tensor(name, shape, dt, kind="ExternalInput").ap()

    x_d = din("x", [NT, H])
    wq_d = din("wq", [16, 128, 16, 128])
    wk_d = din("wk", [16, 128, 16, 128])
    wv_d = din("wv", [16, 128, 16, 128])
    wo_d = din("wo", [16, 128, H])
    bq_d = din("bq_t", [128, 16])
    bk_d = din("bk_t", [128, 16])
    bv_d = din("bv_t", [128, 16])
    bo_d = din("bo_r", [1, H])
    gwt_d = din("gwt", [128, 16, 32])
    gb_d = din("gb", [32, 1])
    w1_d = din("w1", [EPC, 8, 128, 16, 128])
    w3_d = din("w3", [EPC, 8, 128, 16, 128])
    w2_d = din("w2", [EPC, 8, 128, H])
    wsg_d = din("wsg", [16, 128, 16, 128])
    wsu_d = din("wsu", [16, 128, 16, 128])
    wsd_d = din("wsd", [16, 128, H])
    idf_d = din("idf", [128, 128])
    onesb_d = din("onesb", [128, 128], BF16)
    lstr_d = din("lstr", [128, 128], BF16)
    onesf_d = din("onesf", [1, 128])
    qmask_d = din("qmask", [16, 128, 32], BF16)
    ioe_d = din("ioe", [128, 32])
    pcol_d = din("pcol", [128, 1])
    pidx_d = din("pidx", [128, 1])
    cmask_d = din("cmask", [8, 1])

    out_d = nc.dram_tensor("out", [NT, H], F32, kind="ExternalOutput").ap()

    def dint(name, shape, dt):
        return nc.dram_tensor(name, shape, dt).ap()

    kvK_in = dint("kvK_in", [H, NT], BF16)
    kvV_in = dint("kvV_in", [H, NT], BF16)
    K_all = dint("K_all", [4 * H, NT], BF16)
    V_all = dint("V_all", [4 * H, NT], BF16)
    kn_in = dint("kn_in", [32, 1], F32)
    kn_all = dint("kn_all", [128, 1], F32)
    xn1_dr = dint("xn1_dr", [NT, H], BF16)
    rec_dr = dint("rec_dr", [NH, NT], F32)
    xn2_in = dint("xn2_in", [NT, H], BF16)
    xlo_dr = dint("xlo_dr", [NT, H], BF16)
    xn2_all = dint("xn2_all", [NTOK + 128, H], BF16)
    tot_in = dint("tot_in", [1, 32], F32)
    tot_all = dint("tot_all", [8, 32], F32)
    map_in = dint("map_in", [NSLOT + 128, 1], F32)
    map_all = dint("map_all", [NSLOT + 128, 1], F32)
    bufdr = dint("bufdr", [EPC * CAP, H], BF16)
    ob_in = dint("ob_in", [EPC * CAP, H], BF16)
    ob_all = dint("ob_all", [NSLOT + 128, H], BF16)

    dbg = {}
    if debug:
        for name, shape, dt in [
            ("d_xn1", [NT, H], BF16), ("d_kall", [4 * H, NT], BF16),
            ("d_vall", [4 * H, NT], BF16), ("d_q", [128, 16 * NT], BF16),
            ("d_negc", [128, 32], F32), ("d_feats", [128, 16 * NT], BF16),
            ("d_x2", [128, TCH * H], F32), ("d_xn2", [NT, H], BF16),
            ("d_sc", [128, TCH * 32], F32), ("d_idx", [128, TCH * 4], F32),
            ("d_wk", [128, TCH * 4], F32), ("d_gidx", [128, TCH * 4], I32),
            ("d_cs", [128, TCH * 32], F32), ("d_offs", [128, 32], F32),
            ("d_map", [NSLOT + 128, 1], F32), ("d_buf", [EPC * CAP, H], BF16),
            ("d_ob", [EPC * CAP, H], BF16), ("d_hs", [128, 16 * NT], BF16),
            ("d_moe", [128, TCH * H], F32),
        ]:
            dbg[name] = nc.dram_tensor(name, shape, dt, kind="ExternalOutput").ap()

    S = KB(nc)
    S.sem(*["b_" + e for e in ENGS])
    S.sem("c_pe", "c_act", "c_dve", "c_pool")
    S.sem("d_sp", "d_aq", "d_pq")
    S.sem("cc_k", "cc_v", "cc_kn", "cc_x2", "cc_tot", "cc_map", "cc_ob")

    alloc = ExitStack()

    def sb(name, shape, dt):
        return alloc.enter_context(nc.sbuf_tensor(name, shape, dt))

    x_tm = sb("x_tm", [128, TCH, H], F32)
    slotA = sb("slotA", [128, 16 * 1024], BF16)
    slotB = sb("slotB", [128, 16 * 1024], BF16)
    slotC = sb("slotC", [128, 22528], BF16)
    slotF = sb("slotF", [128, 6144], F32)
    idf = sb("idf_s", [128, 128], F32)
    onesb = sb("onesb_s", [128, 128], BF16)
    lstr = sb("lstr_s", [128, 128], BF16)
    onesf = sb("onesf_s", [1, 128], F32)
    qmask = sb("qmask_s", [128, 16, 32], BF16)
    ioe = sb("ioe_s", [128, 32], F32)
    pcol = sb("pcol_s", [128, 1], F32)
    pidx = sb("pidx_s", [128, 1], F32)
    cmask = sb("cmask_s", [8, 1], F32)
    bq_t = sb("bq_s", [128, 16], F32)
    bk_t = sb("bk_s", [128, 16], F32)
    bv_t = sb("bv_s", [128, 16], F32)
    bqs = sb("bqs_s", [128, 16], F32)
    gwt_bf = sb("gwtbf_s", [128, 16, 32], BF16)
    glo_bf = sb("globf_s", [128, 16, 32], BF16)
    gb = sb("gb_s", [32, 1], F32)
    rev32 = sb("rev32_s", [128, 32], F32)
    negc_sb = sb("negc_s", [128, 32], F32)
    qmax = sb("qmax_s", [32, 1], F32)
    kmax = sb("kmax_s", [32, 1], F32)
    knal = sb("knal_s", [32, 4], F32)
    negcT = sb("negcT_s", [1, 32], F32)
    idxf = sb("idxf_s", [128, TCH, 4], F32)
    t2v = sb("t2_s", [128, TCH, 4], F32)
    wkv = sb("wk_s", [128, TCH, 4], F32)
    gidx = sb("gidx_s", [128, TCH, 4], I32)
    csv = sb("cs_s", [128, TCH, 32], F32)
    anv = sb("an_s", [128, TCH, 32], BF16)
    offsb = sb("offs_s", [128, 32], F32)
    totsb = sb("tot_s", [1, 32], F32)
    tota = sb("tota_s", [8, 32], F32)
    offs1 = sb("offs1_s", [1, 32], F32)
    eps_t = sb("eps_s", [128, 1], F32)
    trash_t = sb("trash_s", [128, 1], F32)
    zrow_t = sb("zrow_s", [128, 1], F32)
    mapt = sb("mapt_s", [128, 1], F32)
    mapi = sb("mapi_s", [128, 1], I32)
    mrow = sb("mrow_s", [128, 1], F32)
    mrow_i = sb("mrowi_s", [128, 1], I32)

    sco_t = slotF[:, 0:256].rearrange("p (a b) -> p a b", b=32)
    scb_t = slotF[:, 256:512].rearrange("p (a b) -> p a b", b=32)
    r_gs = slotF[:, 512:516]
    r_m1 = slotF[:, 516:517]
    r_m2 = slotF[:, 517:518]
    r_cnt = slotF[:, 518:519]
    r_w = slotF[:, 519:520]
    r_d = slotF[:, 520:521]
    r_th = slotF[:, 521:522]
    r_eq = slotF[:, 524:532]
    r_t8 = slotF[:, 532:540]
    r_wraw = slotF[:, 540:544]
    r_ws = slotF[:, 544:545]
    r_cat = slotF[:, 545:546]
    r_oat = slotF[:, 546:547]
    r_pos = slotF[:, 547:548]
    r_kp = slotF[:, 548:549]
    r_g0 = slotF[:, 549:550]
    r_adj = slotF[:, 550:551]
    r_tok = slotF[:, 552:560]
    r_work = slotF[:, 576:608]
    r_work2 = slotF[:, 608:640]
    r_r1 = slotF[:, 640:672]
    r_a = slotF[:, 672:704]
    r_oh = slotF[:, 704:832].rearrange("p (a b) -> p a b", b=32)
    gidx_f = slotF[:, 832:864].rearrange("p (a b) -> p a b", b=4)
    sco_f = slotF[0:32, 1024:2048]
    scb_f = slotF[0:32, 2048:3072]
    bn_t = slotF[:, 3104:3128].rearrange("p (a b) -> p a b", b=6)
    gwt_f = slotF[:, 512:1024].rearrange("p (a b) -> p a b", b=32)
    mv = slotF[:, 3128:3130]
    sd1 = slotF[:, 3130:3131]
    rstd = slotF[:, 3131:3132]
    acc = [slotF[:, 1024:3072], slotF[:, 3072:5120]]
    outst = [slotF[:, 5120:5632], slotF[:, 5632:6144]]

    @S.step("sp")
    def _(api):
        for dst, src in [
            (x_tm[:, :, :],
             x_d.rearrange("(a p) f -> p a f", p=128)[:, :, :]),
            (idf[:, :], idf_d[:, :]), (onesb[:, :], onesb_d[:, :]),
            (lstr[:, :], lstr_d[:, :]), (onesf[:, :], onesf_d[:, :]),
            (qmask[:, :, :],
             qmask_d.rearrange("a p e -> p a e")[:, :, :]),
            (ioe[:, :], ioe_d[:, :]), (pcol[:, :], pcol_d[:, :]),
            (pidx[:, :], pidx_d[:, :]), (cmask[:, :], cmask_d[:, :]),
            (bq_t[:, :], bq_d[:, :]), (bk_t[:, :], bk_d[:, :]),
            (bv_t[:, :], bv_d[:, :]), (gb[:, :], gb_d[:, :]),
            (gwt_f[:, :, :], gwt_d[:, :, :]),
        ]:
            api.dma_start(out=dst, in_=src).then_inc(S.h("d_sp"), 16)
    n_setup = S.bump("d_sp", 16 * 15)

    @S.step("pool")
    def _(api):
        api.engine_nop()
    n_setup_pq = S.bump("d_pq", 0)

    @S.step("dve", waits={"d_sp": n_setup})
    def _(api):
        api.tensor_scalar(out=bqs[:, :], in0=bq_t[:, :], scalar1=1.0 / SCALE,
                          scalar2=None, op0=ALU.mult).then_inc(S.h("c_dve"), 1)
        api.tensor_scalar(out=rev32[:, :], in0=ioe[:, :], scalar1=-1.0,
                          scalar2=32.0, op0=ALU.mult, op1=ALU.add).then_inc(S.h("c_dve"), 1)
        api.memset(eps_t[:, :], EPS).then_inc(S.h("c_dve"), 1)
        api.tensor_scalar(out=trash_t[:, :], in0=pidx[:, :],
                          scalar1=float(NSLOT), scalar2=None,
                          op0=ALU.add).then_inc(S.h("c_dve"), 1)
        api.tensor_scalar(out=zrow_t[:, :], in0=pidx[:, :],
                          scalar1=float(NTOK + 1), scalar2=None,
                          op0=ALU.add).then_inc(S.h("c_dve"), 1)
        api.memset(slotC[:, 0:2048], 0.0).then_inc(S.h("c_dve"), 1)
        api.tensor_copy(out=gwt_bf.rearrange("p a e -> p (a e)")[:, :],
                        in_=gwt_f.rearrange("p a e -> p (a e)")[:, :]).then_inc(S.h("c_dve"), 1)
        api.memset(t2v.rearrange("p a k -> p (a k)")[:, :], 0.0).then_inc(S.h("c_dve"), 1)
    m_setup_dve = S.bump("c_dve", 8)
    @S.step("dve", waits={"c_dve": m_setup_dve})
    def _(api):
        api.tensor_tensor(out=glo_bf.rearrange("p a e -> p (a e)")[:, :],
                          in0=gwt_f.rearrange("p a e -> p (a e)")[:, :],
                          in1=gwt_bf.rearrange("p a e -> p (a e)")[:, :],
                          op=ALU.subtract).then_inc(S.h("c_dve"), 1)
    m_setup_dve = S.bump("c_dve")

    @S.step("pool", waits={"c_dve": m_setup_dve})
    def _(api):
        mz = map_in.rearrange("(a b) o -> a (b o)", a=128)
        api.dma_start(out=mz[:, 0:128], in_=slotC[:, 0:128]).then_inc(S.h("d_pq"), 16)
        api.dma_start(out=mz[:, 128:129], in_=slotC[:, 0:1], allow_slow_non_contiguous=True).then_inc(S.h("d_pq"), 16)
        api.dma_start(out=xn2_all[NTOK:NTOK + 128, :],
                      in_=slotC[:, 0:2048]).then_inc(S.h("d_pq"), 16)
        api.dma_start(out=ob_all[NSLOT:NSLOT + 128, :],
                      in_=slotC[:, 0:2048]).then_inc(S.h("d_pq"), 16)
    n_zero_pq = S.bump("d_pq", 64)

    def ph_layernorm(dram_out, dest_fm, first_wait, lo_dram=None, lo_fm=None):
        xnc = [slotC[:, 18432:20480], slotC[:, 20480:22528]]
        xloc = [slotC[:, 0:2048], slotC[:, 2048:4096]]
        xf32 = [slotF[:, 1024:3072], slotF[:, 3072:5120]]
        sp_st = {}
        v_sqrt = 0
        for tc in range(TCH):
            par = tc % 2
            w = dict(first_wait) if tc == 0 else {}
            w["c_act"] = max(w.get("c_act", 0), v_sqrt)
            w["c_dve"] = max(w.get("c_dve", 0), m_setup_dve)
            @S.step("dve", waits=w)
            def _(api, tc=tc):
                for j in range(4):
                    api.bn_stats(out=bn_t[:, j, :],
                                 in_=x_tm[:, tc, j * 512:(j + 1) * 512]).then_inc(S.h("c_dve"), 1)
            v_bn = S.bump("c_dve", 4)
            @S.step("dve", waits={"c_dve": v_bn})
            def _(api):
                api.bn_aggr(out=mv[:, :],
                            in_=slotF[:, 3104:3128]).then_inc(S.h("c_dve"), 1)
            v_ag = S.bump("c_dve")
            @S.step("act", waits={"c_dve": v_ag})
            def _(api):
                api.activation(out=sd1[:, :], in_=mv[:, 1:2], func=AF.Sqrt,
                               bias=eps_t[:, 0:1], scale=1.0).then_inc(S.h("c_act"), 1)
            v_sqrt = S.bump("c_act")
            @S.step("dve", waits={"c_act": v_sqrt})
            def _(api):
                api.reciprocal(out=rstd[:, :], in_=sd1[:, :]).then_inc(S.h("c_dve"), 1)
            v_rs = S.bump("c_dve")
            if lo_dram is None:
                @S.step("dve", waits=mx({"c_dve": v_rs},
                                        {"d_sp": sp_st.get(tc - 2, 0)}))
                def _(api, tc=tc, par=par):
                    api.tensor_scalar(out=xnc[par][:, :], in0=x_tm[:, tc, :],
                                      scalar1=mv[:, 0:1], scalar2=rstd[:, 0:1],
                                      op0=ALU.subtract, op1=ALU.mult).then_inc(S.h("c_dve"), 1)
                v_nm = S.bump("c_dve")
            else:
                @S.step("dve", waits=mx({"c_dve": v_rs},
                                        {"d_sp": sp_st.get(tc - 2, 0)}))
                def _(api, tc=tc, par=par):
                    api.tensor_scalar(out=xf32[par][:, :], in0=x_tm[:, tc, :],
                                      scalar1=mv[:, 0:1], scalar2=rstd[:, 0:1],
                                      op0=ALU.subtract, op1=ALU.mult).then_inc(S.h("c_dve"), 1)
                v_f = S.bump("c_dve")
                @S.step("dve", waits={"c_dve": v_f})
                def _(api, par=par):
                    api.tensor_copy(out=xnc[par][:, :],
                                    in_=xf32[par][:, :]).then_inc(S.h("c_dve"), 1)
                v_hi = S.bump("c_dve")
                @S.step("dve", waits={"c_dve": v_hi})
                def _(api, par=par):
                    api.tensor_tensor(out=xloc[par][:, :], in0=xf32[par][:, :],
                                      in1=xnc[par][:, :],
                                      op=ALU.subtract).then_inc(S.h("c_dve"), 1)
                v_nm = S.bump("c_dve")
            @S.step("sp", waits={"c_dve": v_nm})
            def _(api, tc=tc, par=par):
                api.dma_start(out=dram_out[tc * 128:(tc + 1) * 128, :],
                              in_=xnc[par][:, :]).then_inc(S.h("d_sp"), 16)
                if lo_dram is not None:
                    api.dma_start(out=lo_dram[tc * 128:(tc + 1) * 128, :],
                                  in_=xloc[par][:, :]).then_inc(S.h("d_sp"), 16)
            v_st = S.bump("d_sp", 16 if lo_dram is None else 32)
            sp_st[tc] = v_st
            @S.step("act", waits={"d_sp": v_st})
            def _(api, tc=tc):
                api.dma_start(out=dest_fm[:, :, tc * 128:(tc + 1) * 128],
                              in_=dram_out[tc * 128:(tc + 1) * 128, :],
                              transpose=True).then_inc(S.h("d_aq"), 16)
                if lo_fm is not None:
                    api.dma_start(out=lo_fm[:, :, tc * 128:(tc + 1) * 128],
                                  in_=lo_dram[tc * 128:(tc + 1) * 128, :],
                                  transpose=True).then_inc(S.h("d_aq"), 16)
            S.bump("d_aq", 16 if lo_fm is None else 32)
        return S.n("d_aq"), S.n("d_sp")

    xn_fm = slotA.rearrange("p (a t) -> p a t", t=1024)
    aq_xn1, sp_xn1 = ph_layernorm(xn1_dr, xn_fm, {"d_sp": n_setup})
    if debug:
        @S.step("sp", waits={"d_sp": sp_xn1})
        def _(api):
            api.dma_start(out=dbg["d_xn1"][:, :], in_=xn1_dr[:, :]).then_inc(S.h("d_sp"), 16)
        S.bump("d_sp", 16)

    S.barrier()

    q_fm = slotB.rearrange("p (a t) -> p a t", t=1024)
    wbuf = [slotC[:, 0:2048].rearrange("p (a j) -> p a j", j=128),
            slotC[:, 2048:4096].rearrange("p (a j) -> p a j", j=128)]
    kst = [slotC[:, 4096:5120], slotC[:, 5120:6144]]
    sqt = [slotC[:, 6144:7168], slotC[:, 7168:8192]]

    with nc.psum_tensor("pg", [128, 2, 1024], F32) as pgt, \
         nc.psum_tensor("qn", [32, 1024], F32) as qn_ps, \
         nc.psum_tensor("kn", [32, 1024], F32) as kn_ps:

        v_wb_free = {0: 0, 1: 0}
        v_ps_free = {0: 0, 1: 0}
        v_sq_free = {0: 0, 1: 0}
        v_kst_free = {0: 0, 1: 0}
        mats = [("q", wq_d, None, qn_ps), ("k", wk_d, kvK_in, kn_ps),
                ("v", wv_d, kvV_in, None)]
        for mi, (mn, wd, st_dram, nrm_ps) in enumerate(mats):
            for co in range(16):
                it = mi * 16 + co
                par = it % 2
                @S.step("pool", waits={"c_pe": v_wb_free[par],
                                       "d_pq": n_zero_pq})
                def _(api, wd=wd, co=co, par=par):
                    api.dma_start(out=wbuf[par].rearrange("p a j -> p (a j)")[:, :],
                                  in_=wd[co].rearrange("p a j -> p (a j)")[:, :]
                                  ).then_inc(S.h("d_pq"), 16)
                v_w = S.bump("d_pq", 16)
                @S.step("pe", waits={"d_pq": v_w, "d_aq": aq_xn1,
                                     "c_act": v_ps_free[par]})
                def _(api, par=par):
                    for ci in range(16):
                        for hf in range(2):
                            api.matmul(pgt[:, par, hf * 512:(hf + 1) * 512],
                                       wbuf[par][:, ci, :],
                                       xn_fm[:, ci, hf * 512:(hf + 1) * 512],
                                       start=(ci == 0), stop=(ci == 15)
                                       ).then_inc(S.h("c_pe"), 1)
                v_mm = S.bump("c_pe", 32)
                v_wb_free[par] = v_mm
                if mn == "q":
                    @S.step("act", waits={"c_pe": v_mm})
                    def _(api, co=co, par=par):
                        api.activation(out=q_fm[:, co, :], in_=pgt[:, par, :],
                                       func=AF.Identity, bias=bqs[:, co:co + 1],
                                       scale=1.0 / SCALE).then_inc(S.h("c_act"), 1)
                    v_ev = S.bump("c_act")
                    v_ps_free[par] = v_ev
                    @S.step("act", waits=mx({"c_act": v_ev},
                                            {"c_pe": v_sq_free[par]}))
                    def _(api, co=co, par=par):
                        api.activation(out=sqt[par][:, :], in_=q_fm[:, co, :],
                                       func=AF.Square).then_inc(S.h("c_act"), 1)
                    v_sq = S.bump("c_act")
                    @S.step("pe", waits={"c_act": v_sq})
                    def _(api, co=co, par=par, nrm_ps=nrm_ps):
                        sq = sqt[par].rearrange("p (a b) -> p a b", b=512)
                        for hf in range(2):
                            api.matmul(nrm_ps[:, hf * 512:(hf + 1) * 512],
                                       qmask[:, co, :], sq[:, hf, :],
                                       start=(co == 0), stop=(co == 15)
                                       ).then_inc(S.h("c_pe"), 1)
                    v_sq_free[par] = S.bump("c_pe", 2)
                else:
                    kpar = co % 2
                    @S.step("act", waits=mx({"c_pe": v_mm},
                                            {"d_sp": v_kst_free[kpar]}))
                    def _(api, co=co, par=par, kpar=kpar,
                          bias=(bk_t if mn == "k" else bv_t)):
                        api.activation(out=kst[kpar][:, :], in_=pgt[:, par, :],
                                       func=AF.Identity, bias=bias[:, co:co + 1],
                                       scale=1.0).then_inc(S.h("c_act"), 1)
                    v_ev = S.bump("c_act")
                    v_ps_free[par] = v_ev
                    @S.step("sp", waits={"c_act": v_ev})
                    def _(api, co=co, kpar=kpar, st_dram=st_dram):
                        api.dma_start(out=st_dram[co * 128:(co + 1) * 128, :],
                                      in_=kst[kpar][:, :]).then_inc(S.h("d_sp"), 16)
                    v_kst_free[kpar] = S.bump("d_sp", 16)
                    if mn == "k":
                        @S.step("act", waits=mx({"c_act": v_ev},
                                                {"c_pe": v_sq_free[par]}))
                        def _(api, kpar=kpar, par=par):
                            api.activation(out=sqt[par][:, :], in_=kst[kpar][:, :],
                                           func=AF.Square).then_inc(S.h("c_act"), 1)
                        v_sq = S.bump("c_act")
                        @S.step("pe", waits={"c_act": v_sq})
                        def _(api, co=co, par=par, nrm_ps=nrm_ps):
                            sq = sqt[par].rearrange("p (a b) -> p a b", b=512)
                            for hf in range(2):
                                api.matmul(nrm_ps[:, hf * 512:(hf + 1) * 512],
                                           qmask[:, co, :], sq[:, hf, :],
                                           start=(co == 0), stop=(co == 15)
                                           ).then_inc(S.h("c_pe"), 1)
                        v_sq_free[par] = S.bump("c_pe", 2)
        sp_kv = S.n("d_sp")
        pe_p2 = S.n("c_pe")

        @S.step("pool", waits={"d_sp": sp_kv})
        def _(api):
            api.collective_compute(
                "AllGather", ALU.bypass,
                replica_groups=[[0, 1, 2, 3], [4, 5, 6, 7]],
                ins=[kvK_in[:, :]], outs=[K_all[:, :]]).then_inc(S.h("cc_k"), 1)
            api.collective_compute(
                "AllGather", ALU.bypass,
                replica_groups=[[0, 1, 2, 3], [4, 5, 6, 7]],
                ins=[kvV_in[:, :]], outs=[V_all[:, :]]).then_inc(S.h("cc_v"), 1)
        S.bump("cc_k")
        S.bump("cc_v")

        @S.step("dve", waits={"c_pe": pe_p2})
        def _(api):
            api.tensor_reduce(out=qmax[:, :], in_=qn_ps[:, :], axis=AX,
                              op=ALU.max).then_inc(S.h("c_dve"), 1)
            api.tensor_reduce(out=kmax[:, :], in_=kn_ps[:, :], axis=AX,
                              op=ALU.max).then_inc(S.h("c_dve"), 1)
        v_nr = S.bump("c_dve", 2)
        @S.step("act", waits={"c_dve": v_nr})
        def _(api):
            api.activation(out=qmax[:, :], in_=qmax[:, :],
                           func=AF.Sqrt).then_inc(S.h("c_act"), 1)
            api.activation(out=kmax[:, :], in_=kmax[:, :],
                           func=AF.Sqrt).then_inc(S.h("c_act"), 1)
        v_sqn = S.bump("c_act", 2)
        @S.step("sp", waits={"c_act": v_sqn})
        def _(api):
            api.dma_start(out=kn_in[:, :], in_=kmax[:, :]).then_inc(S.h("d_sp"), 16)
        v_knst = S.bump("d_sp", 16)
        @S.step("pool", waits={"d_sp": v_knst})
        def _(api):
            api.collective_compute(
                "AllGather", ALU.bypass,
                replica_groups=[[0, 1, 2, 3], [4, 5, 6, 7]],
                ins=[kn_in[:, :]], outs=[kn_all[:, :]]).then_inc(S.h("cc_kn"), 1)
        S.bump("cc_kn")
        @S.step("pool", waits={"cc_kn": 1})
        def _(api):
            api.dma_start(out=knal[:, :],
                          in_=kn_all.rearrange("(r h) o -> h (r o)", h=32)[:, :],
                          allow_slow_non_contiguous=True).then_inc(S.h("d_pq"), 16)
        v_knl = S.bump("d_pq", 16)
        @S.step("dve", waits={"d_pq": v_knl, "c_act": v_sqn})
        def _(api):
            api.tensor_reduce(out=kmax[:, :], in_=knal[:, :], axis=AX,
                              op=ALU.max).then_inc(S.h("c_dve"), 1)
        v_km = S.bump("c_dve")
        @S.step("dve", waits={"c_dve": v_km})
        def _(api):
            api.tensor_tensor(out=qmax[:, :], in0=qmax[:, :], in1=kmax[:, :],
                              op=ALU.mult).then_inc(S.h("c_dve"), 1)
        v_ch = S.bump("c_dve")
        @S.step("dve", waits={"c_dve": v_ch})
        def _(api):
            api.tensor_scalar(out=qmax[:, :], in0=qmax[:, :], scalar1=1.0,
                              scalar2=-1.0, op0=ALU.add,
                              op1=ALU.mult).then_inc(S.h("c_dve"), 1)
        v_negc = S.bump("c_dve")

    with nc.psum_tensor("ngT", [1, 32], F32) as ngT, \
         nc.psum_tensor("ngB", [128, 32], F32) as ngB:
        @S.step("pe", waits={"c_dve": v_negc})
        def _(api):
            api.transpose(ngT[:, :], qmax[:, :], idf[0:32, 0:32]).then_inc(S.h("c_pe"), 1)
        v_t = S.bump("c_pe")
        @S.step("dve", waits={"c_pe": v_t})
        def _(api):
            api.tensor_copy(out=negcT[:, :], in_=ngT[:, :]).then_inc(S.h("c_dve"), 1)
        v_tc2 = S.bump("c_dve")
        @S.step("pe", waits={"c_dve": v_tc2})
        def _(api):
            for hh in range(NH):
                api.matmul(ngB[:, hh:hh + 1], onesf[:, :], negcT[:, hh:hh + 1],
                           start=True, stop=True).then_inc(S.h("c_pe"), 1)
        v_bc = S.bump("c_pe", NH)
        @S.step("dve", waits={"c_pe": v_bc})
        def _(api):
            api.tensor_copy(out=negc_sb[:, :], in_=ngB[:, :]).then_inc(S.h("c_dve"), 1)
        v_negcd = S.bump("c_dve")

    if debug:
        @S.step("sp", waits={"c_dve": v_negcd, "cc_k": 1, "cc_v": 1})
        def _(api):
            api.dma_start(out=dbg["d_negc"][:, :], in_=negc_sb[:, :]).then_inc(S.h("d_sp"), 16)
            api.dma_start(out=dbg["d_q"][:, :],
                          in_=q_fm.rearrange("p a t -> p (a t)")[:, :]).then_inc(S.h("d_sp"), 16)
            api.dma_start(out=dbg["d_kall"][:, :], in_=K_all[:, :]).then_inc(S.h("d_sp"), 16)
            api.dma_start(out=dbg["d_vall"][:, :], in_=V_all[:, :]).then_inc(S.h("d_sp"), 16)
        S.bump("d_sp", 64)

    S.barrier()

    feats_fm = slotA.rearrange("p (a t) -> p a t", t=1024)
    kp_t = [slotC[:, 0:4096], slotC[:, 4096:8192]]
    vt_t = [slotC[:, 8192 + i * 2560: 8192 + (i + 1) * 2560]
            .rearrange("p (a b) -> p a b", b=VPAD) for i in range(4)]
    pt_t = [slotC[:, 18432:19456], slotC[:, 19456:20480],
            slotC[:, 20480:21504]]
    recb_t = slotF[:, 0:1024]

    v_p2_pe = S.n("c_pe")
    with nc.psum_tensor("sS", [128, 3, 1024], F32) as sS, \
         nc.psum_tensor("fts", [65, 1024], F32) as fts:

        @S.step("dve", waits={"c_pe": v_p2_pe})
        def _(api):
            for i in range(4):
                api.memset(vt_t[i][:, :, 64:65], 1.0).then_inc(S.h("c_dve"), 1)
        v_ones = S.bump("c_dve", 4)

        kp_ready = [0, 0]
        vt_ready = [0, 0, 0, 0]
        kp_free = [0, 0]
        vt_free = [0, 0, 0, 0]
        s_free = {0: 0, 1: 0, 2: 0}
        pt_free = {0: 0, 1: 0, 2: 0}
        fin = {"act": 0, "dve": 0, "st": 0}
        rec_st = {}

        def load_pair(pr):
            slot = pr % 2
            @S.step("sp", waits={"cc_k": 1,
                                 "c_pe": max(kp_free[slot], v_p2_pe)})
            def _(api, pr=pr, slot=slot):
                kv = kp_t[slot].rearrange("p (r t) -> p r t", r=4)
                for r in range(4):
                    api.dma_start(
                        out=kv[:, r, :],
                        in_=K_all[r * H + pr * 128: r * H + (pr + 1) * 128, :]
                    ).then_inc(S.h("d_sp"), 16)
            kp_ready[slot] = S.bump("d_sp", 64)
            for hl in range(2):
                hh = 2 * pr + hl
                vslot = hh % 4
                @S.step("act", waits={"cc_v": 1, "c_dve": v_ones,
                                      "c_pe": max(vt_free[vslot], v_p2_pe)})
                def _(api, hh=hh, vslot=vslot):
                    for r in range(4):
                        api.dma_start(
                            out=vt_t[vslot][:, r * 8:(r + 1) * 8, 0:64],
                            in_=V_all[r * H + hh * 64: r * H + (hh + 1) * 64, :],
                            transpose=True).then_inc(S.h("d_aq"), 16)
                vt_ready[vslot] = S.bump("d_aq", 64)

        load_pair(0)
        load_pair(1)

        for hh in range(NH):
            pr = hh // 2
            hbase = (hh % 2) * 64
            hchunk = hh // 2
            kslot = pr % 2
            vslot = hh % 4
            kpr = kp_ready[kslot]
            vtr = vt_ready[vslot]

            def qk(kc):
                par = kc % 3
                @S.step("pe", waits=mx({"d_sp": kpr, "c_act": s_free[par]},
                                       {"c_dve": v_negcd}))
                def _(api, kc=kc, par=par, hbase=hbase, hchunk=hchunk,
                      kslot=kslot):
                    kv = kp_t[kslot].rearrange("p (r t) -> p r t", r=4)
                    r, lt = kc // 8, kc % 8
                    lhsT = kv[hbase:hbase + 64, r, lt * 128:(lt + 1) * 128]
                    for hf in range(2):
                        api.matmul(sS[:, par, hf * 512:(hf + 1) * 512], lhsT,
                                   q_fm[hbase:hbase + 64, hchunk,
                                        hf * 512:(hf + 1) * 512],
                                   start=True, stop=True).then_inc(S.h("c_pe"), 1)
                return S.bump("c_pe", 2)

            def ex(kc, v_qk):
                par = kc % 3
                @S.step("act", waits={"c_pe": max(v_qk, pt_free[par])})
                def _(api, kc=kc, par=par, hh=hh):
                    api.activation(out=pt_t[par][:, :], in_=sS[:, par, :],
                                   func=AF.Exp, bias=negc_sb[:, hh:hh + 1],
                                   scale=1.0).then_inc(S.h("c_act"), 1)
                return S.bump("c_act")

            def av(kc, v_ex):
                par = kc % 3
                w = {"c_act": v_ex, "d_aq": vtr}
                if kc == 0:
                    w = mx(w, {"c_act": fin["act"], "c_dve": fin["dve"]})
                @S.step("pe", waits=w)
                def _(api, kc=kc, par=par, vslot=vslot):
                    lhsT = vt_t[vslot][:, kc, 0:65]
                    for hf in range(2):
                        api.matmul(fts[:, hf * 512:(hf + 1) * 512], lhsT,
                                   pt_t[par][:, hf * 512:(hf + 1) * 512],
                                   start=(kc == 0), stop=(kc == NKC - 1)
                                   ).then_inc(S.h("c_pe"), 1)
                return S.bump("c_pe", 2)

            vq = {0: qk(0), 1: qk(1), 2: qk(2)}
            for kc in range(NKC):
                vex = ex(kc, vq[kc])
                s_free[kc % 3] = vex
                vav = av(kc, vex)
                pt_free[kc % 3] = vav
                if kc + 3 < NKC:
                    vq[kc + 3] = qk(kc + 3)
            v_avl = pt_free[(NKC - 1) % 3]
            kp_free[kslot] = max(kp_free[kslot], v_avl)
            vt_free[vslot] = v_avl

            if hh % 2 == 1 and pr + 2 <= 15:
                load_pair(pr + 2)

            @S.step("act", waits={"c_pe": v_avl})
            def _(api, hh=hh, hbase=hbase, hchunk=hchunk):
                api.activation(out=feats_fm[hbase:hbase + 64, hchunk, :],
                               in_=fts[0:64, :],
                               func=AF.Copy).then_inc(S.h("c_act"), 1)
            fin["act"] = S.bump("c_act")
            @S.step("dve", waits={"c_pe": v_avl, "d_sp": fin["st"]})
            def _(api):
                api.reciprocal(out=recb_t[64:65, :],
                               in_=fts[64:65, :]).then_inc(S.h("c_dve"), 1)
            fin["dve"] = S.bump("c_dve")
            @S.step("sp", waits={"c_dve": fin["dve"]})
            def _(api, hh=hh):
                api.dma_start(out=rec_dr[hh:hh + 1, :],
                              in_=recb_t[64:65, :]).then_inc(S.h("d_sp"), 16)
            fin["st"] = S.bump("d_sp", 16)
            rec_st[hh] = fin["st"]

        v_fin_act = fin["act"]
        v_rec_all = fin["st"]

    rec1 = slotF[0:1, 1024:2048]
    with nc.psum_tensor("rb2", [128, 2, 1024], F32) as rb2:
        v_rl = 0
        v_nm2 = {0: 0, 1: 0}
        v_rb2 = {0: 0, 1: 0}
        v_rb_last = 0
        for hh in range(NH):
            par = hh % 2
            hbase = (hh % 2) * 64
            hchunk = hh // 2
            @S.step("act", waits={"d_sp": rec_st[hh], "d_aq": v_rl,
                                  "c_pe": v_rb_last})
            def _(api, hh=hh):
                api.dma_start(out=rec1[:, :], in_=rec_dr[hh:hh + 1, :]
                              ).then_inc(S.h("d_aq"), 16)
            v_rl = S.bump("d_aq", 16)
            @S.step("pe", waits={"d_aq": v_rl, "c_dve": v_nm2[par]})
            def _(api, par=par):
                for hf in range(2):
                    api.matmul(rb2[:, par, hf * 512:(hf + 1) * 512],
                               onesf[0:1, 0:128],
                               rec1[0:1, hf * 512:(hf + 1) * 512],
                               start=True, stop=True).then_inc(S.h("c_pe"), 1)
            v_rb2[par] = S.bump("c_pe", 2)
            v_rb_last = v_rb2[par]
            @S.step("dve", waits={"c_pe": v_rb2[par], "c_act": v_fin_act})
            def _(api, hh=hh, par=par, hbase=hbase, hchunk=hchunk):
                api.tensor_tensor(out=feats_fm[hbase:hbase + 64, hchunk, :],
                                  in0=feats_fm[hbase:hbase + 64, hchunk, :],
                                  in1=rb2[hbase:hbase + 64, par, :],
                                  op=ALU.mult).then_inc(S.h("c_dve"), 1)
            v_nm2[par] = S.bump("c_dve")
    v_attn = S.n("c_dve")
    if debug:
        @S.step("sp", waits={"c_dve": v_attn})
        def _(api):
            api.dma_start(out=dbg["d_feats"][:, :],
                          in_=feats_fm.rearrange("p a t -> p (a t)")[:, :]).then_inc(S.h("d_sp"), 16)
        S.bump("d_sp", 16)

    S.barrier()

    wob = [slotC[:, 0:512], slotC[:, 512:1024]]
    bo_bf = slotC[0:1, 1024:3072]
    @S.step("pool")
    def _(api):
        api.dma_start(out=bo_bf, in_=bo_d[:, :]).then_inc(S.h("d_pq"), 16)
    n_bo_pq = S.bump("d_pq", 16)
    with nc.psum_tensor("po", [128, TCH, 512], F32) as po:
        v_wo_free = {0: 0, 1: 0}
        v_ev = {}
        for fo4 in range(4):
            for ci in range(16):
                it = fo4 * 16 + ci
                par = it % 2
                @S.step("pool", waits={"c_pe": v_wo_free[par]})
                def _(api, fo4=fo4, ci=ci, par=par):
                    api.dma_start(out=wob[par][:, :],
                                  in_=wo_d[ci, :, fo4 * 512:(fo4 + 1) * 512]
                                  ).then_inc(S.h("d_pq"), 16)
                v_l = S.bump("d_pq", 16)
                w = {"d_pq": v_l, "c_dve": v_attn}
                if ci == 0:
                    w = mx(w, {"c_dve": max([v_ev.get((fo4 - 1, tb), 0)
                                             for tb in range(TCH)] + [0])})
                @S.step("pe", waits=w)
                def _(api, fo4=fo4, ci=ci, par=par):
                    for tb in range(TCH):
                        api.matmul(po[:, tb, :],
                                   feats_fm[:, ci, tb * 128:(tb + 1) * 128],
                                   wob[par][:, :], start=(ci == 0), stop=False
                                   ).then_inc(S.h("c_pe"), 1)
                v_wo_free[par] = S.bump("c_pe", TCH)
            @S.step("pe", waits={"d_pq": n_bo_pq})
            def _(api, fo4=fo4):
                for tb in range(TCH):
                    api.matmul(po[:, tb, :], onesb[0:1, :],
                               bo_bf[:, fo4 * 512:(fo4 + 1) * 512],
                               start=False, stop=True).then_inc(S.h("c_pe"), 1)
            v_bo = S.bump("c_pe", TCH)
            for tb in range(TCH):
                @S.step("dve", waits={"c_pe": v_bo})
                def _(api, fo4=fo4, tb=tb):
                    api.tensor_tensor(out=x_tm[:, tb, fo4 * 512:(fo4 + 1) * 512],
                                      in0=po[:, tb, :],
                                      in1=x_tm[:, tb, fo4 * 512:(fo4 + 1) * 512],
                                      op=ALU.add).then_inc(S.h("c_dve"), 1)
                v_ev[(fo4, tb)] = S.bump("c_dve")
    v_x2 = S.n("c_dve")

    if debug:
        @S.step("sp", waits={"c_dve": v_x2})
        def _(api):
            api.dma_start(out=dbg["d_x2"][:, :],
                          in_=x_tm.rearrange("p a f -> p (a f)")[:, :]).then_inc(S.h("d_sp"), 16)
        S.bump("d_sp", 16)

    S.barrier()

    xn2_fm = slotA.rearrange("p (a t) -> p a t", t=1024)
    xlo_fm = slotB.rearrange("p (a t) -> p a t", t=1024)
    aq_xn2, sp_xn2 = ph_layernorm(xn2_in, xn2_fm, {"c_dve": v_x2},
                                  lo_dram=xlo_dr, lo_fm=xlo_fm)
    @S.step("pool", waits={"d_sp": sp_xn2})
    def _(api):
        api.collective_compute(
            "AllGather", ALU.bypass, replica_groups=[list(range(8))],
            ins=[xn2_in[:, :]], outs=[xn2_all[0:NTOK, :]]).then_inc(S.h("cc_x2"), 1)
    S.bump("cc_x2")
    if debug:
        @S.step("sp", waits={"d_sp": sp_xn2})
        def _(api):
            api.dma_start(out=dbg["d_xn2"][:, :], in_=xn2_in[:, :]).then_inc(S.h("d_sp"), 16)
        S.bump("d_sp", 16)

    with nc.psum_tensor("lg", [32, 1024], F32) as lg, \
         nc.psum_tensor("trp", [128, 256], F32) as trp:
        @S.step("pe", waits={"d_aq": aq_xn2, "c_dve": m_setup_dve})
        def _(api):
            terms = [(gwt_bf, xn2_fm), (glo_bf, xn2_fm), (gwt_bf, xlo_fm),
                     (glo_bf, xlo_fm)]
            for ti, (wv, xv) in enumerate(terms):
                for ci in range(16):
                    for hf in range(2):
                        api.matmul(lg[:, hf * 512:(hf + 1) * 512],
                                   wv[:, ci, :],
                                   xv[:, ci, hf * 512:(hf + 1) * 512],
                                   start=(ti == 0 and ci == 0),
                                   stop=(ti == 3 and ci == 15)).then_inc(S.h("c_pe"), 1)
        v_lg = S.bump("c_pe", 128)
        @S.step("act", waits={"c_pe": v_lg})
        def _(api):
            api.activation(out=sco_f[:, :], in_=lg[:, :], func=AF.Exp,
                           scale=-1.0).then_inc(S.h("c_act"), 1)
        v_sg = S.bump("c_act")
        @S.step("dve", waits={"c_act": v_sg})
        def _(api):
            api.tensor_scalar(out=sco_f[:, :], in0=sco_f[:, :],
                              scalar1=1.0, scalar2=None,
                              op0=ALU.add).then_inc(S.h("c_dve"), 1)
        v_s1 = S.bump("c_dve")
        @S.step("dve", waits={"c_dve": v_s1})
        def _(api):
            api.reciprocal(out=sco_f[:, :],
                           in_=sco_f[:, :]).then_inc(S.h("c_dve"), 1)
        v_s2 = S.bump("c_dve")
        @S.step("dve", waits={"c_dve": v_s2})
        def _(api):
            api.tensor_scalar(out=scb_f[:, :], in0=sco_f[:, :],
                              scalar1=gb[:, 0:1], scalar2=None,
                              op0=ALU.add).then_inc(S.h("c_dve"), 1)
        v_scb = S.bump("c_dve")
        @S.step("pe", waits={"c_dve": v_scb})
        def _(api):
            for tc in range(TCH):
                api.transpose(trp[:, tc * 32:(tc + 1) * 32],
                              sco_f[:, tc * 128:(tc + 1) * 128],
                              idf[0:32, 0:32]).then_inc(S.h("c_pe"), 1)
        v_t1 = S.bump("c_pe", TCH)
        @S.step("dve", waits={"c_pe": v_t1})
        def _(api):
            api.tensor_copy(out=sco_t.rearrange("p a b -> p (a b)")[:, :],
                            in_=trp[:, :]).then_inc(S.h("c_dve"), 1)
        v_c1 = S.bump("c_dve")
        @S.step("pe", waits={"c_dve": v_c1})
        def _(api):
            for tc in range(TCH):
                api.transpose(trp[:, tc * 32:(tc + 1) * 32],
                              scb_f[:, tc * 128:(tc + 1) * 128],
                              idf[0:32, 0:32]).then_inc(S.h("c_pe"), 1)
        v_t2 = S.bump("c_pe", TCH)
        @S.step("dve", waits={"c_pe": v_t2})
        def _(api):
            api.tensor_copy(out=scb_t.rearrange("p a b -> p (a b)")[:, :],
                            in_=trp[:, :]).then_inc(S.h("c_dve"), 1)
        v_sc_tm = S.bump("c_dve")

    hs_fm = slotB.rearrange("p (a t) -> p a t", t=1024)
    wgb = [slotC[:, 1024:3072].rearrange("p (a j) -> p a j", j=128),
           slotC[:, 3072:5120].rearrange("p (a j) -> p a j", j=128)]
    wub = [slotC[:, 5120:7168].rearrange("p (a j) -> p a j", j=128),
           slotC[:, 7168:9216].rearrange("p (a j) -> p a j", j=128)]
    gsb = [slotC[:, 9216:10240], slotC[:, 10240:11264]]

    vC = {}
    v_a_bf = {}

    def routing_tc(tc, cps):
        sc = scb_t[:, tc, :]
        sv = sco_t[:, tc, :]
        c = {"v": max(S.n("c_dve"), v_sc_tm)}

        def D(fn, extra=None):
            w = {"c_dve": c["v"]}
            if extra:
                w = mx(w, extra)
            @S.step("dve", waits=w)
            def _(api):
                fn(api).then_inc(S.h("c_dve"), 1)
            c["v"] = S.bump("c_dve")

        D(lambda api, sc=sc: api.tensor_copy(out=r_work[:, :], in_=sc))
        for g in range(4):
            D(lambda api, g=g: api.tensor_reduce(
                out=r_m1[:, :], in_=r_work[:, g * 8:(g + 1) * 8], axis=AX,
                op=ALU.max))
            D(lambda api, g=g: api.tensor_scalar(
                out=r_eq[:, :], in0=r_work[:, g * 8:(g + 1) * 8],
                scalar1=r_m1[:, 0:1], scalar2=None, op0=ALU.is_ge))
            D(lambda api: api.tensor_reduce(
                out=r_cnt[:, :], in_=r_eq[:, :], axis=AX, op=ALU.add))
            D(lambda api: api.tensor_scalar(
                out=r_t8[:, :], in0=r_eq[:, :], scalar1=BIG, scalar2=None,
                op0=ALU.mult))
            D(lambda api, g=g: api.tensor_tensor(
                out=r_t8[:, :], in0=r_work[:, g * 8:(g + 1) * 8],
                in1=r_t8[:, :], op=ALU.subtract))
            D(lambda api: api.tensor_reduce(
                out=r_m2[:, :], in_=r_t8[:, :], axis=AX, op=ALU.max))
            D(lambda api: api.tensor_scalar(
                out=r_w[:, :], in0=r_cnt[:, :], scalar1=1.5, scalar2=None,
                op0=ALU.is_ge))
            D(lambda api: api.tensor_tensor(
                out=r_d[:, :], in0=r_m1[:, :], in1=r_m2[:, :], op=ALU.subtract))
            D(lambda api: api.tensor_tensor(
                out=r_d[:, :], in0=r_w[:, :], in1=r_d[:, :], op=ALU.mult))
            D(lambda api: api.tensor_tensor(
                out=r_d[:, :], in0=r_m2[:, :], in1=r_d[:, :], op=ALU.add))
            D(lambda api, g=g: api.tensor_tensor(
                out=r_gs[:, g:g + 1], in0=r_m1[:, :], in1=r_d[:, :], op=ALU.add))
        D(lambda api: api.tensor_reduce(
            out=r_m1[:, :], in_=r_gs[:, :], axis=AX, op=ALU.max))
        D(lambda api: api.tensor_scalar(
            out=r_eq[:, 0:4], in0=r_gs[:, :], scalar1=r_m1[:, 0:1],
            scalar2=None, op0=ALU.is_ge))
        D(lambda api: api.tensor_reduce(
            out=r_cnt[:, :], in_=r_eq[:, 0:4], axis=AX, op=ALU.add))
        D(lambda api: api.tensor_scalar(
            out=r_t8[:, 0:4], in0=r_eq[:, 0:4], scalar1=BIG, scalar2=None,
            op0=ALU.mult))
        D(lambda api: api.tensor_tensor(
            out=r_t8[:, 0:4], in0=r_gs[:, :], in1=r_t8[:, 0:4],
            op=ALU.subtract))
        D(lambda api: api.tensor_reduce(
            out=r_m2[:, :], in_=r_t8[:, 0:4], axis=AX, op=ALU.max))
        D(lambda api: api.tensor_scalar(
            out=r_w[:, :], in0=r_cnt[:, :], scalar1=1.5, scalar2=None,
            op0=ALU.is_ge))
        D(lambda api: api.tensor_tensor(
            out=r_d[:, :], in0=r_m1[:, :], in1=r_m2[:, :], op=ALU.subtract))
        D(lambda api: api.tensor_tensor(
            out=r_d[:, :], in0=r_w[:, :], in1=r_d[:, :], op=ALU.mult))
        D(lambda api: api.tensor_tensor(
            out=r_th[:, :], in0=r_m2[:, :], in1=r_d[:, :], op=ALU.add))
        for g in range(4):
            D(lambda api, g=g: api.tensor_scalar(
                out=r_eq[:, g:g + 1], in0=r_gs[:, g:g + 1],
                scalar1=r_th[:, 0:1], scalar2=None, op0=ALU.is_ge))
            D(lambda api, g=g, sc=sc: api.tensor_scalar(
                out=r_work[:, g * 8:(g + 1) * 8], in0=sc[:, g * 8:(g + 1) * 8],
                scalar1=r_eq[:, g:g + 1], scalar2=None, op0=ALU.mult))
        for k in range(KTOP):
            src = r_work if k % 2 == 0 else r_work2
            dst = r_work2 if k % 2 == 0 else r_work
            D(lambda api, src=src: api.tensor_reduce(
                out=r_m1[:, :], in_=src[:, :], axis=AX, op=ALU.max))
            D(lambda api, src=src: api.tensor_scalar(
                out=r_r1[:, :], in0=src[:, :], scalar1=r_m1[:, 0:1],
                scalar2=None, op0=ALU.is_ge))
            D(lambda api: api.tensor_tensor(
                out=r_r1[:, :], in0=r_r1[:, :], in1=rev32[:, :], op=ALU.mult))
            D(lambda api: api.tensor_reduce(
                out=r_m2[:, :], in_=r_r1[:, :], axis=AX, op=ALU.max))
            D(lambda api, tc=tc, k=k: api.tensor_scalar(
                out=idxf[:, tc, k:k + 1], in0=r_m2[:, :], scalar1=-1.0,
                scalar2=32.0, op0=ALU.mult, op1=ALU.add))
            D(lambda api, tc=tc, k=k: api.tensor_scalar(
                out=r_oh[:, k, :], in0=ioe[:, :],
                scalar1=idxf[:, tc, k:k + 1], scalar2=None, op0=ALU.is_equal))
            D(lambda api, k=k, sv=sv: api.tensor_tensor(
                out=r_r1[:, :], in0=sv, in1=r_oh[:, k, :], op=ALU.mult))
            D(lambda api, k=k: api.tensor_reduce(
                out=r_wraw[:, k:k + 1], in_=r_r1[:, :], axis=AX, op=ALU.add))
            D(lambda api, k=k, src=src, dst=dst: api.scalar_tensor_tensor(
                out=dst[:, :], in0=r_oh[:, k, :], scalar=-BIG, in1=src[:, :],
                op0=ALU.mult, op1=ALU.add))
        for k in range(1, KTOP):
            for kp2 in range(k):
                D(lambda api, tc=tc, k=k, kp2=kp2: api.tensor_tensor(
                    out=r_d[:, :], in0=idxf[:, tc, kp2:kp2 + 1],
                    in1=idxf[:, tc, k:k + 1], op=ALU.is_equal))
                if kp2 == 0:
                    D(lambda api, tc=tc, k=k: api.tensor_copy(
                        out=t2v[:, tc, k:k + 1], in_=r_d[:, :]))
                else:
                    D(lambda api, tc=tc, k=k: api.tensor_tensor(
                        out=t2v[:, tc, k:k + 1], in0=t2v[:, tc, k:k + 1],
                        in1=r_d[:, :], op=ALU.add))
        D(lambda api: api.tensor_reduce(
            out=r_ws[:, :], in_=r_wraw[:, :], axis=AX, op=ALU.add))
        D(lambda api: api.tensor_scalar(
            out=r_ws[:, :], in0=r_ws[:, :], scalar1=1e-20, scalar2=None,
            op0=ALU.add))
        D(lambda api: api.reciprocal(out=r_ws[:, :], in_=r_ws[:, :]))
        D(lambda api, tc=tc: api.tensor_scalar(
            out=wkv[:, tc, :], in0=r_wraw[:, :], scalar1=r_ws[:, 0:1],
            scalar2=2.0, op0=ALU.mult, op1=ALU.mult))
        D(lambda api: api.tensor_tensor(
            out=r_a[:, :], in0=r_oh[:, 0, :], in1=r_oh[:, 1, :], op=ALU.add))
        D(lambda api: api.tensor_tensor(
            out=r_a[:, :], in0=r_a[:, :], in1=r_oh[:, 2, :], op=ALU.add))
        D(lambda api: api.tensor_tensor(
            out=r_a[:, :], in0=r_a[:, :], in1=r_oh[:, 3, :], op=ALU.add))
        D(lambda api, tc=tc: api.tensor_copy(out=anv[:, tc, :], in_=r_a[:, :]))
        v_a_bf[tc] = c["v"]
        @S.step("pe", waits={"c_dve": v_a_bf[tc]})
        def _(api, tc=tc):
            if tc == 0:
                api.matmul(cps[:, :], lstr[:, :], anv[:, 0, :],
                           start=True, stop=True).then_inc(S.h("c_pe"), 1)
            else:
                for j in range(tc):
                    api.matmul(cps[:, :], onesb[:, :], anv[:, j, :],
                               start=(j == 0), stop=False)
                api.matmul(cps[:, :], lstr[:, :], anv[:, tc, :],
                           start=False, stop=True).then_inc(S.h("c_pe"), 1)
        v_cp = S.bump("c_pe")
        @S.step("dve", waits={"c_pe": v_cp})
        def _(api, tc=tc):
            api.tensor_copy(out=csv[:, tc, :], in_=cps[:, :]).then_inc(S.h("c_dve"), 1)
        vC[tc] = S.bump("c_dve")

    with nc.psum_tensor("pg2", [128, 2, 1024], F32) as pg2, \
         nc.psum_tensor("pu2", [128, 1024], F32) as pu2, \
         nc.psum_tensor("cps", [128, 32], F32) as cps, \
         nc.psum_tensor("totp", [1, 32], F32) as totp:
        v_gb_free = {0: 0, 1: 0}
        v_ub_free = {0: 0, 1: 0}
        v_pg_free = {0: 0, 1: 0}
        v_pu_free = 0
        for co in range(16):
            par = co % 2
            @S.step("pool", waits={"c_pe": v_gb_free[par], "cc_x2": 1})
            def _(api, co=co, par=par):
                api.dma_start(out=wgb[par].rearrange("p a j -> p (a j)")[:, :],
                              in_=wsg_d[co].rearrange("p a j -> p (a j)")[:, :]
                              ).then_inc(S.h("d_pq"), 16)
            v_lg2 = S.bump("d_pq", 16)
            @S.step("pool", waits={"c_pe": v_ub_free[par], "cc_x2": 1})
            def _(api, co=co, par=par):
                api.dma_start(out=wub[par].rearrange("p a j -> p (a j)")[:, :],
                              in_=wsu_d[co].rearrange("p a j -> p (a j)")[:, :]
                              ).then_inc(S.h("d_pq"), 16)
            v_lu2 = S.bump("d_pq", 16)
            @S.step("pe", waits={"d_pq": v_lg2, "d_aq": aq_xn2,
                                 "c_act": v_pg_free[par]})
            def _(api, par=par):
                for ci in range(16):
                    for hf in range(2):
                        api.matmul(pg2[:, par, hf * 512:(hf + 1) * 512],
                                   wgb[par][:, ci, :],
                                   xn2_fm[:, ci, hf * 512:(hf + 1) * 512],
                                   start=(ci == 0), stop=(ci == 15)).then_inc(S.h("c_pe"), 1)
            v_g2 = S.bump("c_pe", 32)
            v_gb_free[par] = v_g2
            @S.step("pe", waits={"d_pq": v_lu2, "c_dve": v_pu_free})
            def _(api, par=par):
                for ci in range(16):
                    for hf in range(2):
                        api.matmul(pu2[:, hf * 512:(hf + 1) * 512],
                                   wub[par][:, ci, :],
                                   xn2_fm[:, ci, hf * 512:(hf + 1) * 512],
                                   start=(ci == 0), stop=(ci == 15)).then_inc(S.h("c_pe"), 1)
            v_u2 = S.bump("c_pe", 32)
            v_ub_free[par] = v_u2
            @S.step("act", waits={"c_pe": v_g2})
            def _(api, par=par):
                api.activation(out=gsb[par][:, :], in_=pg2[:, par, :],
                               func=AF.Silu).then_inc(S.h("c_act"), 1)
            v_si = S.bump("c_act")
            v_pg_free[par] = v_si
            @S.step("dve", waits={"c_act": v_si, "c_pe": max(v_u2, v_lg)})
            def _(api, co=co, par=par):
                api.tensor_tensor(out=hs_fm[:, co, :], in0=gsb[par][:, :],
                                  in1=pu2[:, :], op=ALU.mult).then_inc(S.h("c_dve"), 1)
            v_pu_free = S.bump("c_dve")
            if co % 2 == 1 and co // 2 < TCH:
                routing_tc(co // 2, cps)
        for tc in range(TCH):
            if tc not in vC:
                routing_tc(tc, cps)
        v_hs_done = v_pu_free

        @S.step("pe", waits={"c_dve": v_a_bf[TCH - 1]})
        def _(api):
            for tc in range(TCH):
                api.matmul(totp[:, :], onesb[:, 0:1], anv[:, tc, :],
                           start=(tc == 0), stop=(tc == TCH - 1)).then_inc(S.h("c_pe"), 1)
        v_tp = S.bump("c_pe", TCH)
        @S.step("dve", waits={"c_pe": v_tp})
        def _(api):
            api.tensor_copy(out=totsb[:, :], in_=totp[:, :]).then_inc(S.h("c_dve"), 1)
        v_ts = S.bump("c_dve")
        @S.step("sp", waits={"c_dve": v_ts})
        def _(api):
            api.dma_start(out=tot_in[:, :], in_=totsb[:, :]).then_inc(S.h("d_sp"), 16)
        v_tst = S.bump("d_sp", 16)
        @S.step("pool", waits={"d_sp": v_tst})
        def _(api):
            api.collective_compute(
                "AllGather", ALU.bypass, replica_groups=[list(range(8))],
                ins=[tot_in[:, :]], outs=[tot_all[:, :]]).then_inc(S.h("cc_tot"), 1)
        S.bump("cc_tot")
        @S.step("pool", waits={"cc_tot": 1})
        def _(api):
            api.dma_start(out=tota[:, :], in_=tot_all[:, :]).then_inc(S.h("d_pq"), 16)
        v_tl = S.bump("d_pq", 16)

    v_hs_pe = S.n("c_pe")

    with nc.psum_tensor("ofp", [1, 32], F32) as ofp, \
         nc.psum_tensor("ofb", [128, 32], F32) as ofb:
        @S.step("pe", waits={"d_pq": v_tl})
        def _(api):
            api.matmul(ofp[:, :], cmask[:, :], tota[:, :],
                       start=True, stop=True).then_inc(S.h("c_pe"), 1)
        v_of = S.bump("c_pe")
        @S.step("dve", waits={"c_pe": v_of})
        def _(api):
            api.tensor_copy(out=offs1[:, :], in_=ofp[:, :]).then_inc(S.h("c_dve"), 1)
        v_o1 = S.bump("c_dve")
        @S.step("pe", waits={"c_dve": v_o1})
        def _(api):
            api.matmul(ofb[:, :], onesf[:, :], offs1[:, :],
                       start=True, stop=True).then_inc(S.h("c_pe"), 1)
        v_ob2 = S.bump("c_pe")
        @S.step("dve", waits={"c_pe": v_ob2})
        def _(api):
            api.tensor_copy(out=offsb[:, :], in_=ofb[:, :]).then_inc(S.h("c_dve"), 1)
        v_offs = S.bump("c_dve")

    c2 = {"v": S.n("c_dve")}

    def D2(fn, extra=None):
        w = {"c_dve": c2["v"]}
        if extra:
            w = mx(w, extra)
        @S.step("dve", waits=w)
        def _(api):
            fn(api).then_inc(S.h("c_dve"), 1)
        c2["v"] = S.bump("c_dve")

    v_gidx = {}
    v_scat = 0
    for tc in range(TCH):
        for k in range(KTOP):
            D2(lambda api, tc=tc, k=k: api.tensor_scalar(
                out=r_a[:, :], in0=ioe[:, :], scalar1=idxf[:, tc, k:k + 1],
                scalar2=None, op0=ALU.is_equal),
               {"c_dve": max(v_offs, vC[tc])})
            D2(lambda api, tc=tc: api.tensor_tensor(
                out=r_r1[:, :], in0=csv[:, tc, :], in1=r_a[:, :], op=ALU.mult))
            D2(lambda api: api.tensor_reduce(
                out=r_cat[:, :], in_=r_r1[:, :], axis=AX, op=ALU.add))
            D2(lambda api: api.tensor_tensor(
                out=r_r1[:, :], in0=offsb[:, :], in1=r_a[:, :], op=ALU.mult))
            D2(lambda api: api.tensor_reduce(
                out=r_oat[:, :], in_=r_r1[:, :], axis=AX, op=ALU.add))
            D2(lambda api, tc=tc, k=k: api.tensor_tensor(
                out=r_pos[:, :], in0=r_cat[:, :], in1=t2v[:, tc, k:k + 1],
                op=ALU.add))
            D2(lambda api: api.tensor_tensor(
                out=r_pos[:, :], in0=r_pos[:, :], in1=r_oat[:, :], op=ALU.add))
            D2(lambda api: api.tensor_scalar(
                out=r_kp[:, :], in0=r_pos[:, :], scalar1=511.5, scalar2=None,
                op0=ALU.is_le))
            D2(lambda api, tc=tc, k=k: api.tensor_scalar(
                out=r_g0[:, :], in0=idxf[:, tc, k:k + 1], scalar1=512.0,
                scalar2=None, op0=ALU.mult))
            D2(lambda api: api.tensor_tensor(
                out=r_g0[:, :], in0=r_g0[:, :], in1=r_pos[:, :], op=ALU.add))
            D2(lambda api: api.tensor_scalar(
                out=r_adj[:, :], in0=r_kp[:, :], scalar1=-1.0, scalar2=1.0,
                op0=ALU.mult, op1=ALU.add))
            D2(lambda api: api.tensor_tensor(
                out=r_adj[:, :], in0=r_adj[:, :], in1=trash_t[:, :],
                op=ALU.mult))
            D2(lambda api: api.tensor_tensor(
                out=r_g0[:, :], in0=r_g0[:, :], in1=r_kp[:, :], op=ALU.mult))
            D2(lambda api, tc=tc, k=k: api.tensor_tensor(
                out=gidx_f[:, tc, k:k + 1], in0=r_g0[:, :],
                in1=r_adj[:, :], op=ALU.add))
            D2(lambda api, tc=tc, k=k: api.tensor_copy(
                out=gidx[:, tc, k:k + 1], in_=gidx_f[:, tc, k:k + 1]))
            v_gidx[(tc, k)] = c2["v"]
    for tc in range(TCH):
        D2(lambda api, tc=tc: api.tensor_scalar(
            out=r_tok[:, tc:tc + 1], in0=pcol[:, :],
            scalar1=float(tc * 128 + 1), scalar2=None, op0=ALU.add))
        v_tok = c2["v"]
        @S.step("pool", waits={"c_dve": max(v_tok, v_gidx[(tc, KTOP - 1)])})
        def _(api, tc=tc):
            for k in range(KTOP):
                api.indirect_dma_start(
                    out=map_in[:, :],
                    out_offset=IOA(ap=gidx[:, tc, k:k + 1], axis=0),
                    in_=r_tok[:, tc:tc + 1], in_offset=None,
                ).then_inc(S.h("d_pq"), 16)
        v_scat = S.bump("d_pq", 16 * KTOP)

    @S.step("pool", waits={"d_pq": v_scat})
    def _(api):
        api.collective_compute(
            "AllReduce", ALU.add, replica_groups=[list(range(8))],
            ins=[map_in[:, :]], outs=[map_all[:, :]]).then_inc(S.h("cc_map"), 1)
    S.bump("cc_map")

    if debug:
        @S.step("sp", waits={"c_dve": c2["v"]})
        def _(api):
            api.dma_start(out=dbg["d_sc"][:, :],
                          in_=scb_t.rearrange("p a b -> p (a b)")[:, :]).then_inc(S.h("d_sp"), 16)
            api.dma_start(out=dbg["d_idx"][:, :],
                          in_=idxf.rearrange("p a b -> p (a b)")[:, :]).then_inc(S.h("d_sp"), 16)
            api.dma_start(out=dbg["d_wk"][:, :],
                          in_=wkv.rearrange("p a b -> p (a b)")[:, :]).then_inc(S.h("d_sp"), 16)
            api.dma_start(out=dbg["d_gidx"][:, :],
                          in_=gidx.rearrange("p a b -> p (a b)")[:, :]).then_inc(S.h("d_sp"), 16)
            api.dma_start(out=dbg["d_cs"][:, :],
                          in_=csv.rearrange("p a b -> p (a b)")[:, :]).then_inc(S.h("d_sp"), 16)
            api.dma_start(out=dbg["d_offs"][:, :], in_=offsb[:, :]).then_inc(S.h("d_sp"), 16)
        S.bump("d_sp", 96)
        @S.step("pool", waits={"cc_map": 1})
        def _(api):
            api.dma_start(out=dbg["d_map"][:, :], in_=map_all[:, :]).then_inc(S.h("d_pq"), 16)
        S.bump("d_pq", 16)

    S.barrier()

    w2sb = slotA.rearrange("p (a f) -> p a f", f=2048)
    buf_fm = slotC[:, 0:8192].rearrange("p (a b) -> p a b", b=512)
    obsb = [slotC[:, 0:2048], slotC[:, 2048:4096]]
    buf_tm = slotC[:, 8192:10240]
    hh_fm = slotC[:, 10240:14336].rearrange("p (a b) -> p a b", b=512)
    w1b = [slotC[:, 14336:16384].rearrange("p (a j) -> p a j", j=128),
           slotC[:, 16384:18432].rearrange("p (a j) -> p a j", j=128)]
    w3b = [slotC[:, 18432:20480].rearrange("p (a j) -> p a j", j=128),
           slotC[:, 20480:22528].rearrange("p (a j) -> p a j", j=128)]

    v_w2_free = v_hs_pe
    v_ob_store = 0
    v_gath_d = 0
    v_pq_prev = 0
    for le in range(EPC):
        @S.step("pool", waits={"c_pe": v_w2_free})
        def _(api, le=le):
            for ci in range(8):
                api.dma_start(out=w2sb[:, ci, :],
                              in_=w2_d[le, ci, :, :]).then_inc(S.h("d_pq"), 16)
        v_w2 = S.bump("d_pq", 16 * 8)
        v_gfin = 0
        for j in range(4):
            c3 = {"v": S.n("c_dve")}

            def D3(fn, extra=None):
                w = {"c_dve": c3["v"]}
                if extra:
                    w = mx(w, extra)
                @S.step("dve", waits=w)
                def _(api):
                    fn(api).then_inc(S.h("c_dve"), 1)
                c3["v"] = S.bump("c_dve")

            D3(lambda api: api.tensor_scalar(
                out=mrow[:, :], in0=pcol[:, :], scalar1=2.0, scalar2=None,
                op0=ALU.mult), {"d_pq": v_pq_prev})
            D3(lambda api: api.tensor_tensor(
                out=mrow[:, :], in0=mrow[:, :], in1=pidx[:, :],
                op=ALU.subtract))
            D3(lambda api, le=le, j=j: api.tensor_scalar(
                out=mrow[:, :], in0=mrow[:, :],
                scalar1=float(le * CAP + j * 128), scalar2=None, op0=ALU.add))
            D3(lambda api: api.tensor_copy(out=mrow_i[:, :], in_=mrow[:, :]))
            v_mr = c3["v"]
            @S.step("pool", waits={"c_dve": v_mr, "cc_map": 1})
            def _(api):
                api.indirect_dma_start(
                    out=mapt[:, :], out_offset=None,
                    in_=map_all[:, :], in_offset=IOA(ap=mrow_i[:, :], axis=0),
                ).then_inc(S.h("d_pq"), 16)
            v_mg = S.bump("d_pq", 16)
            D3(lambda api: api.tensor_scalar(
                out=r_kp[:, :], in0=mapt[:, :], scalar1=0.5, scalar2=None,
                op0=ALU.is_le), {"d_pq": v_mg})
            D3(lambda api: api.tensor_tensor(
                out=r_kp[:, :], in0=r_kp[:, :], in1=zrow_t[:, :],
                op=ALU.mult))
            D3(lambda api: api.tensor_scalar(
                out=r_g0[:, :], in0=mapt[:, :], scalar1=1.0, scalar2=None,
                op0=ALU.subtract))
            D3(lambda api: api.tensor_tensor(
                out=r_g0[:, :], in0=r_g0[:, :], in1=r_kp[:, :], op=ALU.add))
            D3(lambda api: api.tensor_copy(out=mapi[:, :], in_=r_g0[:, :]))
            v_mi = c3["v"]
            @S.step("pool", waits={"c_dve": v_mi, "cc_x2": 1,
                                   "d_sp": v_gath_d})
            def _(api):
                api.indirect_dma_start(
                    out=buf_tm[:, :], out_offset=None,
                    in_=xn2_all[:, :], in_offset=IOA(ap=mapi[:, :], axis=0),
                ).then_inc(S.h("d_pq"), 16)
            v_bg = S.bump("d_pq", 16)
            @S.step("sp", waits={"d_pq": v_bg})
            def _(api, le=le, j=j):
                api.dma_start(
                    out=bufdr[le * CAP + j * 128: le * CAP + (j + 1) * 128, :],
                    in_=buf_tm[:, :]).then_inc(S.h("d_sp"), 16)
            v_gfin = S.bump("d_sp", 16)
            v_gath_d = v_gfin
            v_pq_prev = v_bg
        @S.step("act", waits={"d_sp": max(v_gfin, v_ob_store)})
        def _(api, le=le):
            api.dma_start(out=buf_fm[:, :, :],
                          in_=bufdr[le * CAP:(le + 1) * CAP, :],
                          transpose=True).then_inc(S.h("d_aq"), 16)
        v_bfm = S.bump("d_aq", 16)
        if debug and le == EPC - 1:
            @S.step("sp", waits={"d_sp": v_gfin})
            def _(api):
                api.dma_start(out=dbg["d_buf"][:, :], in_=bufdr[:, :]).then_inc(S.h("d_sp"), 16)
            S.bump("d_sp", 16)

        with nc.psum_tensor(f"h1p{le}", [128, 2, 512], F32) as h1p, \
             nc.psum_tensor(f"h3p{le}", [128, 2, 512], F32) as h3p:
            v_w13_free = {0: 0, 1: 0}
            v_hh_free = {0: 0, 1: 0}
            for hc in range(8):
                par = hc % 2
                @S.step("pool", waits={"c_pe": v_w13_free[par]})
                def _(api, le=le, hc=hc, par=par):
                    api.dma_start(out=w1b[par].rearrange("p a j -> p (a j)")[:, :],
                                  in_=w1_d[le, hc].rearrange("p a j -> p (a j)")[:, :]
                                  ).then_inc(S.h("d_pq"), 16)
                    api.dma_start(out=w3b[par].rearrange("p a j -> p (a j)")[:, :],
                                  in_=w3_d[le, hc].rearrange("p a j -> p (a j)")[:, :]
                                  ).then_inc(S.h("d_pq"), 16)
                v_wld = S.bump("d_pq", 32)
                @S.step("pe", waits={"d_pq": v_wld, "d_aq": v_bfm,
                                     "c_dve": v_hh_free[par]})
                def _(api, par=par):
                    for ci in range(16):
                        api.matmul(h1p[:, par, :], w1b[par][:, ci, :],
                                   buf_fm[:, ci, :], start=(ci == 0),
                                   stop=(ci == 15)).then_inc(S.h("c_pe"), 1)
                    for ci in range(16):
                        api.matmul(h3p[:, par, :], w3b[par][:, ci, :],
                                   buf_fm[:, ci, :], start=(ci == 0),
                                   stop=(ci == 15)).then_inc(S.h("c_pe"), 1)
                v_hm = S.bump("c_pe", 32)
                v_w13_free[par] = v_hm
                @S.step("act", waits={"c_pe": v_hm})
                def _(api, hc=hc, par=par):
                    api.activation(out=hh_fm[:, hc, :], in_=h1p[:, par, :],
                                   func=AF.Silu).then_inc(S.h("c_act"), 1)
                v_si2 = S.bump("c_act")
                @S.step("dve", waits={"c_act": v_si2})
                def _(api, hc=hc, par=par):
                    api.tensor_tensor(out=hh_fm[:, hc, :], in0=hh_fm[:, hc, :],
                                      in1=h3p[:, par, :], op=ALU.mult
                                      ).then_inc(S.h("c_dve"), 1)
                v_hh_free[par] = S.bump("c_dve")
            v_hh_all = max(v_hh_free[0], v_hh_free[1])

        with nc.psum_tensor(f"obp{le}", [128, 2, H], F32) as obp:
            v_obe = {0: 0, 1: 0}
            v_obs = {0: 0, 1: 0}
            for j in range(4):
                par = j % 2
                @S.step("pe", waits={"c_dve": v_hh_all, "d_pq": v_w2,
                                     "c_act": v_obe[par]})
                def _(api, j=j, par=par):
                    for hc in range(8):
                        for q in range(4):
                            api.matmul(obp[:, par, q * 512:(q + 1) * 512],
                                       hh_fm[:, hc, j * 128:(j + 1) * 128],
                                       w2sb[:, hc, q * 512:(q + 1) * 512],
                                       start=(hc == 0), stop=(hc == 7)
                                       ).then_inc(S.h("c_pe"), 1)
                v_od = S.bump("c_pe", 32)
                v_w2_free = v_od
                @S.step("act", waits={"c_pe": v_od, "d_sp": v_obs[par]})
                def _(api, par=par):
                    api.activation(out=obsb[par][:, :], in_=obp[:, par, :],
                                   func=AF.Copy).then_inc(S.h("c_act"), 1)
                v_oe = S.bump("c_act")
                v_obe[par] = v_oe
                @S.step("sp", waits={"c_act": v_oe})
                def _(api, le=le, j=j, par=par):
                    api.dma_start(
                        out=ob_in[le * CAP + j * 128: le * CAP + (j + 1) * 128, :],
                        in_=obsb[par][:, :]).then_inc(S.h("d_sp"), 16)
                v_obs[par] = S.bump("d_sp", 16)
            v_ob_store = S.n("d_sp")

    @S.step("pool", waits={"d_sp": v_ob_store})
    def _(api):
        api.collective_compute(
            "AllGather", ALU.bypass, replica_groups=[list(range(8))],
            ins=[ob_in[:, :]], outs=[ob_all[0:NSLOT, :]]).then_inc(S.h("cc_ob"), 1)
    S.bump("cc_ob")
    if debug:
        @S.step("sp", waits={"d_sp": v_ob_store})
        def _(api):
            api.dma_start(out=dbg["d_ob"][:, :], in_=ob_in[:, :]).then_inc(S.h("d_sp"), 16)
        S.bump("d_sp", 16)

    S.barrier()

    ysb = [slotC[:, 0:2048], slotC[:, 2048:4096]]
    wsb2 = [slotC[:, 4096:4608], slotC[:, 4608:5120]]
    v_y_free = {0: 0, 1: 0}
    v_xc = 0
    for tc in range(TCH):
        vdv = S.n("c_dve")
        for k in range(KTOP):
            par = k % 2
            @S.step("pool", waits={"cc_ob": 1, "c_dve": v_y_free[par]})
            def _(api, tc=tc, k=k, par=par):
                api.indirect_dma_start(
                    out=ysb[par][:, :], out_offset=None,
                    in_=ob_all[:, :],
                    in_offset=IOA(ap=gidx[:, tc, k:k + 1], axis=0),
                ).then_inc(S.h("d_pq"), 16)
            v_yg = S.bump("d_pq", 16)
            if k == 0:
                @S.step("dve", waits={"d_pq": v_yg, "c_dve": vdv})
                def _(api, tc=tc):
                    api.tensor_scalar(out=acc[0][:, :], in0=ysb[0][:, :],
                                      scalar1=wkv[:, tc, 0:1], scalar2=None,
                                      op0=ALU.mult).then_inc(S.h("c_dve"), 1)
            else:
                @S.step("dve", waits={"d_pq": v_yg, "c_dve": S.n("c_dve")})
                def _(api, tc=tc, k=k, par=par):
                    api.scalar_tensor_tensor(
                        out=acc[par][:, :], in0=ysb[par][:, :],
                        scalar=wkv[:, tc, k:k + 1], in1=acc[1 - par][:, :],
                        op0=ALU.mult, op1=ALU.add).then_inc(S.h("c_dve"), 1)
            v_y_free[par] = S.bump("c_dve")
        @S.step("dve", waits={"c_dve": S.n("c_dve")})
        def _(api, tc=tc):
            api.tensor_tensor(out=x_tm[:, tc, :], in0=x_tm[:, tc, :],
                              in1=acc[(KTOP - 1) % 2][:, :], op=ALU.add
                              ).then_inc(S.h("c_dve"), 1)
        v_xc = S.bump("c_dve")
        v_y_free[(KTOP - 1) % 2] = v_xc
    v_moe = v_xc
    if debug:
        @S.step("sp", waits={"c_dve": v_moe})
        def _(api):
            api.dma_start(out=dbg["d_moe"][:, :],
                          in_=x_tm.rearrange("p a f -> p (a f)")[:, :]).then_inc(S.h("d_sp"), 16)
        S.bump("d_sp", 16)

    with nc.psum_tensor("sdp", [128, TCH, 512], F32) as sdp:
        v_sd_free = {0: 0, 1: 0}
        v_oev = {}
        v_ost = {0: 0, 1: 0}
        for fo4 in range(4):
            for ci in range(16):
                it = fo4 * 16 + ci
                par = it % 2
                @S.step("pool", waits={"c_pe": v_sd_free[par], "cc_ob": 1})
                def _(api, fo4=fo4, ci=ci, par=par):
                    api.dma_start(out=wsb2[par][:, :],
                                  in_=wsd_d[ci, :, fo4 * 512:(fo4 + 1) * 512]
                                  ).then_inc(S.h("d_pq"), 16)
                v_l = S.bump("d_pq", 16)
                w = {"d_pq": v_l, "c_dve": v_moe}
                if ci == 0:
                    w = mx(w, {"c_dve": max([v_oev.get((fo4 - 1, tb), 0)
                                             for tb in range(TCH)] + [0])})
                @S.step("pe", waits=w)
                def _(api, fo4=fo4, ci=ci, par=par):
                    for tb in range(TCH):
                        api.matmul(sdp[:, tb, :],
                                   hs_fm[:, ci, tb * 128:(tb + 1) * 128],
                                   wsb2[par][:, :], start=(ci == 0),
                                   stop=(ci == 15)).then_inc(S.h("c_pe"), 1)
                v_sd_free[par] = S.bump("c_pe", TCH)
            v_last = v_sd_free[(fo4 * 16 + 15) % 2]
            for tb in range(TCH):
                @S.step("dve", waits={"c_pe": v_last, "d_sp": v_ost[tb % 2]})
                def _(api, fo4=fo4, tb=tb):
                    api.tensor_tensor(out=outst[tb % 2][:, :],
                                      in0=sdp[:, tb, :],
                                      in1=x_tm[:, tb, fo4 * 512:(fo4 + 1) * 512],
                                      op=ALU.add).then_inc(S.h("c_dve"), 1)
                v_oe = S.bump("c_dve")
                v_oev[(fo4, tb)] = v_oe
                @S.step("sp", waits={"c_dve": v_oe})
                def _(api, fo4=fo4, tb=tb):
                    api.dma_start(out=out_d[tb * 128:(tb + 1) * 128,
                                            fo4 * 512:(fo4 + 1) * 512],
                                  in_=outst[tb % 2][:, :]).then_inc(S.h("d_sp"), 16)
                v_ost[tb % 2] = S.bump("d_sp", 16)

    v_final = S.n("d_sp")
    @S.step("sp", waits={"d_sp": v_final})
    def _(api):
        pass

    with nc.Block() as block:
        @block.tensor
        def _(api):
            for f in S.prog["pe"]:
                f(api)

        @block.scalar
        def _(api):
            for f in S.prog["act"]:
                f(api)

        @block.vector
        def _(api):
            for f in S.prog["dve"]:
                f(api)

        @block.gpsimd
        def _(api):
            for f in S.prog["pool"]:
                f(api)

        @block.sync
        def _(api):
            for f in S.prog["sp"]:
                f(api)

    alloc.close()
    S.sem_stack.close()
    return nc


def _host_inputs(inputs):
    f32 = np.float32
    bf = ml_dtypes.bfloat16
    x = np.asarray(inputs["L0_attn_x"], f32).reshape(NTOK, H)

    def lhsT16(w):
        w = np.asarray(w, f32)
        return np.ascontiguousarray(
            w.reshape(16, 128, 16, 128).transpose(2, 1, 0, 3))

    wq = lhsT16(inputs["L0_attn_Wq"])
    wk = lhsT16(inputs["L0_attn_Wk"])
    wv = lhsT16(inputs["L0_attn_Wv"])
    wsg = lhsT16(inputs["L0_ffn_shared_gate"])
    wsu = lhsT16(inputs["L0_ffn_shared_up"])
    wo = np.ascontiguousarray(
        np.asarray(inputs["L0_attn_Wo"], f32).reshape(16, 128, H))
    wsd = np.ascontiguousarray(
        np.asarray(inputs["L0_ffn_shared_down"], f32).reshape(16, 128, H))

    def bias_t(b):
        return np.ascontiguousarray(np.asarray(b, f32).reshape(16, 128).T)

    bq_t = bias_t(inputs["L0_attn_bq"])
    bk_t = bias_t(inputs["L0_attn_bk"])
    bv_t = bias_t(inputs["L0_attn_bv"])
    bo_r = np.ascontiguousarray(np.asarray(inputs["L0_attn_bo"], f32).reshape(1, H))
    gwt = np.ascontiguousarray(
        np.asarray(inputs["L0_ffn_gate_w"], f32).T.reshape(16, 128, 32)
        .transpose(1, 0, 2))
    gb = np.ascontiguousarray(np.asarray(inputs["L0_ffn_gate_bias"], f32).reshape(32, 1))

    w1_all = np.asarray(inputs["L0_ffn_w1"], f32)
    w3_all = np.asarray(inputs["L0_ffn_w3"], f32)
    w2_all = np.asarray(inputs["L0_ffn_w2"], f32)

    idf = np.eye(128, dtype=f32)
    onesb = np.ones((128, 128), dtype=bf)
    p_i = np.arange(128)
    lstr = (p_i[:, None] < p_i[None, :]).astype(bf)
    onesf = np.ones((1, 128), dtype=f32)
    qmask = np.zeros((16, 128, 32), dtype=f32)
    for ci in range(16):
        qmask[ci, 0:64, 2 * ci] = 1
        qmask[ci, 64:128, 2 * ci + 1] = 1
    qmask = qmask.astype(bf)
    ioe = np.ascontiguousarray(
        np.broadcast_to(np.arange(32, dtype=f32), (128, 32)))
    pidx = np.arange(128, dtype=f32).reshape(128, 1)

    common = dict(wq=wq, wk=wk, wv=wv, wo=wo, bq_t=bq_t, bk_t=bk_t,
                  bv_t=bv_t, bo_r=bo_r, gwt=gwt, gb=gb, wsg=wsg, wsu=wsu,
                  wsd=wsd, idf=idf, onesb=onesb, lstr=lstr, onesf=onesf,
                  qmask=qmask, ioe=ioe, pidx=pidx)

    in_maps = []
    for c in range(NCORES):
        m = dict(common)
        m["x"] = np.ascontiguousarray(x[c * NT:(c + 1) * NT])
        es = slice(c * EPC, (c + 1) * EPC)
        m["w1"] = np.ascontiguousarray(
            w1_all[es].reshape(EPC, 16, 128, 8, 128).transpose(0, 3, 2, 1, 4))
        m["w3"] = np.ascontiguousarray(
            w3_all[es].reshape(EPC, 16, 128, 8, 128).transpose(0, 3, 2, 1, 4))
        m["w2"] = np.ascontiguousarray(w2_all[es].reshape(EPC, 8, 128, H))
        m["pcol"] = (np.arange(128, dtype=f32) + c * NT).reshape(128, 1)
        m["cmask"] = (np.arange(8) < c).astype(f32).reshape(8, 1)
        in_maps.append(m)
    return in_maps


_LAST_RESULTS = None


def kernel(**inputs):
    global _LAST_RESULTS
    in_maps = _host_inputs(inputs)
    nc = build_program(debug=DEBUG)
    res = run_bass_kernel_spmd(nc, in_maps, core_ids=list(range(NCORES)))
    _LAST_RESULTS = res
    out = np.concatenate([np.asarray(res.results[c]["out"], np.float32)
                          for c in range(NCORES)], 0)
    return out.reshape(2, 4096, H)


# version 18 (final)
# speedup vs baseline: 1.1269x; 1.0121x over previous
import math
from contextlib import ExitStack

import numpy as np
import ml_dtypes

import concourse.bass as bass
import concourse.mybir as mybir
from concourse.bass_utils import run_bass_kernel_spmd

F32 = mybir.dt.float32
BF16 = mybir.dt.bfloat16
I32 = mybir.dt.int32
AF = mybir.ActivationFunctionType
ALU = mybir.AluOpType
AX = mybir.AxisListType.X
IOA = bass.IndirectOffsetOnAxis

NCORES = 8
H = 2048
NT = 1024
TCH = 8
NH = 32
DH = 64
SCALE = math.sqrt(32.0)
NKC = 32
EPC = 4
CAP = 512
KTOP = 4
NTOK = 8192
NSLOT = 16384
BIG = 2.0e6
EPS = 1e-5
VPAD = 80

DEBUG = False
ENGS = ("pe", "act", "dve", "pool", "sp")


class KB:
    def __init__(self, nc):
        self.nc = nc
        self.prog = {e: [] for e in ENGS}
        self.sems = {}
        self.cnt = {}
        self.lastw = {e: {} for e in ENGS}
        self.sem_stack = ExitStack()

    def sem(self, *names):
        for name in names:
            self.sems[name] = self.sem_stack.enter_context(self.nc.semaphore(name))
            self.cnt[name] = 0

    def n(self, name):
        return self.cnt[name]

    def bump(self, name, k=1):
        self.cnt[name] += k
        return self.cnt[name]

    def h(self, name):
        return self.sems[name]

    def step(self, eng, waits=None):
        wl = []
        for s, v in (waits or {}).items():
            if v > self.lastw[eng].get(s, 0):
                self.lastw[eng][s] = v
                wl.append((self.sems[s], v))

        def deco(fn):
            def thunk(api, wl=wl, fn=fn):
                for hh, v in wl:
                    api.wait_ge(hh, v)
                fn(api)
            self.prog[eng].append(thunk)
        return deco

    def barrier(self):
        vals = {e: self.bump("b_" + e) for e in ENGS}
        for e in ENGS:
            def f(api, e=e, vals=vals):
                api.sem_inc(self.sems["b_" + e], 1)
                for o in ENGS:
                    if o != e:
                        api.wait_ge(self.sems["b_" + o], vals[o])
            self.prog[e].append(f)
            for o in ENGS:
                self.lastw[e]["b_" + o] = vals[o]


def mx(*ds):
    out = {}
    for d in ds:
        for k, v in d.items():
            out[k] = max(out.get(k, 0), v)
    return out


def build_program(debug=False):
    nc = bass.Bass("TRN2", target_bir_lowering=False)

    def din(name, shape, dt=F32):
        return nc.dram_tensor(name, shape, dt, kind="ExternalInput").ap()

    x_d = din("x", [NT, H])
    wq_d = din("wq", [16, 128, 16, 128])
    wk_d = din("wk", [16, 128, 16, 128])
    wv_d = din("wv", [16, 128, 16, 128])
    wo_d = din("wo", [16, 128, H])
    bq_d = din("bq_t", [128, 16])
    bk_d = din("bk_t", [128, 16])
    bv_d = din("bv_t", [128, 16])
    bo_d = din("bo_r", [1, H])
    gwt_d = din("gwt", [128, 16, 32])
    gb_d = din("gb", [32, 1])
    w1_d = din("w1", [EPC, 8, 128, 16, 128])
    w3_d = din("w3", [EPC, 8, 128, 16, 128])
    w2_d = din("w2", [EPC, 8, 128, H])
    wsg_d = din("wsg", [16, 128, 16, 128])
    wsu_d = din("wsu", [16, 128, 16, 128])
    wsd_d = din("wsd", [16, 128, H])
    idf_d = din("idf", [128, 128])
    onesb_d = din("onesb", [128, 128], BF16)
    lstr_d = din("lstr", [128, 128], BF16)
    onesf_d = din("onesf", [1, 128])
    qmask_d = din("qmask", [16, 128, 32], BF16)
    ioe_d = din("ioe", [128, 32])
    pcol_d = din("pcol", [128, 1])
    pidx_d = din("pidx", [128, 1])
    cmask_d = din("cmask", [8, 1])

    out_d = nc.dram_tensor("out", [NT, H], F32, kind="ExternalOutput").ap()

    def dint(name, shape, dt):
        return nc.dram_tensor(name, shape, dt).ap()

    kvK_in = dint("kvK_in", [H, NT], BF16)
    kvV_in = dint("kvV_in", [H, NT], BF16)
    K_all = dint("K_all", [4 * H, NT], BF16)
    V_all = dint("V_all", [4 * H, NT], BF16)
    kn_in = dint("kn_in", [32, 1], F32)
    kn_all = dint("kn_all", [128, 1], F32)
    xn1_dr = dint("xn1_dr", [NT, H], BF16)
    rec_dr = dint("rec_dr", [NH, NT], F32)
    xn2_in = dint("xn2_in", [NT, H], BF16)
    xlo_dr = dint("xlo_dr", [NT, H], BF16)
    xn2_all = dint("xn2_all", [NTOK + 128, H], BF16)
    tot_in = dint("tot_in", [1, 32], F32)
    tot_all = dint("tot_all", [8, 32], F32)
    map_in = dint("map_in", [NSLOT + 128, 1], F32)
    map_all = dint("map_all", [NSLOT + 128, 1], F32)
    bufdr = dint("bufdr", [EPC * CAP, H], BF16)
    ob_in = dint("ob_in", [EPC * CAP, H], BF16)
    ob_all = dint("ob_all", [NSLOT + 128, H], BF16)

    dbg = {}
    if debug:
        for name, shape, dt in [
            ("d_xn1", [NT, H], BF16), ("d_kall", [4 * H, NT], BF16),
            ("d_vall", [4 * H, NT], BF16), ("d_q", [128, 16 * NT], BF16),
            ("d_negc", [128, 32], F32), ("d_feats", [128, 16 * NT], BF16),
            ("d_x2", [128, TCH * H], F32), ("d_xn2", [NT, H], BF16),
            ("d_sc", [128, TCH * 32], F32), ("d_idx", [128, TCH * 4], F32),
            ("d_wk", [128, TCH * 4], F32), ("d_gidx", [128, TCH * 4], I32),
            ("d_cs", [128, TCH * 32], F32), ("d_offs", [128, 32], F32),
            ("d_map", [NSLOT + 128, 1], F32), ("d_buf", [EPC * CAP, H], BF16),
            ("d_ob", [EPC * CAP, H], BF16), ("d_hs", [128, 16 * NT], BF16),
            ("d_moe", [128, TCH * H], F32),
        ]:
            dbg[name] = nc.dram_tensor(name, shape, dt, kind="ExternalOutput").ap()

    S = KB(nc)
    S.sem(*["b_" + e for e in ENGS])
    S.sem("c_pe", "c_act", "c_dve", "c_pool")
    S.sem("d_sp", "d_aq", "d_pq")
    S.sem("cc_k", "cc_v", "cc_kn", "cc_x2", "cc_tot", "cc_map", "cc_ob")

    alloc = ExitStack()

    def sb(name, shape, dt):
        return alloc.enter_context(nc.sbuf_tensor(name, shape, dt))

    x_tm = sb("x_tm", [128, TCH, H], F32)
    slotA = sb("slotA", [128, 16 * 1024], BF16)
    slotB = sb("slotB", [128, 16 * 1024], BF16)
    slotC = sb("slotC", [128, 22528], BF16)
    slotF = sb("slotF", [128, 6144], F32)
    idf = sb("idf_s", [128, 128], F32)
    onesb = sb("onesb_s", [128, 128], BF16)
    lstr = sb("lstr_s", [128, 128], BF16)
    onesf = sb("onesf_s", [1, 128], F32)
    qmask = sb("qmask_s", [128, 16, 32], BF16)
    ioe = sb("ioe_s", [128, 32], F32)
    pcol = sb("pcol_s", [128, 1], F32)
    pidx = sb("pidx_s", [128, 1], F32)
    cmask = sb("cmask_s", [8, 1], F32)
    bq_t = sb("bq_s", [128, 16], F32)
    bk_t = sb("bk_s", [128, 16], F32)
    bv_t = sb("bv_s", [128, 16], F32)
    bqs = sb("bqs_s", [128, 16], F32)
    gwt_bf = sb("gwtbf_s", [128, 16, 32], BF16)
    glo_bf = sb("globf_s", [128, 16, 32], BF16)
    gb = sb("gb_s", [32, 1], F32)
    rev32 = sb("rev32_s", [128, 32], F32)
    negc_sb = sb("negc_s", [128, 32], F32)
    qmax = sb("qmax_s", [32, 1], F32)
    kmax = sb("kmax_s", [32, 1], F32)
    knal = sb("knal_s", [32, 4], F32)
    negcT = sb("negcT_s", [1, 32], F32)
    idxf = sb("idxf_s", [128, TCH, 4], F32)
    t2v = sb("t2_s", [128, TCH, 4], F32)
    wkv = sb("wk_s", [128, TCH, 4], F32)
    gidx = sb("gidx_s", [128, TCH, 4], I32)
    csv = sb("cs_s", [128, TCH, 32], F32)
    anv = sb("an_s", [128, TCH, 32], BF16)
    offsb = sb("offs_s", [128, 32], F32)
    totsb = sb("tot_s", [1, 32], F32)
    tota = sb("tota_s", [8, 32], F32)
    offs1 = sb("offs1_s", [1, 32], F32)
    eps_t = sb("eps_s", [128, 1], F32)
    trash_t = sb("trash_s", [128, 1], F32)
    zrow_t = sb("zrow_s", [128, 1], F32)
    mapt = sb("mapt_s", [128, 1], F32)
    mapi = sb("mapi_s", [128, 1], I32)
    mrow = sb("mrow_s", [128, 1], F32)
    mrow_i = sb("mrowi_s", [128, 1], I32)

    sco_t = slotF[:, 0:256].rearrange("p (a b) -> p a b", b=32)
    scb_t = slotF[:, 256:512].rearrange("p (a b) -> p a b", b=32)
    r_gs = slotF[:, 512:516]
    r_m1 = slotF[:, 516:517]
    r_m2 = slotF[:, 517:518]
    r_cnt = slotF[:, 518:519]
    r_w = slotF[:, 519:520]
    r_d = slotF[:, 520:521]
    r_th = slotF[:, 521:522]
    r_eq = slotF[:, 524:532]
    r_t8 = slotF[:, 532:540]
    r_wraw = slotF[:, 540:544]
    r_ws = slotF[:, 544:545]
    r_cat = slotF[:, 545:546]
    r_oat = slotF[:, 546:547]
    r_pos = slotF[:, 547:548]
    r_kp = slotF[:, 548:549]
    r_g0 = slotF[:, 549:550]
    r_adj = slotF[:, 550:551]
    r_tok = slotF[:, 552:560]
    r_work = slotF[:, 576:608]
    r_work2 = slotF[:, 608:640]
    r_r1 = slotF[:, 640:672]
    r_a = slotF[:, 672:704]
    r_oh = slotF[:, 704:832].rearrange("p (a b) -> p a b", b=32)
    gidx_f = slotF[:, 832:864].rearrange("p (a b) -> p a b", b=4)
    sco_f = slotF[0:32, 1024:2048]
    scb_f = slotF[0:32, 2048:3072]
    bn_t = slotF[:, 3104:3128].rearrange("p (a b) -> p a b", b=6)
    gwt_f = slotF[:, 512:1024].rearrange("p (a b) -> p a b", b=32)
    mv = slotF[:, 3128:3130]
    sd1 = slotF[:, 3130:3131]
    rstd = slotF[:, 3131:3132]
    acc = [slotF[:, 1024:3072], slotF[:, 3072:5120]]
    outst = [slotF[:, 5120:5632], slotF[:, 5632:6144]]

    @S.step("sp")
    def _(api):
        for dst, src in [
            (x_tm[:, :, :],
             x_d.rearrange("(a p) f -> p a f", p=128)[:, :, :]),
            (idf[:, :], idf_d[:, :]), (onesb[:, :], onesb_d[:, :]),
            (lstr[:, :], lstr_d[:, :]), (onesf[:, :], onesf_d[:, :]),
            (qmask[:, :, :],
             qmask_d.rearrange("a p e -> p a e")[:, :, :]),
            (ioe[:, :], ioe_d[:, :]), (pcol[:, :], pcol_d[:, :]),
            (pidx[:, :], pidx_d[:, :]), (cmask[:, :], cmask_d[:, :]),
            (bq_t[:, :], bq_d[:, :]), (bk_t[:, :], bk_d[:, :]),
            (bv_t[:, :], bv_d[:, :]), (gb[:, :], gb_d[:, :]),
            (gwt_f[:, :, :], gwt_d[:, :, :]),
        ]:
            api.dma_start(out=dst, in_=src).then_inc(S.h("d_sp"), 16)
    n_setup = S.bump("d_sp", 16 * 15)

    @S.step("pool")
    def _(api):
        api.engine_nop()
    n_setup_pq = S.bump("d_pq", 0)

    @S.step("dve", waits={"d_sp": n_setup})
    def _(api):
        api.tensor_scalar(out=bqs[:, :], in0=bq_t[:, :], scalar1=1.0 / SCALE,
                          scalar2=None, op0=ALU.mult).then_inc(S.h("c_dve"), 1)
        api.tensor_scalar(out=rev32[:, :], in0=ioe[:, :], scalar1=-1.0,
                          scalar2=32.0, op0=ALU.mult, op1=ALU.add).then_inc(S.h("c_dve"), 1)
        api.memset(eps_t[:, :], EPS).then_inc(S.h("c_dve"), 1)
        api.tensor_scalar(out=trash_t[:, :], in0=pidx[:, :],
                          scalar1=float(NSLOT), scalar2=None,
                          op0=ALU.add).then_inc(S.h("c_dve"), 1)
        api.tensor_scalar(out=zrow_t[:, :], in0=pidx[:, :],
                          scalar1=float(NTOK + 1), scalar2=None,
                          op0=ALU.add).then_inc(S.h("c_dve"), 1)
        api.memset(slotC[:, 0:2048], 0.0).then_inc(S.h("c_dve"), 1)
        api.tensor_copy(out=gwt_bf.rearrange("p a e -> p (a e)")[:, :],
                        in_=gwt_f.rearrange("p a e -> p (a e)")[:, :]).then_inc(S.h("c_dve"), 1)
        api.memset(t2v.rearrange("p a k -> p (a k)")[:, :], 0.0).then_inc(S.h("c_dve"), 1)
    m_setup_dve = S.bump("c_dve", 8)
    @S.step("dve", waits={"c_dve": m_setup_dve})
    def _(api):
        api.tensor_tensor(out=glo_bf.rearrange("p a e -> p (a e)")[:, :],
                          in0=gwt_f.rearrange("p a e -> p (a e)")[:, :],
                          in1=gwt_bf.rearrange("p a e -> p (a e)")[:, :],
                          op=ALU.subtract).then_inc(S.h("c_dve"), 1)
    m_setup_dve = S.bump("c_dve")

    @S.step("pool", waits={"c_dve": m_setup_dve})
    def _(api):
        mz = map_in.rearrange("(a b) o -> a (b o)", a=128)
        api.dma_start(out=mz[:, 0:128], in_=slotC[:, 0:128]).then_inc(S.h("d_pq"), 16)
        api.dma_start(out=mz[:, 128:129], in_=slotC[:, 0:1], allow_slow_non_contiguous=True).then_inc(S.h("d_pq"), 16)
        api.dma_start(out=xn2_all[NTOK:NTOK + 128, :],
                      in_=slotC[:, 0:2048]).then_inc(S.h("d_pq"), 16)
        api.dma_start(out=ob_all[NSLOT:NSLOT + 128, :],
                      in_=slotC[:, 0:2048]).then_inc(S.h("d_pq"), 16)
    n_zero_pq = S.bump("d_pq", 64)

    def ph_layernorm(dram_out, dest_fm, first_wait, lo_dram=None, lo_fm=None):
        xnc = [slotC[:, 18432:20480], slotC[:, 20480:22528]]
        xloc = [slotC[:, 0:2048], slotC[:, 2048:4096]]
        xf32 = [slotF[:, 1024:3072], slotF[:, 3072:5120]]
        sp_st = {}
        v_sqrt = 0
        for tc in range(TCH):
            par = tc % 2
            w = dict(first_wait) if tc == 0 else {}
            w["c_act"] = max(w.get("c_act", 0), v_sqrt)
            w["c_dve"] = max(w.get("c_dve", 0), m_setup_dve)
            @S.step("dve", waits=w)
            def _(api, tc=tc):
                for j in range(4):
                    api.bn_stats(out=bn_t[:, j, :],
                                 in_=x_tm[:, tc, j * 512:(j + 1) * 512]).then_inc(S.h("c_dve"), 1)
            v_bn = S.bump("c_dve", 4)
            @S.step("dve", waits={"c_dve": v_bn})
            def _(api):
                api.bn_aggr(out=mv[:, :],
                            in_=slotF[:, 3104:3128]).then_inc(S.h("c_dve"), 1)
            v_ag = S.bump("c_dve")
            @S.step("act", waits={"c_dve": v_ag})
            def _(api):
                api.activation(out=sd1[:, :], in_=mv[:, 1:2], func=AF.Sqrt,
                               bias=eps_t[:, 0:1], scale=1.0).then_inc(S.h("c_act"), 1)
            v_sqrt = S.bump("c_act")
            @S.step("dve", waits={"c_act": v_sqrt})
            def _(api):
                api.reciprocal(out=rstd[:, :], in_=sd1[:, :]).then_inc(S.h("c_dve"), 1)
            v_rs = S.bump("c_dve")
            if lo_dram is None:
                @S.step("dve", waits=mx({"c_dve": v_rs},
                                        {"d_sp": sp_st.get(tc - 2, 0)}))
                def _(api, tc=tc, par=par):
                    api.tensor_scalar(out=xnc[par][:, :], in0=x_tm[:, tc, :],
                                      scalar1=mv[:, 0:1], scalar2=rstd[:, 0:1],
                                      op0=ALU.subtract, op1=ALU.mult).then_inc(S.h("c_dve"), 1)
                v_nm = S.bump("c_dve")
            else:
                @S.step("dve", waits=mx({"c_dve": v_rs},
                                        {"d_sp": sp_st.get(tc - 2, 0)}))
                def _(api, tc=tc, par=par):
                    api.tensor_scalar(out=xf32[par][:, :], in0=x_tm[:, tc, :],
                                      scalar1=mv[:, 0:1], scalar2=rstd[:, 0:1],
                                      op0=ALU.subtract, op1=ALU.mult).then_inc(S.h("c_dve"), 1)
                v_f = S.bump("c_dve")
                @S.step("dve", waits={"c_dve": v_f})
                def _(api, par=par):
                    api.tensor_copy(out=xnc[par][:, :],
                                    in_=xf32[par][:, :]).then_inc(S.h("c_dve"), 1)
                v_hi = S.bump("c_dve")
                @S.step("dve", waits={"c_dve": v_hi})
                def _(api, par=par):
                    api.tensor_tensor(out=xloc[par][:, :], in0=xf32[par][:, :],
                                      in1=xnc[par][:, :],
                                      op=ALU.subtract).then_inc(S.h("c_dve"), 1)
                v_nm = S.bump("c_dve")
            @S.step("sp", waits={"c_dve": v_nm})
            def _(api, tc=tc, par=par):
                api.dma_start(out=dram_out[tc * 128:(tc + 1) * 128, :],
                              in_=xnc[par][:, :]).then_inc(S.h("d_sp"), 16)
                if lo_dram is not None:
                    api.dma_start(out=lo_dram[tc * 128:(tc + 1) * 128, :],
                                  in_=xloc[par][:, :]).then_inc(S.h("d_sp"), 16)
            v_st = S.bump("d_sp", 16 if lo_dram is None else 32)
            sp_st[tc] = v_st
            @S.step("act", waits={"d_sp": v_st})
            def _(api, tc=tc):
                api.dma_start(out=dest_fm[:, :, tc * 128:(tc + 1) * 128],
                              in_=dram_out[tc * 128:(tc + 1) * 128, :],
                              transpose=True).then_inc(S.h("d_aq"), 16)
                if lo_fm is not None:
                    api.dma_start(out=lo_fm[:, :, tc * 128:(tc + 1) * 128],
                                  in_=lo_dram[tc * 128:(tc + 1) * 128, :],
                                  transpose=True).then_inc(S.h("d_aq"), 16)
            S.bump("d_aq", 16 if lo_fm is None else 32)
        return S.n("d_aq"), S.n("d_sp")

    xn_fm = slotA.rearrange("p (a t) -> p a t", t=1024)
    aq_xn1, sp_xn1 = ph_layernorm(xn1_dr, xn_fm, {"d_sp": n_setup})
    if debug:
        @S.step("sp", waits={"d_sp": sp_xn1})
        def _(api):
            api.dma_start(out=dbg["d_xn1"][:, :], in_=xn1_dr[:, :]).then_inc(S.h("d_sp"), 16)
        S.bump("d_sp", 16)

    S.barrier()

    q_fm = slotB.rearrange("p (a t) -> p a t", t=1024)
    wbuf = [slotC[:, 0:2048].rearrange("p (a j) -> p a j", j=128),
            slotC[:, 2048:4096].rearrange("p (a j) -> p a j", j=128)]
    kst = [slotC[:, 4096:5120], slotC[:, 5120:6144]]
    sqt = [slotC[:, 6144:7168], slotC[:, 7168:8192]]

    with nc.psum_tensor("pg", [128, 2, 1024], F32) as pgt, \
         nc.psum_tensor("qn", [32, 1024], F32) as qn_ps, \
         nc.psum_tensor("kn", [32, 1024], F32) as kn_ps:

        v_wb_free = {0: 0, 1: 0}
        v_ps_free = {0: 0, 1: 0}
        v_sq_free = {0: 0, 1: 0}
        v_kst_free = {0: 0, 1: 0}
        mats = [("q", wq_d, None, qn_ps), ("k", wk_d, kvK_in, kn_ps),
                ("v", wv_d, kvV_in, None)]
        for mi, (mn, wd, st_dram, nrm_ps) in enumerate(mats):
            for co in range(16):
                it = mi * 16 + co
                par = it % 2
                @S.step("pool", waits={"c_pe": v_wb_free[par],
                                       "d_pq": n_zero_pq})
                def _(api, wd=wd, co=co, par=par):
                    api.dma_start(out=wbuf[par].rearrange("p a j -> p (a j)")[:, :],
                                  in_=wd[co].rearrange("p a j -> p (a j)")[:, :]
                                  ).then_inc(S.h("d_pq"), 16)
                v_w = S.bump("d_pq", 16)
                @S.step("pe", waits={"d_pq": v_w, "d_aq": aq_xn1,
                                     "c_act": v_ps_free[par]})
                def _(api, par=par):
                    for ci in range(16):
                        for hf in range(2):
                            api.matmul(pgt[:, par, hf * 512:(hf + 1) * 512],
                                       wbuf[par][:, ci, :],
                                       xn_fm[:, ci, hf * 512:(hf + 1) * 512],
                                       start=(ci == 0), stop=(ci == 15)
                                       ).then_inc(S.h("c_pe"), 1)
                v_mm = S.bump("c_pe", 32)
                v_wb_free[par] = v_mm
                if mn == "q":
                    @S.step("act", waits={"c_pe": v_mm})
                    def _(api, co=co, par=par):
                        api.activation(out=q_fm[:, co, :], in_=pgt[:, par, :],
                                       func=AF.Identity, bias=bqs[:, co:co + 1],
                                       scale=1.0 / SCALE).then_inc(S.h("c_act"), 1)
                    v_ev = S.bump("c_act")
                    v_ps_free[par] = v_ev
                    @S.step("act", waits=mx({"c_act": v_ev},
                                            {"c_pe": v_sq_free[par]}))
                    def _(api, co=co, par=par):
                        api.activation(out=sqt[par][:, :], in_=q_fm[:, co, :],
                                       func=AF.Square).then_inc(S.h("c_act"), 1)
                    v_sq = S.bump("c_act")
                    @S.step("pe", waits={"c_act": v_sq})
                    def _(api, co=co, par=par, nrm_ps=nrm_ps):
                        sq = sqt[par].rearrange("p (a b) -> p a b", b=512)
                        for hf in range(2):
                            api.matmul(nrm_ps[:, hf * 512:(hf + 1) * 512],
                                       qmask[:, co, :], sq[:, hf, :],
                                       start=(co == 0), stop=(co == 15)
                                       ).then_inc(S.h("c_pe"), 1)
                    v_sq_free[par] = S.bump("c_pe", 2)
                else:
                    kpar = co % 2
                    @S.step("act", waits=mx({"c_pe": v_mm},
                                            {"d_sp": v_kst_free[kpar]}))
                    def _(api, co=co, par=par, kpar=kpar,
                          bias=(bk_t if mn == "k" else bv_t)):
                        api.activation(out=kst[kpar][:, :], in_=pgt[:, par, :],
                                       func=AF.Identity, bias=bias[:, co:co + 1],
                                       scale=1.0).then_inc(S.h("c_act"), 1)
                    v_ev = S.bump("c_act")
                    v_ps_free[par] = v_ev
                    @S.step("sp", waits={"c_act": v_ev})
                    def _(api, co=co, kpar=kpar, st_dram=st_dram):
                        api.dma_start(out=st_dram[co * 128:(co + 1) * 128, :],
                                      in_=kst[kpar][:, :]).then_inc(S.h("d_sp"), 16)
                    v_kst_free[kpar] = S.bump("d_sp", 16)
                    if mn == "k":
                        @S.step("act", waits=mx({"c_act": v_ev},
                                                {"c_pe": v_sq_free[par]}))
                        def _(api, kpar=kpar, par=par):
                            api.activation(out=sqt[par][:, :], in_=kst[kpar][:, :],
                                           func=AF.Square).then_inc(S.h("c_act"), 1)
                        v_sq = S.bump("c_act")
                        @S.step("pe", waits={"c_act": v_sq})
                        def _(api, co=co, par=par, nrm_ps=nrm_ps):
                            sq = sqt[par].rearrange("p (a b) -> p a b", b=512)
                            for hf in range(2):
                                api.matmul(nrm_ps[:, hf * 512:(hf + 1) * 512],
                                           qmask[:, co, :], sq[:, hf, :],
                                           start=(co == 0), stop=(co == 15)
                                           ).then_inc(S.h("c_pe"), 1)
                        v_sq_free[par] = S.bump("c_pe", 2)
        sp_kv = S.n("d_sp")
        pe_p2 = S.n("c_pe")

        @S.step("pool", waits={"d_sp": sp_kv})
        def _(api):
            api.collective_compute(
                "AllGather", ALU.bypass,
                replica_groups=[[0, 1, 2, 3], [4, 5, 6, 7]],
                ins=[kvK_in[:, :]], outs=[K_all[:, :]]).then_inc(S.h("cc_k"), 1)
            api.collective_compute(
                "AllGather", ALU.bypass,
                replica_groups=[[0, 1, 2, 3], [4, 5, 6, 7]],
                ins=[kvV_in[:, :]], outs=[V_all[:, :]]).then_inc(S.h("cc_v"), 1)
        S.bump("cc_k")
        S.bump("cc_v")

        @S.step("dve", waits={"c_pe": pe_p2})
        def _(api):
            api.tensor_reduce(out=qmax[:, :], in_=qn_ps[:, :], axis=AX,
                              op=ALU.max).then_inc(S.h("c_dve"), 1)
            api.tensor_reduce(out=kmax[:, :], in_=kn_ps[:, :], axis=AX,
                              op=ALU.max).then_inc(S.h("c_dve"), 1)
        v_nr = S.bump("c_dve", 2)
        @S.step("act", waits={"c_dve": v_nr})
        def _(api):
            api.activation(out=qmax[:, :], in_=qmax[:, :],
                           func=AF.Sqrt).then_inc(S.h("c_act"), 1)
            api.activation(out=kmax[:, :], in_=kmax[:, :],
                           func=AF.Sqrt).then_inc(S.h("c_act"), 1)
        v_sqn = S.bump("c_act", 2)
        @S.step("sp", waits={"c_act": v_sqn})
        def _(api):
            api.dma_start(out=kn_in[:, :], in_=kmax[:, :]).then_inc(S.h("d_sp"), 16)
        v_knst = S.bump("d_sp", 16)
        @S.step("pool", waits={"d_sp": v_knst})
        def _(api):
            api.collective_compute(
                "AllGather", ALU.bypass,
                replica_groups=[[0, 1, 2, 3], [4, 5, 6, 7]],
                ins=[kn_in[:, :]], outs=[kn_all[:, :]]).then_inc(S.h("cc_kn"), 1)
        S.bump("cc_kn")
        @S.step("pool", waits={"cc_kn": 1})
        def _(api):
            api.dma_start(out=knal[:, :],
                          in_=kn_all.rearrange("(r h) o -> h (r o)", h=32)[:, :],
                          allow_slow_non_contiguous=True).then_inc(S.h("d_pq"), 16)
        v_knl = S.bump("d_pq", 16)
        @S.step("dve", waits={"d_pq": v_knl, "c_act": v_sqn})
        def _(api):
            api.tensor_reduce(out=kmax[:, :], in_=knal[:, :], axis=AX,
                              op=ALU.max).then_inc(S.h("c_dve"), 1)
        v_km = S.bump("c_dve")
        @S.step("dve", waits={"c_dve": v_km})
        def _(api):
            api.tensor_tensor(out=qmax[:, :], in0=qmax[:, :], in1=kmax[:, :],
                              op=ALU.mult).then_inc(S.h("c_dve"), 1)
        v_ch = S.bump("c_dve")
        @S.step("dve", waits={"c_dve": v_ch})
        def _(api):
            api.tensor_scalar(out=qmax[:, :], in0=qmax[:, :], scalar1=1.0,
                              scalar2=-1.0, op0=ALU.add,
                              op1=ALU.mult).then_inc(S.h("c_dve"), 1)
        v_negc = S.bump("c_dve")

    with nc.psum_tensor("ngT", [1, 32], F32) as ngT, \
         nc.psum_tensor("ngB", [128, 32], F32) as ngB:
        @S.step("pe", waits={"c_dve": v_negc})
        def _(api):
            api.transpose(ngT[:, :], qmax[:, :], idf[0:32, 0:32]).then_inc(S.h("c_pe"), 1)
        v_t = S.bump("c_pe")
        @S.step("dve", waits={"c_pe": v_t})
        def _(api):
            api.tensor_copy(out=negcT[:, :], in_=ngT[:, :]).then_inc(S.h("c_dve"), 1)
        v_tc2 = S.bump("c_dve")
        @S.step("pe", waits={"c_dve": v_tc2})
        def _(api):
            for hh in range(NH):
                api.matmul(ngB[:, hh:hh + 1], onesf[:, :], negcT[:, hh:hh + 1],
                           start=True, stop=True).then_inc(S.h("c_pe"), 1)
        v_bc = S.bump("c_pe", NH)
        @S.step("dve", waits={"c_pe": v_bc})
        def _(api):
            api.tensor_copy(out=negc_sb[:, :], in_=ngB[:, :]).then_inc(S.h("c_dve"), 1)
        v_negcd = S.bump("c_dve")

    if debug:
        @S.step("sp", waits={"c_dve": v_negcd, "cc_k": 1, "cc_v": 1})
        def _(api):
            api.dma_start(out=dbg["d_negc"][:, :], in_=negc_sb[:, :]).then_inc(S.h("d_sp"), 16)
            api.dma_start(out=dbg["d_q"][:, :],
                          in_=q_fm.rearrange("p a t -> p (a t)")[:, :]).then_inc(S.h("d_sp"), 16)
            api.dma_start(out=dbg["d_kall"][:, :], in_=K_all[:, :]).then_inc(S.h("d_sp"), 16)
            api.dma_start(out=dbg["d_vall"][:, :], in_=V_all[:, :]).then_inc(S.h("d_sp"), 16)
        S.bump("d_sp", 64)

    S.barrier()

    feats_fm = slotA.rearrange("p (a t) -> p a t", t=1024)
    kp_t = [slotC[:, 0:4096], slotC[:, 4096:8192]]
    vt_t = [slotC[:, 8192 + i * 2560: 8192 + (i + 1) * 2560]
            .rearrange("p (a b) -> p a b", b=VPAD) for i in range(4)]
    pt_t = [slotC[:, 18432:19456], slotC[:, 19456:20480],
            slotC[:, 20480:21504]]
    recb_t = slotF[:, 0:1024]

    v_p2_pe = S.n("c_pe")
    with nc.psum_tensor("sS", [128, 3, 1024], F32) as sS, \
         nc.psum_tensor("fts", [65, 1024], F32) as fts:

        @S.step("dve", waits={"c_pe": v_p2_pe})
        def _(api):
            for i in range(4):
                api.memset(vt_t[i][:, :, 64:65], 1.0).then_inc(S.h("c_dve"), 1)
        v_ones = S.bump("c_dve", 4)

        kp_ready = [0, 0]
        vt_ready = [0, 0, 0, 0]
        kp_free = [0, 0]
        vt_free = [0, 0, 0, 0]
        s_free = {0: 0, 1: 0, 2: 0}
        pt_free = {0: 0, 1: 0, 2: 0}
        fin = {"act": 0, "dve": 0, "st": 0}
        rec_st = {}

        def load_pair(pr):
            slot = pr % 2
            @S.step("sp", waits={"cc_k": 1,
                                 "c_pe": max(kp_free[slot], v_p2_pe)})
            def _(api, pr=pr, slot=slot):
                kv = kp_t[slot].rearrange("p (r t) -> p r t", r=4)
                for r in range(4):
                    api.dma_start(
                        out=kv[:, r, :],
                        in_=K_all[r * H + pr * 128: r * H + (pr + 1) * 128, :]
                    ).then_inc(S.h("d_sp"), 16)
            kp_ready[slot] = S.bump("d_sp", 64)
            for hl in range(2):
                hh = 2 * pr + hl
                vslot = hh % 4
                @S.step("act", waits={"cc_v": 1, "c_dve": v_ones,
                                      "c_pe": max(vt_free[vslot], v_p2_pe)})
                def _(api, hh=hh, vslot=vslot):
                    for r in range(4):
                        api.dma_start(
                            out=vt_t[vslot][:, r * 8:(r + 1) * 8, 0:64],
                            in_=V_all[r * H + hh * 64: r * H + (hh + 1) * 64, :],
                            transpose=True).then_inc(S.h("d_aq"), 16)
                vt_ready[vslot] = S.bump("d_aq", 64)

        load_pair(0)
        load_pair(1)

        for hh in range(NH):
            pr = hh // 2
            hbase = (hh % 2) * 64
            hchunk = hh // 2
            kslot = pr % 2
            vslot = hh % 4
            kpr = kp_ready[kslot]
            vtr = vt_ready[vslot]

            def qk(kc):
                par = kc % 3
                @S.step("pe", waits=mx({"d_sp": kpr, "c_act": s_free[par]},
                                       {"c_dve": v_negcd}))
                def _(api, kc=kc, par=par, hbase=hbase, hchunk=hchunk,
                      kslot=kslot):
                    kv = kp_t[kslot].rearrange("p (r t) -> p r t", r=4)
                    r, lt = kc // 8, kc % 8
                    lhsT = kv[hbase:hbase + 64, r, lt * 128:(lt + 1) * 128]
                    for hf in range(2):
                        api.matmul(sS[:, par, hf * 512:(hf + 1) * 512], lhsT,
                                   q_fm[hbase:hbase + 64, hchunk,
                                        hf * 512:(hf + 1) * 512],
                                   start=True, stop=True).then_inc(S.h("c_pe"), 1)
                return S.bump("c_pe", 2)

            def ex(kc, v_qk):
                par = kc % 3
                @S.step("act", waits={"c_pe": max(v_qk, pt_free[par])})
                def _(api, kc=kc, par=par, hh=hh):
                    api.activation(out=pt_t[par][:, :], in_=sS[:, par, :],
                                   func=AF.Exp, bias=negc_sb[:, hh:hh + 1],
                                   scale=1.0).then_inc(S.h("c_act"), 1)
                return S.bump("c_act")

            def av(kc, v_ex):
                par = kc % 3
                w = {"c_act": v_ex, "d_aq": vtr}
                if kc == 0:
                    w = mx(w, {"c_act": fin["act"], "c_dve": fin["dve"]})
                @S.step("pe", waits=w)
                def _(api, kc=kc, par=par, vslot=vslot):
                    lhsT = vt_t[vslot][:, kc, 0:65]
                    for hf in range(2):
                        api.matmul(fts[:, hf * 512:(hf + 1) * 512], lhsT,
                                   pt_t[par][:, hf * 512:(hf + 1) * 512],
                                   start=(kc == 0), stop=(kc == NKC - 1)
                                   ).then_inc(S.h("c_pe"), 1)
                return S.bump("c_pe", 2)

            vq = {0: qk(0), 1: qk(1), 2: qk(2)}
            for kc in range(NKC):
                vex = ex(kc, vq[kc])
                s_free[kc % 3] = vex
                vav = av(kc, vex)
                pt_free[kc % 3] = vav
                if kc + 3 < NKC:
                    vq[kc + 3] = qk(kc + 3)
            v_avl = pt_free[(NKC - 1) % 3]
            kp_free[kslot] = max(kp_free[kslot], v_avl)
            vt_free[vslot] = v_avl

            if hh % 2 == 1 and pr + 2 <= 15:
                load_pair(pr + 2)

            @S.step("act", waits={"c_pe": v_avl})
            def _(api, hh=hh, hbase=hbase, hchunk=hchunk):
                api.activation(out=feats_fm[hbase:hbase + 64, hchunk, :],
                               in_=fts[0:64, :],
                               func=AF.Copy).then_inc(S.h("c_act"), 1)
            fin["act"] = S.bump("c_act")
            @S.step("dve", waits={"c_pe": v_avl, "d_sp": fin["st"]})
            def _(api):
                api.reciprocal(out=recb_t[64:65, :],
                               in_=fts[64:65, :]).then_inc(S.h("c_dve"), 1)
            fin["dve"] = S.bump("c_dve")
            @S.step("sp", waits={"c_dve": fin["dve"]})
            def _(api, hh=hh):
                api.dma_start(out=rec_dr[hh:hh + 1, :],
                              in_=recb_t[64:65, :]).then_inc(S.h("d_sp"), 16)
            fin["st"] = S.bump("d_sp", 16)
            rec_st[hh] = fin["st"]

        v_fin_act = fin["act"]
        v_rec_all = fin["st"]

    rec1 = slotF[0:1, 1024:2048]
    with nc.psum_tensor("rb2", [128, 2, 1024], F32) as rb2:
        v_rl = 0
        v_nm2 = {0: 0, 1: 0}
        v_rb2 = {0: 0, 1: 0}
        v_rb_last = 0
        for hh in range(NH):
            par = hh % 2
            hbase = (hh % 2) * 64
            hchunk = hh // 2
            @S.step("act", waits={"d_sp": rec_st[hh], "d_aq": v_rl,
                                  "c_pe": v_rb_last})
            def _(api, hh=hh):
                api.dma_start(out=rec1[:, :], in_=rec_dr[hh:hh + 1, :]
                              ).then_inc(S.h("d_aq"), 16)
            v_rl = S.bump("d_aq", 16)
            @S.step("pe", waits={"d_aq": v_rl, "c_dve": v_nm2[par]})
            def _(api, par=par):
                for hf in range(2):
                    api.matmul(rb2[:, par, hf * 512:(hf + 1) * 512],
                               onesf[0:1, 0:128],
                               rec1[0:1, hf * 512:(hf + 1) * 512],
                               start=True, stop=True).then_inc(S.h("c_pe"), 1)
            v_rb2[par] = S.bump("c_pe", 2)
            v_rb_last = v_rb2[par]
            @S.step("dve", waits={"c_pe": v_rb2[par], "c_act": v_fin_act})
            def _(api, hh=hh, par=par, hbase=hbase, hchunk=hchunk):
                api.tensor_tensor(out=feats_fm[hbase:hbase + 64, hchunk, :],
                                  in0=feats_fm[hbase:hbase + 64, hchunk, :],
                                  in1=rb2[hbase:hbase + 64, par, :],
                                  op=ALU.mult).then_inc(S.h("c_dve"), 1)
            v_nm2[par] = S.bump("c_dve")
    v_attn = S.n("c_dve")
    if debug:
        @S.step("sp", waits={"c_dve": v_attn})
        def _(api):
            api.dma_start(out=dbg["d_feats"][:, :],
                          in_=feats_fm.rearrange("p a t -> p (a t)")[:, :]).then_inc(S.h("d_sp"), 16)
        S.bump("d_sp", 16)

    S.barrier()

    wob = [slotC[:, 0:512], slotC[:, 512:1024]]
    bo_bf = slotC[0:1, 1024:3072]
    @S.step("pool")
    def _(api):
        api.dma_start(out=bo_bf, in_=bo_d[:, :]).then_inc(S.h("d_pq"), 16)
    n_bo_pq = S.bump("d_pq", 16)
    with nc.psum_tensor("po", [128, TCH, 512], F32) as po:
        v_wo_free = {0: 0, 1: 0}
        v_ev = {}
        for fo4 in range(4):
            for ci in range(16):
                it = fo4 * 16 + ci
                par = it % 2
                @S.step("pool", waits={"c_pe": v_wo_free[par]})
                def _(api, fo4=fo4, ci=ci, par=par):
                    api.dma_start(out=wob[par][:, :],
                                  in_=wo_d[ci, :, fo4 * 512:(fo4 + 1) * 512]
                                  ).then_inc(S.h("d_pq"), 16)
                v_l = S.bump("d_pq", 16)
                w = {"d_pq": v_l, "c_dve": v_attn}
                if ci == 0:
                    w = mx(w, {"c_dve": max([v_ev.get((fo4 - 1, tb), 0)
                                             for tb in range(TCH)] + [0])})
                @S.step("pe", waits=w)
                def _(api, fo4=fo4, ci=ci, par=par):
                    for tb in range(TCH):
                        api.matmul(po[:, tb, :],
                                   feats_fm[:, ci, tb * 128:(tb + 1) * 128],
                                   wob[par][:, :], start=(ci == 0), stop=False
                                   ).then_inc(S.h("c_pe"), 1)
                v_wo_free[par] = S.bump("c_pe", TCH)
            @S.step("pe", waits={"d_pq": n_bo_pq})
            def _(api, fo4=fo4):
                for tb in range(TCH):
                    api.matmul(po[:, tb, :], onesb[0:1, :],
                               bo_bf[:, fo4 * 512:(fo4 + 1) * 512],
                               start=False, stop=True).then_inc(S.h("c_pe"), 1)
            v_bo = S.bump("c_pe", TCH)
            for tb in range(TCH):
                @S.step("dve", waits={"c_pe": v_bo})
                def _(api, fo4=fo4, tb=tb):
                    api.tensor_tensor(out=x_tm[:, tb, fo4 * 512:(fo4 + 1) * 512],
                                      in0=po[:, tb, :],
                                      in1=x_tm[:, tb, fo4 * 512:(fo4 + 1) * 512],
                                      op=ALU.add).then_inc(S.h("c_dve"), 1)
                v_ev[(fo4, tb)] = S.bump("c_dve")
    v_x2 = S.n("c_dve")

    if debug:
        @S.step("sp", waits={"c_dve": v_x2})
        def _(api):
            api.dma_start(out=dbg["d_x2"][:, :],
                          in_=x_tm.rearrange("p a f -> p (a f)")[:, :]).then_inc(S.h("d_sp"), 16)
        S.bump("d_sp", 16)

    S.barrier()

    xn2_fm = slotA.rearrange("p (a t) -> p a t", t=1024)
    xlo_fm = slotB.rearrange("p (a t) -> p a t", t=1024)
    aq_xn2, sp_xn2 = ph_layernorm(xn2_in, xn2_fm, {"c_dve": v_x2},
                                  lo_dram=xlo_dr, lo_fm=xlo_fm)
    @S.step("pool", waits={"d_sp": sp_xn2})
    def _(api):
        api.collective_compute(
            "AllGather", ALU.bypass, replica_groups=[list(range(8))],
            ins=[xn2_in[:, :]], outs=[xn2_all[0:NTOK, :]]).then_inc(S.h("cc_x2"), 1)
    S.bump("cc_x2")
    if debug:
        @S.step("sp", waits={"d_sp": sp_xn2})
        def _(api):
            api.dma_start(out=dbg["d_xn2"][:, :], in_=xn2_in[:, :]).then_inc(S.h("d_sp"), 16)
        S.bump("d_sp", 16)

    with nc.psum_tensor("lg", [32, 1024], F32) as lg, \
         nc.psum_tensor("trp", [128, 256], F32) as trp:
        @S.step("pe", waits={"d_aq": aq_xn2, "c_dve": m_setup_dve})
        def _(api):
            terms = [(gwt_bf, xn2_fm), (glo_bf, xn2_fm), (gwt_bf, xlo_fm),
                     (glo_bf, xlo_fm)]
            for ti, (wv, xv) in enumerate(terms):
                for ci in range(16):
                    for hf in range(2):
                        api.matmul(lg[:, hf * 512:(hf + 1) * 512],
                                   wv[:, ci, :],
                                   xv[:, ci, hf * 512:(hf + 1) * 512],
                                   start=(ti == 0 and ci == 0),
                                   stop=(ti == 3 and ci == 15)).then_inc(S.h("c_pe"), 1)
        v_lg = S.bump("c_pe", 128)
        @S.step("act", waits={"c_pe": v_lg})
        def _(api):
            api.activation(out=sco_f[:, :], in_=lg[:, :], func=AF.Exp,
                           scale=-1.0).then_inc(S.h("c_act"), 1)
        v_sg = S.bump("c_act")
        @S.step("dve", waits={"c_act": v_sg})
        def _(api):
            api.tensor_scalar(out=sco_f[:, :], in0=sco_f[:, :],
                              scalar1=1.0, scalar2=None,
                              op0=ALU.add).then_inc(S.h("c_dve"), 1)
        v_s1 = S.bump("c_dve")
        @S.step("dve", waits={"c_dve": v_s1})
        def _(api):
            api.reciprocal(out=sco_f[:, :],
                           in_=sco_f[:, :]).then_inc(S.h("c_dve"), 1)
        v_s2 = S.bump("c_dve")
        @S.step("dve", waits={"c_dve": v_s2})
        def _(api):
            api.tensor_scalar(out=scb_f[:, :], in0=sco_f[:, :],
                              scalar1=gb[:, 0:1], scalar2=None,
                              op0=ALU.add).then_inc(S.h("c_dve"), 1)
        v_scb = S.bump("c_dve")
        @S.step("pe", waits={"c_dve": v_scb})
        def _(api):
            for tc in range(TCH):
                api.transpose(trp[:, tc * 32:(tc + 1) * 32],
                              sco_f[:, tc * 128:(tc + 1) * 128],
                              idf[0:32, 0:32]).then_inc(S.h("c_pe"), 1)
        v_t1 = S.bump("c_pe", TCH)
        @S.step("dve", waits={"c_pe": v_t1})
        def _(api):
            api.tensor_copy(out=sco_t.rearrange("p a b -> p (a b)")[:, :],
                            in_=trp[:, :]).then_inc(S.h("c_dve"), 1)
        v_c1 = S.bump("c_dve")
        @S.step("pe", waits={"c_dve": v_c1})
        def _(api):
            for tc in range(TCH):
                api.transpose(trp[:, tc * 32:(tc + 1) * 32],
                              scb_f[:, tc * 128:(tc + 1) * 128],
                              idf[0:32, 0:32]).then_inc(S.h("c_pe"), 1)
        v_t2 = S.bump("c_pe", TCH)
        @S.step("dve", waits={"c_pe": v_t2})
        def _(api):
            api.tensor_copy(out=scb_t.rearrange("p a b -> p (a b)")[:, :],
                            in_=trp[:, :]).then_inc(S.h("c_dve"), 1)
        v_sc_tm = S.bump("c_dve")

    hs_fm = slotB.rearrange("p (a t) -> p a t", t=1024)
    wgb = [slotC[:, 1024:3072].rearrange("p (a j) -> p a j", j=128),
           slotC[:, 3072:5120].rearrange("p (a j) -> p a j", j=128)]
    wub = [slotC[:, 5120:7168].rearrange("p (a j) -> p a j", j=128),
           slotC[:, 7168:9216].rearrange("p (a j) -> p a j", j=128)]
    gsb = [slotC[:, 9216:10240], slotC[:, 10240:11264]]
    usb = [slotC[:, 11264:12288], slotC[:, 12288:13312]]

    vC = {}
    v_a_bf = {}

    def routing_tc(tc, cps):
        sc = scb_t[:, tc, :]
        sv = sco_t[:, tc, :]
        c = {"v": max(S.n("c_dve"), v_sc_tm)}

        def D(fn, extra=None):
            w = {"c_dve": c["v"]}
            if extra:
                w = mx(w, extra)
            @S.step("dve", waits=w)
            def _(api):
                fn(api).then_inc(S.h("c_dve"), 1)
            c["v"] = S.bump("c_dve")

        D(lambda api, sc=sc: api.tensor_copy(out=r_work[:, :], in_=sc))
        for g in range(4):
            D(lambda api, g=g: api.tensor_reduce(
                out=r_m1[:, :], in_=r_work[:, g * 8:(g + 1) * 8], axis=AX,
                op=ALU.max))
            D(lambda api, g=g: api.tensor_scalar(
                out=r_eq[:, :], in0=r_work[:, g * 8:(g + 1) * 8],
                scalar1=r_m1[:, 0:1], scalar2=None, op0=ALU.is_ge))
            D(lambda api: api.tensor_reduce(
                out=r_cnt[:, :], in_=r_eq[:, :], axis=AX, op=ALU.add))
            D(lambda api: api.tensor_scalar(
                out=r_t8[:, :], in0=r_eq[:, :], scalar1=BIG, scalar2=None,
                op0=ALU.mult))
            D(lambda api, g=g: api.tensor_tensor(
                out=r_t8[:, :], in0=r_work[:, g * 8:(g + 1) * 8],
                in1=r_t8[:, :], op=ALU.subtract))
            D(lambda api: api.tensor_reduce(
                out=r_m2[:, :], in_=r_t8[:, :], axis=AX, op=ALU.max))
            D(lambda api: api.tensor_scalar(
                out=r_w[:, :], in0=r_cnt[:, :], scalar1=1.5, scalar2=None,
                op0=ALU.is_ge))
            D(lambda api: api.tensor_tensor(
                out=r_d[:, :], in0=r_m1[:, :], in1=r_m2[:, :], op=ALU.subtract))
            D(lambda api: api.tensor_tensor(
                out=r_d[:, :], in0=r_w[:, :], in1=r_d[:, :], op=ALU.mult))
            D(lambda api: api.tensor_tensor(
                out=r_d[:, :], in0=r_m2[:, :], in1=r_d[:, :], op=ALU.add))
            D(lambda api, g=g: api.tensor_tensor(
                out=r_gs[:, g:g + 1], in0=r_m1[:, :], in1=r_d[:, :], op=ALU.add))
        D(lambda api: api.tensor_reduce(
            out=r_m1[:, :], in_=r_gs[:, :], axis=AX, op=ALU.max))
        D(lambda api: api.tensor_scalar(
            out=r_eq[:, 0:4], in0=r_gs[:, :], scalar1=r_m1[:, 0:1],
            scalar2=None, op0=ALU.is_ge))
        D(lambda api: api.tensor_reduce(
            out=r_cnt[:, :], in_=r_eq[:, 0:4], axis=AX, op=ALU.add))
        D(lambda api: api.tensor_scalar(
            out=r_t8[:, 0:4], in0=r_eq[:, 0:4], scalar1=BIG, scalar2=None,
            op0=ALU.mult))
        D(lambda api: api.tensor_tensor(
            out=r_t8[:, 0:4], in0=r_gs[:, :], in1=r_t8[:, 0:4],
            op=ALU.subtract))
        D(lambda api: api.tensor_reduce(
            out=r_m2[:, :], in_=r_t8[:, 0:4], axis=AX, op=ALU.max))
        D(lambda api: api.tensor_scalar(
            out=r_w[:, :], in0=r_cnt[:, :], scalar1=1.5, scalar2=None,
            op0=ALU.is_ge))
        D(lambda api: api.tensor_tensor(
            out=r_d[:, :], in0=r_m1[:, :], in1=r_m2[:, :], op=ALU.subtract))
        D(lambda api: api.tensor_tensor(
            out=r_d[:, :], in0=r_w[:, :], in1=r_d[:, :], op=ALU.mult))
        D(lambda api: api.tensor_tensor(
            out=r_th[:, :], in0=r_m2[:, :], in1=r_d[:, :], op=ALU.add))
        for g in range(4):
            D(lambda api, g=g: api.tensor_scalar(
                out=r_eq[:, g:g + 1], in0=r_gs[:, g:g + 1],
                scalar1=r_th[:, 0:1], scalar2=None, op0=ALU.is_ge))
            D(lambda api, g=g, sc=sc: api.tensor_scalar(
                out=r_work[:, g * 8:(g + 1) * 8], in0=sc[:, g * 8:(g + 1) * 8],
                scalar1=r_eq[:, g:g + 1], scalar2=None, op0=ALU.mult))
        for k in range(KTOP):
            src = r_work if k % 2 == 0 else r_work2
            dst = r_work2 if k % 2 == 0 else r_work
            D(lambda api, src=src: api.tensor_reduce(
                out=r_m1[:, :], in_=src[:, :], axis=AX, op=ALU.max))
            D(lambda api, src=src: api.tensor_scalar(
                out=r_r1[:, :], in0=src[:, :], scalar1=r_m1[:, 0:1],
                scalar2=None, op0=ALU.is_ge))
            D(lambda api: api.tensor_tensor(
                out=r_r1[:, :], in0=r_r1[:, :], in1=rev32[:, :], op=ALU.mult))
            D(lambda api: api.tensor_reduce(
                out=r_m2[:, :], in_=r_r1[:, :], axis=AX, op=ALU.max))
            D(lambda api, tc=tc, k=k: api.tensor_scalar(
                out=idxf[:, tc, k:k + 1], in0=r_m2[:, :], scalar1=-1.0,
                scalar2=32.0, op0=ALU.mult, op1=ALU.add))
            D(lambda api, tc=tc, k=k: api.tensor_scalar(
                out=r_oh[:, k, :], in0=ioe[:, :],
                scalar1=idxf[:, tc, k:k + 1], scalar2=None, op0=ALU.is_equal))
            D(lambda api, k=k, sv=sv: api.tensor_tensor(
                out=r_r1[:, :], in0=sv, in1=r_oh[:, k, :], op=ALU.mult))
            D(lambda api, k=k: api.tensor_reduce(
                out=r_wraw[:, k:k + 1], in_=r_r1[:, :], axis=AX, op=ALU.add))
            D(lambda api, k=k, src=src, dst=dst: api.scalar_tensor_tensor(
                out=dst[:, :], in0=r_oh[:, k, :], scalar=-BIG, in1=src[:, :],
                op0=ALU.mult, op1=ALU.add))
        for k in range(1, KTOP):
            for kp2 in range(k):
                D(lambda api, tc=tc, k=k, kp2=kp2: api.tensor_tensor(
                    out=r_d[:, :], in0=idxf[:, tc, kp2:kp2 + 1],
                    in1=idxf[:, tc, k:k + 1], op=ALU.is_equal))
                if kp2 == 0:
                    D(lambda api, tc=tc, k=k: api.tensor_copy(
                        out=t2v[:, tc, k:k + 1], in_=r_d[:, :]))
                else:
                    D(lambda api, tc=tc, k=k: api.tensor_tensor(
                        out=t2v[:, tc, k:k + 1], in0=t2v[:, tc, k:k + 1],
                        in1=r_d[:, :], op=ALU.add))
        D(lambda api: api.tensor_reduce(
            out=r_ws[:, :], in_=r_wraw[:, :], axis=AX, op=ALU.add))
        D(lambda api: api.tensor_scalar(
            out=r_ws[:, :], in0=r_ws[:, :], scalar1=1e-20, scalar2=None,
            op0=ALU.add))
        D(lambda api: api.reciprocal(out=r_ws[:, :], in_=r_ws[:, :]))
        D(lambda api, tc=tc: api.tensor_scalar(
            out=wkv[:, tc, :], in0=r_wraw[:, :], scalar1=r_ws[:, 0:1],
            scalar2=2.0, op0=ALU.mult, op1=ALU.mult))
        D(lambda api: api.tensor_tensor(
            out=r_a[:, :], in0=r_oh[:, 0, :], in1=r_oh[:, 1, :], op=ALU.add))
        D(lambda api: api.tensor_tensor(
            out=r_a[:, :], in0=r_a[:, :], in1=r_oh[:, 2, :], op=ALU.add))
        D(lambda api: api.tensor_tensor(
            out=r_a[:, :], in0=r_a[:, :], in1=r_oh[:, 3, :], op=ALU.add))
        D(lambda api, tc=tc: api.tensor_copy(out=anv[:, tc, :], in_=r_a[:, :]))
        v_a_bf[tc] = c["v"]
        @S.step("pe", waits={"c_dve": v_a_bf[tc]})
        def _(api, tc=tc):
            if tc == 0:
                api.matmul(cps[:, :], lstr[:, :], anv[:, 0, :],
                           start=True, stop=True).then_inc(S.h("c_pe"), 1)
            else:
                for j in range(tc):
                    api.matmul(cps[:, :], onesb[:, :], anv[:, j, :],
                               start=(j == 0), stop=False)
                api.matmul(cps[:, :], lstr[:, :], anv[:, tc, :],
                           start=False, stop=True).then_inc(S.h("c_pe"), 1)
        v_cp = S.bump("c_pe")
        @S.step("dve", waits={"c_pe": v_cp})
        def _(api, tc=tc):
            api.tensor_copy(out=csv[:, tc, :], in_=cps[:, :]).then_inc(S.h("c_dve"), 1)
        vC[tc] = S.bump("c_dve")

    with nc.psum_tensor("pg2", [128, 2, 1024], F32) as pg2, \
         nc.psum_tensor("pu2", [128, 1024], F32) as pu2, \
         nc.psum_tensor("cps", [128, 32], F32) as cps, \
         nc.psum_tensor("totp", [1, 32], F32) as totp:
        v_gb_free = {0: 0, 1: 0}
        v_ub_free = {0: 0, 1: 0}
        v_pg_free = {0: 0, 1: 0}
        v_hs_mult = {0: 0, 1: 0}
        v_pu_free = 0
        for co in range(16):
            par = co % 2
            @S.step("pool", waits={"c_pe": v_gb_free[par], "cc_x2": 1})
            def _(api, co=co, par=par):
                api.dma_start(out=wgb[par].rearrange("p a j -> p (a j)")[:, :],
                              in_=wsg_d[co].rearrange("p a j -> p (a j)")[:, :]
                              ).then_inc(S.h("d_pq"), 16)
            v_lg2 = S.bump("d_pq", 16)
            @S.step("pool", waits={"c_pe": v_ub_free[par], "cc_x2": 1})
            def _(api, co=co, par=par):
                api.dma_start(out=wub[par].rearrange("p a j -> p (a j)")[:, :],
                              in_=wsu_d[co].rearrange("p a j -> p (a j)")[:, :]
                              ).then_inc(S.h("d_pq"), 16)
            v_lu2 = S.bump("d_pq", 16)
            @S.step("pe", waits={"d_pq": v_lg2, "d_aq": aq_xn2,
                                 "c_act": v_pg_free[par]})
            def _(api, par=par):
                for ci in range(16):
                    for hf in range(2):
                        api.matmul(pg2[:, par, hf * 512:(hf + 1) * 512],
                                   wgb[par][:, ci, :],
                                   xn2_fm[:, ci, hf * 512:(hf + 1) * 512],
                                   start=(ci == 0), stop=(ci == 15)).then_inc(S.h("c_pe"), 1)
            v_g2 = S.bump("c_pe", 32)
            v_gb_free[par] = v_g2
            @S.step("pe", waits={"d_pq": v_lu2, "c_act": v_pu_free})
            def _(api, par=par):
                for ci in range(16):
                    for hf in range(2):
                        api.matmul(pu2[:, hf * 512:(hf + 1) * 512],
                                   wub[par][:, ci, :],
                                   xn2_fm[:, ci, hf * 512:(hf + 1) * 512],
                                   start=(ci == 0), stop=(ci == 15)).then_inc(S.h("c_pe"), 1)
            v_u2 = S.bump("c_pe", 32)
            v_ub_free[par] = v_u2
            @S.step("act", waits={"c_pe": v_g2,
                                  "c_pool": v_hs_mult[par]})
            def _(api, par=par):
                api.activation(out=gsb[par][:, :], in_=pg2[:, par, :],
                               func=AF.Silu).then_inc(S.h("c_act"), 1)
            v_si = S.bump("c_act")
            v_pg_free[par] = v_si
            @S.step("act", waits={"c_pe": v_u2,
                                  "c_pool": v_hs_mult[par]})
            def _(api, co=co, par=par):
                api.activation(out=usb[par][:, :], in_=pu2[:, :],
                               func=AF.Copy).then_inc(S.h("c_act"), 1)
            v_ue = S.bump("c_act")
            v_pu_free = v_ue
            @S.step("pool", waits={"c_act": v_ue, "c_pe": v_lg})
            def _(api, co=co, par=par):
                api.tensor_tensor(out=hs_fm[:, co, :], in0=gsb[par][:, :],
                                  in1=usb[par][:, :], op=ALU.mult
                                  ).then_inc(S.h("c_pool"), 1)
            v_hs_mult[par] = S.bump("c_pool")
            if co % 2 == 1 and co // 2 < TCH:
                routing_tc(co // 2, cps)
        for tc in range(TCH):
            if tc not in vC:
                routing_tc(tc, cps)
        v_hs_done = S.n("c_pool")

        @S.step("pe", waits={"c_dve": v_a_bf[TCH - 1]})
        def _(api):
            for tc in range(TCH):
                api.matmul(totp[:, :], onesb[:, 0:1], anv[:, tc, :],
                           start=(tc == 0), stop=(tc == TCH - 1)).then_inc(S.h("c_pe"), 1)
        v_tp = S.bump("c_pe", TCH)
        @S.step("dve", waits={"c_pe": v_tp})
        def _(api):
            api.tensor_copy(out=totsb[:, :], in_=totp[:, :]).then_inc(S.h("c_dve"), 1)
        v_ts = S.bump("c_dve")
        @S.step("sp", waits={"c_dve": v_ts})
        def _(api):
            api.dma_start(out=tot_in[:, :], in_=totsb[:, :]).then_inc(S.h("d_sp"), 16)
        v_tst = S.bump("d_sp", 16)
        @S.step("pool", waits={"d_sp": v_tst})
        def _(api):
            api.collective_compute(
                "AllGather", ALU.bypass, replica_groups=[list(range(8))],
                ins=[tot_in[:, :]], outs=[tot_all[:, :]]).then_inc(S.h("cc_tot"), 1)
        S.bump("cc_tot")
        @S.step("pool", waits={"cc_tot": 1})
        def _(api):
            api.dma_start(out=tota[:, :], in_=tot_all[:, :]).then_inc(S.h("d_pq"), 16)
        v_tl = S.bump("d_pq", 16)

    v_hs_pe = S.n("c_pe")

    with nc.psum_tensor("ofp", [1, 32], F32) as ofp, \
         nc.psum_tensor("ofb", [128, 32], F32) as ofb:
        @S.step("pe", waits={"d_pq": v_tl})
        def _(api):
            api.matmul(ofp[:, :], cmask[:, :], tota[:, :],
                       start=True, stop=True).then_inc(S.h("c_pe"), 1)
        v_of = S.bump("c_pe")
        @S.step("dve", waits={"c_pe": v_of})
        def _(api):
            api.tensor_copy(out=offs1[:, :], in_=ofp[:, :]).then_inc(S.h("c_dve"), 1)
        v_o1 = S.bump("c_dve")
        @S.step("pe", waits={"c_dve": v_o1})
        def _(api):
            api.matmul(ofb[:, :], onesf[:, :], offs1[:, :],
                       start=True, stop=True).then_inc(S.h("c_pe"), 1)
        v_ob2 = S.bump("c_pe")
        @S.step("dve", waits={"c_pe": v_ob2})
        def _(api):
            api.tensor_copy(out=offsb[:, :], in_=ofb[:, :]).then_inc(S.h("c_dve"), 1)
        v_offs = S.bump("c_dve")

    c2 = {"v": S.n("c_dve")}

    def D2(fn, extra=None):
        w = {"c_dve": c2["v"]}
        if extra:
            w = mx(w, extra)
        @S.step("dve", waits=w)
        def _(api):
            fn(api).then_inc(S.h("c_dve"), 1)
        c2["v"] = S.bump("c_dve")

    v_gidx = {}
    v_scat = 0
    for tc in range(TCH):
        for k in range(KTOP):
            D2(lambda api, tc=tc, k=k: api.tensor_scalar(
                out=r_a[:, :], in0=ioe[:, :], scalar1=idxf[:, tc, k:k + 1],
                scalar2=None, op0=ALU.is_equal),
               {"c_dve": max(v_offs, vC[tc])})
            D2(lambda api, tc=tc: api.tensor_tensor(
                out=r_r1[:, :], in0=csv[:, tc, :], in1=r_a[:, :], op=ALU.mult))
            D2(lambda api: api.tensor_reduce(
                out=r_cat[:, :], in_=r_r1[:, :], axis=AX, op=ALU.add))
            D2(lambda api: api.tensor_tensor(
                out=r_r1[:, :], in0=offsb[:, :], in1=r_a[:, :], op=ALU.mult))
            D2(lambda api: api.tensor_reduce(
                out=r_oat[:, :], in_=r_r1[:, :], axis=AX, op=ALU.add))
            D2(lambda api, tc=tc, k=k: api.tensor_tensor(
                out=r_pos[:, :], in0=r_cat[:, :], in1=t2v[:, tc, k:k + 1],
                op=ALU.add))
            D2(lambda api: api.tensor_tensor(
                out=r_pos[:, :], in0=r_pos[:, :], in1=r_oat[:, :], op=ALU.add))
            D2(lambda api: api.tensor_scalar(
                out=r_kp[:, :], in0=r_pos[:, :], scalar1=511.5, scalar2=None,
                op0=ALU.is_le))
            D2(lambda api, tc=tc, k=k: api.tensor_scalar(
                out=r_g0[:, :], in0=idxf[:, tc, k:k + 1], scalar1=512.0,
                scalar2=None, op0=ALU.mult))
            D2(lambda api: api.tensor_tensor(
                out=r_g0[:, :], in0=r_g0[:, :], in1=r_pos[:, :], op=ALU.add))
            D2(lambda api: api.tensor_scalar(
                out=r_adj[:, :], in0=r_kp[:, :], scalar1=-1.0, scalar2=1.0,
                op0=ALU.mult, op1=ALU.add))
            D2(lambda api: api.tensor_tensor(
                out=r_adj[:, :], in0=r_adj[:, :], in1=trash_t[:, :],
                op=ALU.mult))
            D2(lambda api: api.tensor_tensor(
                out=r_g0[:, :], in0=r_g0[:, :], in1=r_kp[:, :], op=ALU.mult))
            D2(lambda api, tc=tc, k=k: api.tensor_tensor(
                out=gidx_f[:, tc, k:k + 1], in0=r_g0[:, :],
                in1=r_adj[:, :], op=ALU.add))
            D2(lambda api, tc=tc, k=k: api.tensor_copy(
                out=gidx[:, tc, k:k + 1], in_=gidx_f[:, tc, k:k + 1]))
            v_gidx[(tc, k)] = c2["v"]
    for tc in range(TCH):
        D2(lambda api, tc=tc: api.tensor_scalar(
            out=r_tok[:, tc:tc + 1], in0=pcol[:, :],
            scalar1=float(tc * 128 + 1), scalar2=None, op0=ALU.add))
        v_tok = c2["v"]
        @S.step("pool", waits={"c_dve": max(v_tok, v_gidx[(tc, KTOP - 1)])})
        def _(api, tc=tc):
            for k in range(KTOP):
                api.indirect_dma_start(
                    out=map_in[:, :],
                    out_offset=IOA(ap=gidx[:, tc, k:k + 1], axis=0),
                    in_=r_tok[:, tc:tc + 1], in_offset=None,
                ).then_inc(S.h("d_pq"), 16)
        v_scat = S.bump("d_pq", 16 * KTOP)

    @S.step("pool", waits={"d_pq": v_scat})
    def _(api):
        api.collective_compute(
            "AllReduce", ALU.add, replica_groups=[list(range(8))],
            ins=[map_in[:, :]], outs=[map_all[:, :]]).then_inc(S.h("cc_map"), 1)
    S.bump("cc_map")

    if debug:
        @S.step("sp", waits={"c_dve": c2["v"]})
        def _(api):
            api.dma_start(out=dbg["d_sc"][:, :],
                          in_=scb_t.rearrange("p a b -> p (a b)")[:, :]).then_inc(S.h("d_sp"), 16)
            api.dma_start(out=dbg["d_idx"][:, :],
                          in_=idxf.rearrange("p a b -> p (a b)")[:, :]).then_inc(S.h("d_sp"), 16)
            api.dma_start(out=dbg["d_wk"][:, :],
                          in_=wkv.rearrange("p a b -> p (a b)")[:, :]).then_inc(S.h("d_sp"), 16)
            api.dma_start(out=dbg["d_gidx"][:, :],
                          in_=gidx.rearrange("p a b -> p (a b)")[:, :]).then_inc(S.h("d_sp"), 16)
            api.dma_start(out=dbg["d_cs"][:, :],
                          in_=csv.rearrange("p a b -> p (a b)")[:, :]).then_inc(S.h("d_sp"), 16)
            api.dma_start(out=dbg["d_offs"][:, :], in_=offsb[:, :]).then_inc(S.h("d_sp"), 16)
        S.bump("d_sp", 96)
        @S.step("pool", waits={"cc_map": 1})
        def _(api):
            api.dma_start(out=dbg["d_map"][:, :], in_=map_all[:, :]).then_inc(S.h("d_pq"), 16)
        S.bump("d_pq", 16)

    S.barrier()

    w2sb = slotA.rearrange("p (a f) -> p a f", f=2048)
    buf_fm = slotC[:, 0:8192].rearrange("p (a b) -> p a b", b=512)
    obsb = [slotC[:, 0:2048], slotC[:, 2048:4096]]
    buf_tm = slotC[:, 8192:10240]
    hh_fm = slotC[:, 10240:14336].rearrange("p (a b) -> p a b", b=512)
    w1b = [slotC[:, 14336:16384].rearrange("p (a j) -> p a j", j=128),
           slotC[:, 16384:18432].rearrange("p (a j) -> p a j", j=128)]
    w3b = [slotC[:, 18432:20480].rearrange("p (a j) -> p a j", j=128),
           slotC[:, 20480:22528].rearrange("p (a j) -> p a j", j=128)]

    v_w2_free = v_hs_pe
    v_ob_store = 0
    v_gath_d = 0
    v_pq_prev = 0
    for le in range(EPC):
        @S.step("pool", waits={"c_pe": v_w2_free})
        def _(api, le=le):
            for ci in range(8):
                api.dma_start(out=w2sb[:, ci, :],
                              in_=w2_d[le, ci, :, :]).then_inc(S.h("d_pq"), 16)
        v_w2 = S.bump("d_pq", 16 * 8)
        v_gfin = 0
        for j in range(4):
            c3 = {"v": S.n("c_dve")}

            def D3(fn, extra=None):
                w = {"c_dve": c3["v"]}
                if extra:
                    w = mx(w, extra)
                @S.step("dve", waits=w)
                def _(api):
                    fn(api).then_inc(S.h("c_dve"), 1)
                c3["v"] = S.bump("c_dve")

            D3(lambda api: api.tensor_scalar(
                out=mrow[:, :], in0=pcol[:, :], scalar1=2.0, scalar2=None,
                op0=ALU.mult), {"d_pq": v_pq_prev})
            D3(lambda api: api.tensor_tensor(
                out=mrow[:, :], in0=mrow[:, :], in1=pidx[:, :],
                op=ALU.subtract))
            D3(lambda api, le=le, j=j: api.tensor_scalar(
                out=mrow[:, :], in0=mrow[:, :],
                scalar1=float(le * CAP + j * 128), scalar2=None, op0=ALU.add))
            D3(lambda api: api.tensor_copy(out=mrow_i[:, :], in_=mrow[:, :]))
            v_mr = c3["v"]
            @S.step("pool", waits={"c_dve": v_mr, "cc_map": 1})
            def _(api):
                api.indirect_dma_start(
                    out=mapt[:, :], out_offset=None,
                    in_=map_all[:, :], in_offset=IOA(ap=mrow_i[:, :], axis=0),
                ).then_inc(S.h("d_pq"), 16)
            v_mg = S.bump("d_pq", 16)
            D3(lambda api: api.tensor_scalar(
                out=r_kp[:, :], in0=mapt[:, :], scalar1=0.5, scalar2=None,
                op0=ALU.is_le), {"d_pq": v_mg})
            D3(lambda api: api.tensor_tensor(
                out=r_kp[:, :], in0=r_kp[:, :], in1=zrow_t[:, :],
                op=ALU.mult))
            D3(lambda api: api.tensor_scalar(
                out=r_g0[:, :], in0=mapt[:, :], scalar1=1.0, scalar2=None,
                op0=ALU.subtract))
            D3(lambda api: api.tensor_tensor(
                out=r_g0[:, :], in0=r_g0[:, :], in1=r_kp[:, :], op=ALU.add))
            D3(lambda api: api.tensor_copy(out=mapi[:, :], in_=r_g0[:, :]))
            v_mi = c3["v"]
            @S.step("pool", waits={"c_dve": v_mi, "cc_x2": 1,
                                   "d_sp": v_gath_d})
            def _(api):
                api.indirect_dma_start(
                    out=buf_tm[:, :], out_offset=None,
                    in_=xn2_all[:, :], in_offset=IOA(ap=mapi[:, :], axis=0),
                ).then_inc(S.h("d_pq"), 16)
            v_bg = S.bump("d_pq", 16)
            @S.step("sp", waits={"d_pq": v_bg})
            def _(api, le=le, j=j):
                api.dma_start(
                    out=bufdr[le * CAP + j * 128: le * CAP + (j + 1) * 128, :],
                    in_=buf_tm[:, :]).then_inc(S.h("d_sp"), 16)
            v_gfin = S.bump("d_sp", 16)
            v_gath_d = v_gfin
            v_pq_prev = v_bg
        @S.step("act", waits={"d_sp": max(v_gfin, v_ob_store)})
        def _(api, le=le):
            api.dma_start(out=buf_fm[:, :, :],
                          in_=bufdr[le * CAP:(le + 1) * CAP, :],
                          transpose=True).then_inc(S.h("d_aq"), 16)
        v_bfm = S.bump("d_aq", 16)
        if debug and le == EPC - 1:
            @S.step("sp", waits={"d_sp": v_gfin})
            def _(api):
                api.dma_start(out=dbg["d_buf"][:, :], in_=bufdr[:, :]).then_inc(S.h("d_sp"), 16)
            S.bump("d_sp", 16)

        with nc.psum_tensor(f"h1p{le}", [128, 2, 512], F32) as h1p, \
             nc.psum_tensor(f"h3p{le}", [128, 2, 512], F32) as h3p:
            v_w13_free = {0: 0, 1: 0}
            v_hh_free = {0: 0, 1: 0}
            for hc in range(8):
                par = hc % 2
                @S.step("pool", waits={"c_pe": v_w13_free[par]})
                def _(api, le=le, hc=hc, par=par):
                    api.dma_start(out=w1b[par].rearrange("p a j -> p (a j)")[:, :],
                                  in_=w1_d[le, hc].rearrange("p a j -> p (a j)")[:, :]
                                  ).then_inc(S.h("d_pq"), 16)
                    api.dma_start(out=w3b[par].rearrange("p a j -> p (a j)")[:, :],
                                  in_=w3_d[le, hc].rearrange("p a j -> p (a j)")[:, :]
                                  ).then_inc(S.h("d_pq"), 16)
                v_wld = S.bump("d_pq", 32)
                @S.step("pe", waits={"d_pq": v_wld, "d_aq": v_bfm,
                                     "c_dve": v_hh_free[par]})
                def _(api, par=par):
                    for ci in range(16):
                        api.matmul(h1p[:, par, :], w1b[par][:, ci, :],
                                   buf_fm[:, ci, :], start=(ci == 0),
                                   stop=(ci == 15)).then_inc(S.h("c_pe"), 1)
                    for ci in range(16):
                        api.matmul(h3p[:, par, :], w3b[par][:, ci, :],
                                   buf_fm[:, ci, :], start=(ci == 0),
                                   stop=(ci == 15)).then_inc(S.h("c_pe"), 1)
                v_hm = S.bump("c_pe", 32)
                v_w13_free[par] = v_hm
                @S.step("act", waits={"c_pe": v_hm})
                def _(api, hc=hc, par=par):
                    api.activation(out=hh_fm[:, hc, :], in_=h1p[:, par, :],
                                   func=AF.Silu).then_inc(S.h("c_act"), 1)
                v_si2 = S.bump("c_act")
                @S.step("dve", waits={"c_act": v_si2})
                def _(api, hc=hc, par=par):
                    api.tensor_tensor(out=hh_fm[:, hc, :], in0=hh_fm[:, hc, :],
                                      in1=h3p[:, par, :], op=ALU.mult
                                      ).then_inc(S.h("c_dve"), 1)
                v_hh_free[par] = S.bump("c_dve")
            v_hh_all = max(v_hh_free[0], v_hh_free[1])

        with nc.psum_tensor(f"obp{le}", [128, 2, H], F32) as obp:
            v_obe = {0: 0, 1: 0}
            v_obs = {0: 0, 1: 0}
            for j in range(4):
                par = j % 2
                @S.step("pe", waits={"c_dve": v_hh_all, "d_pq": v_w2,
                                     "c_act": v_obe[par]})
                def _(api, j=j, par=par):
                    for hc in range(8):
                        for q in range(4):
                            api.matmul(obp[:, par, q * 512:(q + 1) * 512],
                                       hh_fm[:, hc, j * 128:(j + 1) * 128],
                                       w2sb[:, hc, q * 512:(q + 1) * 512],
                                       start=(hc == 0), stop=(hc == 7)
                                       ).then_inc(S.h("c_pe"), 1)
                v_od = S.bump("c_pe", 32)
                v_w2_free = v_od
                @S.step("act", waits={"c_pe": v_od, "d_sp": v_obs[par]})
                def _(api, par=par):
                    api.activation(out=obsb[par][:, :], in_=obp[:, par, :],
                                   func=AF.Copy).then_inc(S.h("c_act"), 1)
                v_oe = S.bump("c_act")
                v_obe[par] = v_oe
                @S.step("sp", waits={"c_act": v_oe})
                def _(api, le=le, j=j, par=par):
                    api.dma_start(
                        out=ob_in[le * CAP + j * 128: le * CAP + (j + 1) * 128, :],
                        in_=obsb[par][:, :]).then_inc(S.h("d_sp"), 16)
                v_obs[par] = S.bump("d_sp", 16)
            v_ob_store = S.n("d_sp")

    @S.step("pool", waits={"d_sp": v_ob_store})
    def _(api):
        api.collective_compute(
            "AllGather", ALU.bypass, replica_groups=[list(range(8))],
            ins=[ob_in[:, :]], outs=[ob_all[0:NSLOT, :]]).then_inc(S.h("cc_ob"), 1)
    S.bump("cc_ob")
    if debug:
        @S.step("sp", waits={"d_sp": v_ob_store})
        def _(api):
            api.dma_start(out=dbg["d_ob"][:, :], in_=ob_in[:, :]).then_inc(S.h("d_sp"), 16)
        S.bump("d_sp", 16)

    S.barrier()

    ysb = [slotC[:, 0:2048], slotC[:, 2048:4096]]
    wsb2 = [slotC[:, 4096:4608], slotC[:, 4608:5120]]
    v_y_free = {0: 0, 1: 0}
    v_xc = 0
    for tc in range(TCH):
        vdv = S.n("c_dve")
        for k in range(KTOP):
            par = k % 2
            @S.step("pool", waits={"cc_ob": 1, "c_dve": v_y_free[par]})
            def _(api, tc=tc, k=k, par=par):
                api.indirect_dma_start(
                    out=ysb[par][:, :], out_offset=None,
                    in_=ob_all[:, :],
                    in_offset=IOA(ap=gidx[:, tc, k:k + 1], axis=0),
                ).then_inc(S.h("d_pq"), 16)
            v_yg = S.bump("d_pq", 16)
            if k == 0:
                @S.step("dve", waits={"d_pq": v_yg, "c_dve": vdv})
                def _(api, tc=tc):
                    api.tensor_scalar(out=acc[0][:, :], in0=ysb[0][:, :],
                                      scalar1=wkv[:, tc, 0:1], scalar2=None,
                                      op0=ALU.mult).then_inc(S.h("c_dve"), 1)
            else:
                @S.step("dve", waits={"d_pq": v_yg, "c_dve": S.n("c_dve")})
                def _(api, tc=tc, k=k, par=par):
                    api.scalar_tensor_tensor(
                        out=acc[par][:, :], in0=ysb[par][:, :],
                        scalar=wkv[:, tc, k:k + 1], in1=acc[1 - par][:, :],
                        op0=ALU.mult, op1=ALU.add).then_inc(S.h("c_dve"), 1)
            v_y_free[par] = S.bump("c_dve")
        @S.step("dve", waits={"c_dve": S.n("c_dve")})
        def _(api, tc=tc):
            api.tensor_tensor(out=x_tm[:, tc, :], in0=x_tm[:, tc, :],
                              in1=acc[(KTOP - 1) % 2][:, :], op=ALU.add
                              ).then_inc(S.h("c_dve"), 1)
        v_xc = S.bump("c_dve")
        v_y_free[(KTOP - 1) % 2] = v_xc
    v_moe = v_xc
    if debug:
        @S.step("sp", waits={"c_dve": v_moe})
        def _(api):
            api.dma_start(out=dbg["d_moe"][:, :],
                          in_=x_tm.rearrange("p a f -> p (a f)")[:, :]).then_inc(S.h("d_sp"), 16)
        S.bump("d_sp", 16)

    with nc.psum_tensor("sdp", [128, TCH, 512], F32) as sdp:
        v_sd_free = {0: 0, 1: 0}
        v_oev = {}
        v_ost = {0: 0, 1: 0}
        for fo4 in range(4):
            for ci in range(16):
                it = fo4 * 16 + ci
                par = it % 2
                @S.step("pool", waits={"c_pe": v_sd_free[par], "cc_ob": 1})
                def _(api, fo4=fo4, ci=ci, par=par):
                    api.dma_start(out=wsb2[par][:, :],
                                  in_=wsd_d[ci, :, fo4 * 512:(fo4 + 1) * 512]
                                  ).then_inc(S.h("d_pq"), 16)
                v_l = S.bump("d_pq", 16)
                w = {"d_pq": v_l, "c_dve": v_moe,
                     "c_pool": v_hs_done}
                if ci == 0:
                    w = mx(w, {"c_dve": max([v_oev.get((fo4 - 1, tb), 0)
                                             for tb in range(TCH)] + [0])})
                @S.step("pe", waits=w)
                def _(api, fo4=fo4, ci=ci, par=par):
                    for tb in range(TCH):
                        api.matmul(sdp[:, tb, :],
                                   hs_fm[:, ci, tb * 128:(tb + 1) * 128],
                                   wsb2[par][:, :], start=(ci == 0),
                                   stop=(ci == 15)).then_inc(S.h("c_pe"), 1)
                v_sd_free[par] = S.bump("c_pe", TCH)
            v_last = v_sd_free[(fo4 * 16 + 15) % 2]
            for tb in range(TCH):
                @S.step("dve", waits={"c_pe": v_last, "d_sp": v_ost[tb % 2]})
                def _(api, fo4=fo4, tb=tb):
                    api.tensor_tensor(out=outst[tb % 2][:, :],
                                      in0=sdp[:, tb, :],
                                      in1=x_tm[:, tb, fo4 * 512:(fo4 + 1) * 512],
                                      op=ALU.add).then_inc(S.h("c_dve"), 1)
                v_oe = S.bump("c_dve")
                v_oev[(fo4, tb)] = v_oe
                @S.step("sp", waits={"c_dve": v_oe})
                def _(api, fo4=fo4, tb=tb):
                    api.dma_start(out=out_d[tb * 128:(tb + 1) * 128,
                                            fo4 * 512:(fo4 + 1) * 512],
                                  in_=outst[tb % 2][:, :]).then_inc(S.h("d_sp"), 16)
                v_ost[tb % 2] = S.bump("d_sp", 16)

    v_final = S.n("d_sp")
    @S.step("sp", waits={"d_sp": v_final})
    def _(api):
        pass

    with nc.Block() as block:
        @block.tensor
        def _(api):
            for f in S.prog["pe"]:
                f(api)

        @block.scalar
        def _(api):
            for f in S.prog["act"]:
                f(api)

        @block.vector
        def _(api):
            for f in S.prog["dve"]:
                f(api)

        @block.gpsimd
        def _(api):
            for f in S.prog["pool"]:
                f(api)

        @block.sync
        def _(api):
            for f in S.prog["sp"]:
                f(api)

    alloc.close()
    S.sem_stack.close()
    return nc


def _host_inputs(inputs):
    f32 = np.float32
    bf = ml_dtypes.bfloat16
    x = np.asarray(inputs["L0_attn_x"], f32).reshape(NTOK, H)

    def lhsT16(w):
        w = np.asarray(w, f32)
        return np.ascontiguousarray(
            w.reshape(16, 128, 16, 128).transpose(2, 1, 0, 3))

    wq = lhsT16(inputs["L0_attn_Wq"])
    wk = lhsT16(inputs["L0_attn_Wk"])
    wv = lhsT16(inputs["L0_attn_Wv"])
    wsg = lhsT16(inputs["L0_ffn_shared_gate"])
    wsu = lhsT16(inputs["L0_ffn_shared_up"])
    wo = np.ascontiguousarray(
        np.asarray(inputs["L0_attn_Wo"], f32).reshape(16, 128, H))
    wsd = np.ascontiguousarray(
        np.asarray(inputs["L0_ffn_shared_down"], f32).reshape(16, 128, H))

    def bias_t(b):
        return np.ascontiguousarray(np.asarray(b, f32).reshape(16, 128).T)

    bq_t = bias_t(inputs["L0_attn_bq"])
    bk_t = bias_t(inputs["L0_attn_bk"])
    bv_t = bias_t(inputs["L0_attn_bv"])
    bo_r = np.ascontiguousarray(np.asarray(inputs["L0_attn_bo"], f32).reshape(1, H))
    gwt = np.ascontiguousarray(
        np.asarray(inputs["L0_ffn_gate_w"], f32).T.reshape(16, 128, 32)
        .transpose(1, 0, 2))
    gb = np.ascontiguousarray(np.asarray(inputs["L0_ffn_gate_bias"], f32).reshape(32, 1))

    w1_all = np.asarray(inputs["L0_ffn_w1"], f32)
    w3_all = np.asarray(inputs["L0_ffn_w3"], f32)
    w2_all = np.asarray(inputs["L0_ffn_w2"], f32)

    idf = np.eye(128, dtype=f32)
    onesb = np.ones((128, 128), dtype=bf)
    p_i = np.arange(128)
    lstr = (p_i[:, None] < p_i[None, :]).astype(bf)
    onesf = np.ones((1, 128), dtype=f32)
    qmask = np.zeros((16, 128, 32), dtype=f32)
    for ci in range(16):
        qmask[ci, 0:64, 2 * ci] = 1
        qmask[ci, 64:128, 2 * ci + 1] = 1
    qmask = qmask.astype(bf)
    ioe = np.ascontiguousarray(
        np.broadcast_to(np.arange(32, dtype=f32), (128, 32)))
    pidx = np.arange(128, dtype=f32).reshape(128, 1)

    common = dict(wq=wq, wk=wk, wv=wv, wo=wo, bq_t=bq_t, bk_t=bk_t,
                  bv_t=bv_t, bo_r=bo_r, gwt=gwt, gb=gb, wsg=wsg, wsu=wsu,
                  wsd=wsd, idf=idf, onesb=onesb, lstr=lstr, onesf=onesf,
                  qmask=qmask, ioe=ioe, pidx=pidx)

    in_maps = []
    for c in range(NCORES):
        m = dict(common)
        m["x"] = np.ascontiguousarray(x[c * NT:(c + 1) * NT])
        es = slice(c * EPC, (c + 1) * EPC)
        m["w1"] = np.ascontiguousarray(
            w1_all[es].reshape(EPC, 16, 128, 8, 128).transpose(0, 3, 2, 1, 4))
        m["w3"] = np.ascontiguousarray(
            w3_all[es].reshape(EPC, 16, 128, 8, 128).transpose(0, 3, 2, 1, 4))
        m["w2"] = np.ascontiguousarray(w2_all[es].reshape(EPC, 8, 128, H))
        m["pcol"] = (np.arange(128, dtype=f32) + c * NT).reshape(128, 1)
        m["cmask"] = (np.arange(8) < c).astype(f32).reshape(8, 1)
        in_maps.append(m)
    return in_maps


_LAST_RESULTS = None


def kernel(**inputs):
    global _LAST_RESULTS
    in_maps = _host_inputs(inputs)
    nc = build_program(debug=DEBUG)
    res = run_bass_kernel_spmd(nc, in_maps, core_ids=list(range(NCORES)))
    _LAST_RESULTS = res
    out = np.concatenate([np.asarray(res.results[c]["out"], np.float32)
                          for c in range(NCORES)], 0)
    return out.reshape(2, 4096, H)
